# version 04
# baseline (speedup 1.0000x reference)
import numpy as np
import ml_dtypes
import concourse.bass as bass
import concourse.mybir as mybir
from concourse.bass import IndirectOffsetOnAxis
from concourse.bass_utils import run_bass_kernel_spmd
from contextlib import ExitStack

F32 = mybir.dt.float32
BF16 = mybir.dt.bfloat16
U32 = mybir.dt.uint32
AF = mybir.ActivationFunctionType
ALU = mybir.AluOpType
AX = mybir.AxisListType

B_CORE = 2
T = 2048
D = 1024
NTOK = B_CORE * T
NH, DH = 16, 64
C = 128
NCH = T // C
LCAP = 1536
NE = 8
EOROWS = NE * LCAP + 1
NIDX = NE * LCAP + 256
BLOCK_SIZES = [4, 8, 16, 32]
RANKS = [4, 8, 8, 8]
NUM_NODES = [8, 4, 2, 1]
EPSM = 1.1920929e-07
SPT = 3


def f32(x):
    return np.ascontiguousarray(np.asarray(x), dtype=np.float32)



def _comp_list():
    comps = []
    for n in range(16):
        sl = (4 * n, 4 * n + 4)
        comps.append(dict(wsrc='v', lvl=-1, wrows=sl, qrows=sl,
                          ustream='ktm', ucols=sl, rank=4, kind='leaf'))
    for lvl in range(4):
        bs = BLOCK_SIZES[lvl]
        for n in range(NUM_NODES[lvl]):
            top = (2 * bs * n, 2 * bs * n + bs)
            bot = (2 * bs * n + bs, 2 * bs * n + 2 * bs)
            comps.append(dict(wsrc='v', lvl=lvl, wrows=bot, qrows=bot,
                              ustream=('ktm' if lvl < 2 else f"G{lvl}k"),
                              ucols=top, rank=RANKS[lvl], kind='top'))
    for lvl in range(4):
        bs = BLOCK_SIZES[lvl]
        for n in range(NUM_NODES[lvl]):
            top = (2 * bs * n, 2 * bs * n + bs)
            bot = (2 * bs * n + bs, 2 * bs * n + 2 * bs)
            comps.append(dict(wsrc='k', lvl=lvl, wrows=top, qrows=top,
                              ustream=f"R{lvl}v", ucols=bot,
                              rank=RANKS[lvl], kind='bot'))
    assert len(comps) == 46
    return comps


COMPS = _comp_list()
NVC = sum(1 for c in COMPS if c['wsrc'] == 'v')
NKC = 46 - NVC
NWT_V = (NVC + SPT - 1) // SPT
NWT_K = (NKC + SPT - 1) // SPT
NWT = NWT_V + NWT_K
NQT = (46 + SPT - 1) // SPT


def comp_slot(i):
    if COMPS[i]['wsrc'] == 'v':
        j = i
        return j // SPT, j % SPT
    j = i - NVC
    return NWT_V + j // SPT, j % SPT


def q_slot(i):
    return comp_slot(i)



_CLS = {0: [], 1: [], 2: []}
for _i in range(46):
    _CLS[comp_slot(_i)[1]].append(_i)
SIG_ORDER = _CLS[0] + ["G"] + _CLS[1] + _CLS[2]
SIGCOL = {c: j * 128 for j, c in enumerate(SIG_ORDER)}


def build_packers(UL, VR, VL):
    PW = np.zeros((128, NWT * 128), np.float32)
    TQ = np.zeros((64, NQT * 128), np.float32)
    for i, c in enumerate(COMPS):
        lvl = c['lvl']
        if c['kind'] == 'leaf' or (c['kind'] == 'top' and lvl < 2):
            wmat = np.eye(c['wrows'][1] - c['wrows'][0], dtype=np.float32)
            qmat = wmat
        elif c['kind'] == 'top':
            wmat = VR[lvl]
            qmat = VR[lvl]
        else:
            wmat = UL[lvl]
            qmat = VL[lvl]
        r = wmat.shape[1]
        wt, ws = comp_slot(i)
        rbase = (64 if c['wsrc'] == 'k' else 0)
        rows = np.arange(c['wrows'][0], c['wrows'][1])
        PW[rbase + rows, wt * 128 + 32 * ws: wt * 128 + 32 * ws + r] = wmat
        qt_, qs = q_slot(i)
        qrows = np.arange(c['qrows'][0], c['qrows'][1])
        TQ[qrows, qt_ * 128 + 32 * qs: qt_ * 128 + 32 * qs + r] = qmat
    return PW, TQ


def blockdiag(mat, nn, bs, side):
    r = mat.shape[1]
    out = np.zeros((64, nn * r), np.float32)
    for n in range(nn):
        off = 2 * bs * n + (bs if side == 'bot' else 0)
        out[off:off + bs, n * r:(n + 1) * r] = mat
    return out


def build_mask7():
    ml = np.zeros((64, 64), np.float32)
    for n in range(16):
        ml[4 * n:4 * n + 4, 4 * n:4 * n + 4] = 1
    trT, blT = [], []
    for lvl in range(4):
        bs = BLOCK_SIZES[lvl]
        mt = np.zeros((64, 64), np.float32)
        mb = np.zeros((64, 64), np.float32)
        for n in range(NUM_NODES[lvl]):
            top = slice(2 * bs * n, 2 * bs * n + bs)
            bot = slice(2 * bs * n + bs, 2 * bs * n + 2 * bs)
            mt[top, bot] = 1
            mb[bot, top] = 1
        trT.append(mt.T.copy())
        blT.append(mb.T.copy())
    p0 = ml + trT[0] + trT[1]
    return np.concatenate([p0, trT[2], trT[3]] + blT, axis=1)


def build_rope_tables():
    inv_freq = 1.0 / (10000.0 ** (np.arange(0, DH, 2, dtype=np.float32) / DH))
    tt = np.arange(T, dtype=np.float32)
    freqs = np.outer(tt, inv_freq)
    emb = np.concatenate([freqs, freqs], -1)
    cos = np.cos(emb).T
    sin = np.sin(emb).T
    sgn = np.concatenate([-np.ones(32), np.ones(32)]).astype(np.float32)
    sinw = sin * sgn[:, None]
    return (np.concatenate([cos, cos], 0).astype(np.float32),
            np.concatenate([sinw, sinw], 0).astype(np.float32))


def rope_perm_cols(w):
    perm = np.concatenate([np.arange(32, 64), np.arange(0, 32)])
    out = np.empty_like(w)
    for h in range(w.shape[1] // 64):
        out[:, h * 64:(h + 1) * 64] = w[:, h * 64:(h + 1) * 64][:, perm]
    return out


CONST_SHAPES = dict(
    ID128=([128, 128], BF16), IDF32=([128, 128], F32),
    CAUSREP=([128, 1024], BF16),
    TRIU=([128, 128], BF16), UTRI64=([64, 64], BF16),
    IOTAE=([128, 8], F32), ONESB=([128, 1], BF16), ONESM=([128, 128], BF16),
    ONESRF=([1, 128], F32),
    PW=([128, NWT * 128], BF16), TQ=([64, NQT * 128], BF16),
    BD=([32, 1024], BF16), MASK7=([64, 448], F32),
    COSQ=([128, 2048], F32), SINQ=([128, 2048], F32),
    COSK=([128, 2048], F32), SINK=([128, 2048], F32),
    L0W=([128, 1536], BF16), L0WO=([64, 1024], BF16), WIN=([128, 512], BF16),
    FFNP=([128, 50], F32), WOUT=([128, 1024], BF16), BROW=([128, 1024], F32),
    RWT=([128, 64], BF16),
)
CONST_NAMES = list(CONST_SHAPES)


def host_constants(inp):
    UL = [f32(inp[f'L0_attn_UL{l}']) for l in range(4)]
    VR = [f32(inp[f'L0_attn_VR{l}']) for l in range(4)]
    UR = [f32(inp[f'L0_attn_UR{l}']) for l in range(4)]
    VL = [f32(inp[f'L0_attn_VL{l}']) for l in range(4)]
    PW, TQ = build_packers(UL, VR, VL)
    cns = {}
    cns['ID128'] = np.eye(128, dtype=np.float32)
    cns['IDF32'] = np.eye(128, dtype=np.float32)
    caus = np.triu(np.ones((C, C), np.float32))
    cns['CAUSREP'] = np.tile(caus, (1, 8))
    tri = np.zeros((128, 128), np.float32)
    tri[np.triu_indices(128, 1)] = 1
    cns['TRIU'] = tri
    u64 = np.zeros((64, 64), np.float32)
    u64[np.triu_indices(64, 1)] = 1
    cns['UTRI64'] = u64
    cns['IOTAE'] = np.tile(np.arange(8, dtype=np.float32), (128, 1))
    cns['ONESB'] = np.ones((128, 1), np.float32)
    cns['ONESM'] = np.ones((128, 128), np.float32)
    cns['ONESRF'] = np.ones((1, 128), np.float32)
    cns['PW'] = PW
    cns['TQ'] = TQ
    bd = np.zeros((32, 16 * 64), np.float32)
    for lvl in range(4):
        for wi, W in enumerate([VR[lvl], UR[lvl], UL[lvl], VL[lvl]]):
            side = 'bot' if wi < 2 else 'top'
            m = blockdiag(W, NUM_NODES[lvl], BLOCK_SIZES[lvl], side)
            kr = NUM_NODES[lvl] * RANKS[lvl]
            bd[0:kr, (lvl * 4 + wi) * 64:(lvl * 4 + wi) * 64 + 64] = m.T
    cns['BD'] = bd
    cns['MASK7'] = build_mask7()
    cq, sq = build_rope_tables()
    cns['COSQ'] = cq / 8.0
    cns['SINQ'] = sq / 8.0
    cns['COSK'] = cq
    cns['SINK'] = sq
    wqkv0 = np.concatenate([f32(inp['L0_attn_Wq']), f32(inp['L0_attn_Wk']),
                            f32(inp['L0_attn_Wv'])], axis=1)
    cns['L0W'] = np.ascontiguousarray(
        wqkv0.reshape(8, 128, 192).transpose(1, 0, 2).reshape(128, 8 * 192))
    cns['L0WO'] = f32(inp['L0_attn_Wo'])
    cns['WIN'] = np.ascontiguousarray(
        f32(inp['L0_ffn_Win']).reshape(8, 128, 64).transpose(1, 0, 2).reshape(128, 512))
    ffnp = np.concatenate([f32(inp['L0_ffn_W1']), f32(inp['L0_ffn_b1']),
                           f32(inp['L0_ffn_W2']),
                           f32(inp['L0_ffn_bin_'])[:, None],
                           f32(inp['L0_ffn_b2'])[:, None]], axis=1)
    cns['FFNP'] = np.concatenate([ffnp, ffnp], axis=0)
    wout = f32(inp['L0_ffn_Wout'])
    cns['WOUT'] = np.concatenate([wout, wout], axis=0)
    brow = np.zeros((128, 1024), np.float32)
    brow[0] = f32(inp['L0_ffn_bout'])
    cns['BROW'] = brow
    cns['RWT'] = np.ascontiguousarray(
        f32(inp['L1_ffn_router_weight']).T.reshape(8, 128, 8)
        .transpose(1, 0, 2).reshape(128, 64))
    return cns



class Ctx:
    def __init__(self, nc, st):
        self.nc = nc
        self.st = st
        self._sems = {}
        self.sb = {}

    def sem(self, name):
        if name not in self._sems:
            h = self.st.enter_context(self.nc.semaphore(name))
            self._sems[name] = [h, 0]
        return self._sems[name][0]

    def inc(self, inst, name, k=1):
        self.sem(name)
        s = self._sems[name]
        s[1] += k
        inst.then_inc(s[0], k)
        return s[1]

    def val(self, name):
        self.sem(name)
        return self._sems[name][1]

    def wait(self, eng, name, v=None):
        self.sem(name)
        s = self._sems[name]
        eng.wait_ge(s[0], s[1] if v is None else v)

    def alloc(self, name, shape, dtype):
        h = self.st.enter_context(self.nc.sbuf_tensor(name, shape, dtype))
        self.sb[name] = h
        return h


class _EngWrap:

    def __init__(self, eng):
        object.__setattr__(self, "_e", eng)
        object.__setattr__(self, "last", None)

    def __getattr__(self, k):
        f = getattr(self._e, k)
        if callable(k) or callable(f):
            def g(*a, **kw):
                r = f(*a, **kw)
                try:
                    if hasattr(r, "then_inc"):
                        object.__setattr__(self, "last", r)
                except Exception:
                    pass
                return r
            return g
        return f


class _BlkWrap:
    def __init__(self, blk):
        self._b = blk

    def _mk(self, name):
        def deco(f):
            return getattr(self._b, name)(lambda e: f(_EngWrap(e)))
        return deco

    @property
    def tensor(self):
        return self._mk("tensor")

    @property
    def vector(self):
        return self._mk("vector")

    @property
    def scalar(self):
        return self._mk("scalar")

    @property
    def gpsimd(self):
        return self._mk("gpsimd")

    @property
    def sync(self):
        return self._mk("sync")


TRF_IDX = dict(G2=0, G3=1, H2=2, H3=3, RT0=4, RT1=5, RT2=6, RT3=7,
               WT0=8, WT1=9, WT2=10, WT3=11)


def build(debug_outputs=(), phases=("l0", "ln1", "route", "moe", "attn", "fin"),
          l0_parts=("ln", "proj", "kt", "ffn", "scan", "ytr", "asm"), scan_chunks=NCH,
          scan_steps=(1, 2, 3, 4, 5, 6, 7), moe_experts=NE, attn_heads=NH):
    nc = bass.Bass("TRN2", target_bir_lowering=False)
    st = ExitStack()
    ctx = Ctx(nc, st)

    x_tm = nc.dram_tensor("x_tm", [NTOK, D], F32, kind="ExternalInput").ap()
    wqkv = nc.dram_tensor("w_qkv_aug", [D, 5120], F32, kind="ExternalInput").ap()
    w_od = nc.dram_tensor("w_o", [D, D], F32, kind="ExternalInput").ap()
    w_preT = nc.dram_tensor("w_preT", [NE, D, 4096], F32, kind="ExternalInput").ap()
    w_gateT = nc.dram_tensor("w_gateT", [NE, D, 4096], F32, kind="ExternalInput").ap()
    w_postT = nc.dram_tensor("w_postT", [NE, 4096, D], F32, kind="ExternalInput").ap()
    cdram = {}
    for nm in CONST_NAMES:
        shp, dt = CONST_SHAPES[nm]
        cdram[nm] = nc.dram_tensor("c_" + nm, shp, dt, kind="ExternalInput").ap()
    out_t = nc.dram_tensor("out", [NTOK, D], F32, kind="ExternalOutput").ap()

    dbg = {}

    def idram(name, shape, dt):
        kind = "ExternalOutput" if name in debug_outputs else "Internal"
        h = nc.dram_tensor(name, shape, dt, kind=kind).ap()
        dbg[name] = h
        return h

    d_x1 = idram("d_x1", [NTOK, D], F32)
    d_xn1tm = idram("d_xn1tm", [NTOK, D], BF16)
    d_xn1fm = idram("d_xn1fm", [B_CORE * D, T], BF16)
    d_eo = idram("d_eo", [EOROWS, D], F32)
    d_idx = idram("d_idx", [NIDX, 1], U32)
    d_natt = idram("d_natt", [B_CORE * NH * DH, T], BF16)
    d_qk = idram("d_qk", [B_CORE * 2 * D, T], BF16)
    d_v = idram("d_v", [NTOK, D], BF16)
    d_rt = idram("d_rt", [128, 512], F32)
    d_y0 = idram("d_y0", [NTOK, 64], F32)

    for nm in CONST_NAMES:
        shp, dt = CONST_SHAPES[nm]
        ctx.alloc(nm, shp, dt)
    TRFSB = ctx.alloc("TRFSB", [64, 768], BF16)
    BOUTBC = ctx.alloc("BOUTBC", [128, 1024], F32)
    ARM = ctx.alloc("ARM", [128, 16384], BF16)
    ARK = ctx.alloc("ARK", [128, 8192], BF16)
    ARL = ctx.alloc("ARL", [128, 4096], BF16)
    ARS = ctx.alloc("ARS", [128, 2048], F32)
    ARW = ctx.alloc("ARW", [128, 6144], BF16)
    ARW2 = ctx.alloc("ARW2", [128, 6144], BF16)
    ARW3 = ctx.alloc("ARW3", [128, 6144], BF16)
    ARW4 = ctx.alloc("ARW4", [128, 6144], BF16)
    MOEO = ctx.alloc("MOEO", [128, 1024], F32)
    MOEO2 = ctx.alloc("MOEO2", [128, 1024], F32)
    RWIDE = ctx.alloc("RWIDE", [128, 256], F32)
    RTOP = ctx.alloc("RTOP", [128, 64], F32)
    RANKW = ctx.alloc("RANKW", [128, 256], F32)
    OH1W = ctx.alloc("OH1W", [128, 256], F32)
    OH2W = ctx.alloc("OH2W", [128, 256], F32)
    OHB = ctx.alloc("OHB", [128, 512], BF16)
    SLW = ctx.alloc("SLW", [128, 512], F32)
    EW = ctx.alloc("EW", [128, 64], F32)
    SELW = ctx.alloc("SELW", [128, 64], F32)
    KEEPW = ctx.alloc("KEEPW", [128, 64], F32)
    W12 = ctx.alloc("W12", [128, 64], F32)
    RROWS = ctx.alloc("RROWS", [128, 64], U32)
    RFW = ctx.alloc("RFW", [128, 64], F32)
    BASEST = ctx.alloc("BASEST", [8, 64], BF16)
    BASESTF = ctx.alloc("BASESTF", [8, 64], F32)
    SST = ctx.alloc("SST", [64, 66], F32)
    SSB = ctx.alloc("SSB", [64, 66], BF16)
    MISC = ctx.alloc("MISC", [128, 1024], F32)
    MISC2 = ctx.alloc("MISC2", [128, 1024], F32)
    MISCB = ctx.alloc("MISCB", [128, 512], BF16)
    IDXS = ctx.alloc("IDXS", [128, 16], U32)
    SCL = ctx.alloc("SCL", [128, 32], F32)
    XNT = ctx.alloc("XNT", [128, 1024], BF16)

    P0 = st.enter_context(nc.psum_tensor("P0", [128, 1024], F32))
    P1 = st.enter_context(nc.psum_tensor("P1", [128, 1024], F32))
    P2 = st.enter_context(nc.psum_tensor("P2", [128, 1024], F32))
    P3 = st.enter_context(nc.psum_tensor("P3", [128, 512], F32))
    PT = st.enter_context(nc.psum_tensor("PT", [128, 1024], BF16))

    blk = _BlkWrap(st.enter_context(nc.Block()))
    for s in ("s_ld", "s_g", "s_t", "s_v", "s_a", "s_o", "s_so", "s_bar"):
        ctx.sem(s)
    s_ld, s_g, s_t, s_v, s_a, s_o, s_so = ("s_ld", "s_g", "s_t", "s_v",
                                           "s_a", "s_o", "s_so")

    ID = ctx.sb["ID128"]
    ONESB = ctx.sb["ONESB"]
    ONESRF = ctx.sb["ONESRF"]
    ARM8 = ARM.rearrange("p (c u) -> p c u", c=8)
    PT8 = PT.rearrange("p (c u) -> p c u", c=8)

    def trf(name):
        i = TRF_IDX[name]
        return TRFSB[0:64, i * 64:(i + 1) * 64]

    def VC(eng, inst):
        val = ctx.inc(inst, "s_vc", 1)
        eng.wait_ge(ctx.sem("s_vc"), val)
        return inst

    def barrier():
        dma_sems = [(s, ctx.val(s)) for s in (s_ld, s_o, s_so, s_g)]
        for name in ("tensor", "vector", "scalar", "gpsimd", "sync"):
            def mark(eng):
                for s, v in dma_sems:
                    eng.wait_ge(ctx.sem(s), v)
                ctx.inc(eng.nop(), "s_bar", 1)
            getattr(blk, name)(mark)
        tgt = ctx.val("s_bar")
        for name in ("tensor", "vector", "scalar", "gpsimd", "sync"):
            def wt(eng):
                eng.wait_ge(ctx.sem("s_bar"), tgt)
            getattr(blk, name)(wt)

    @blk.sync
    def _(s):
        for nm in CONST_NAMES:
            ctx.inc(s.dma_start(out=ctx.sb[nm][:], in_=cdram[nm][:]), s_ld, 16)

    @blk.vector
    def _(v):
        v.memset(MISC[:], 0.0)
        ctx.inc(v.memset(IDXS[:], 0), s_v, 1)

    @blk.scalar
    def _(s):
        ctx.wait(s, s_v)
        for i0 in range(0, NIDX, 128):
            ctx.inc(s.dma_start(out=d_idx[i0:i0 + 128, :], in_=IDXS[:, 0:1]),
                    s_so, 16)
        ctx.inc(s.dma_start(out=d_eo[EOROWS - 1:EOROWS, :],
                            in_=MISC[0:1, 0:1024]), s_so, 16)

    @blk.tensor
    def _(t):
        ctx.wait(t, s_ld)
        bd = ctx.sb["BD"]

        def bdap(lvl, which):
            off = (lvl * 4 + which) * 64
            kr = NUM_NODES[lvl] * RANKS[lvl]
            return bd[0:kr, off:off + 64]
        specs = [(bdap(2, 2), bdap(2, 2)), (bdap(3, 2), bdap(3, 2)),
                 (bdap(2, 0), bdap(2, 0)), (bdap(3, 0), bdap(3, 0))]
        for lvl in range(4):
            specs.append((bdap(lvl, 0), bdap(lvl, 1)))
        for lvl in range(4):
            specs.append((bdap(lvl, 2), bdap(lvl, 3)))
        for i, (lhsT, rhs) in enumerate(specs):
            t.matmul(P0[0:64, i * 64:(i + 1) * 64], lhsT, rhs,
                     start=(i % 8 == 0), stop=True, skip_group_check=True)
        t.matmul(P1[:, 0:512], ONESRF[0:1, :], ctx.sb["BROW"][0:1, 0:512],
                 start=True, stop=True, skip_group_check=True)
        t.matmul(P1[:, 512:1024], ONESRF[0:1, :], ctx.sb["BROW"][0:1, 512:1024],
                 start=True, stop=True, skip_group_check=True)
        ctx.inc(t.last, s_t, 1)

    @blk.vector
    def _(v):
        ctx.wait(v, s_t)
        v.tensor_copy(TRFSB[0:64, :], P0[0:64, 0:768])
        ctx.inc(v.tensor_copy(BOUTBC[:], P1[:]), s_v, 1)

    barrier()

    def emit_ln_tile(dram_src, row0, fm_sink, after_tm=None):
        @blk.sync
        def _(s):
            ctx.wait(s, s_v)
            ctx.wait(s, s_a)
            ctx.inc(s.dma_start(out=MISC[:], in_=dram_src[row0:row0 + 128, :]),
                    s_ld, 16)
        ldv = ctx.val(s_ld)

        @blk.scalar
        def _(sc):
            sc.wait_ge(ctx.sem(s_ld), ldv)
            ctx.wait(sc, s_v)
            ctx.inc(sc.activation(MISC2[:], MISC[:], AF.Square,
                                  accum_out=SCL[:, 1:2]), s_a, 1)
        av = ctx.val(s_a)

        @blk.vector
        def _(v):
            v.wait_ge(ctx.sem(s_ld), ldv)
            VC(v, v.reduce_sum(SCL[:, 0:1], MISC[:], axis=AX.X))
            v.wait_ge(ctx.sem(s_a), av)
            VC(v, v.tensor_scalar_mul(SCL[:, 2:3], SCL[:, 0:1], 1.0 / D))
            VC(v, v.tensor_scalar_mul(SCL[:, 3:4], SCL[:, 1:2], 1.0 / D))
            VC(v, v.tensor_tensor(SCL[:, 4:5], SCL[:, 2:3], SCL[:, 2:3], ALU.mult))
            VC(v, v.tensor_sub(SCL[:, 5:6], SCL[:, 3:4], SCL[:, 4:5]))
            ctx.inc(v.tensor_scalar_add(SCL[:, 6:7], SCL[:, 5:6], 1e-5), s_v, 1)
        vv = ctx.val(s_v)

        @blk.scalar
        def _(sc):
            sc.wait_ge(ctx.sem(s_v), vv)
            ctx.inc(sc.activation(SCL[:, 7:8], SCL[:, 6:7], AF.Sqrt), s_a, 1)
        av2 = ctx.val(s_a)

        @blk.vector
        def _(v):
            v.wait_ge(ctx.sem(s_a), av2)
            VC(v, v.reciprocal(SCL[:, 8:9], SCL[:, 7:8]))
            ctx.wait(v, s_so)
            ctx.inc(v.tensor_scalar(XNT[:], MISC[:], SCL[:, 2:3], SCL[:, 8:9],
                                    ALU.subtract, ALU.mult), s_v, 1)
        vv2 = ctx.val(s_v)
        if after_tm is not None:
            after_tm(vv2)

        @blk.tensor
        def _(t):
            t.wait_ge(ctx.sem(s_v), vv2)
            ctx.wait(t, s_v)
            for j in range(8):
                t.matmul(PT[:, j * 128:(j + 1) * 128],
                         XNT[:, j * 128:(j + 1) * 128], ID[:],
                         start=True, stop=True, is_transpose=True,
                         skip_group_check=True)
            ctx.inc(t.last, s_t, 1)
        tv = ctx.val(s_t)

        @blk.vector
        def _(v):
            v.wait_ge(ctx.sem(s_t), tv)
            fm_sink(v)
            ctx.inc(v.last, s_v, 1)

    def emit_scan_chunk(b, ci):
        SS = scan_steps
        PWsb = ctx.sb["PW"]
        TQsb = ctx.sb["TQ"]
        CAUS = ctx.sb["CAUSREP"]
        M7 = ctx.sb["MASK7"]
        qf = ARK[0:64, ci * 128:ci * 128 + 128]
        kf = ARK[64:128, ci * 128:ci * 128 + 128]
        vf = ARK[0:64, 2048 + ci * 128:2048 + ci * 128 + 128]
        kf0 = ARL[0:64, 2048 + ci * 128:2048 + ci * 128 + 128]
        ktc = ARK[:, 4096 + ci * 64:4096 + ci * 64 + 64]
        vtc = ARK[:, 5120 + ci * 64:5120 + ci * 64 + 64]
        ytm = ARK[:, 6144 + ci * 64:6144 + ci * 64 + 64]
        SIG = ARM[:, 0:6016]
        WP = ARM[:, 6144:8192]
        QP = ARM[:, 8192:10240]
        US = ARM[:, 10240:10624]
        t1sb = MISCB[0:64, 0:64]
        t2sb = MISCB[0:64, 64:128]
        SAND = MISCB[0:64, 128:512]

        def ustream_ap(name, c0, c1):
            if name == 'ktm':
                return ktc[:, c0:c1]
            i = dict(G2k=0, G3k=1, R0v=2, R1v=3, R2v=4, R3v=5)[name]
            return US[:, i * 64 + c0:i * 64 + c1]

        if 1 not in SS:
            return

        @blk.tensor
        def _(t):
            ctx.wait(t, s_v)
            for j in range(8):
                t.matmul(P0[:, (j % 8) * 128:(j % 8) * 128 + 128],
                         PWsb[0:64, j * 128:j * 128 + 128], vf,
                         start=(j % 4 == 0), stop=True, skip_group_check=True)
            for j in range(8, 11):
                t.matmul(P1[:, (j - 8) * 128:(j - 8) * 128 + 128],
                         PWsb[0:64, j * 128:j * 128 + 128], vf,
                         start=(j == 8), stop=True, skip_group_check=True)
            for j in range(11, 15):
                t.matmul(P1[:, 512 + (j - 11) * 128:512 + (j - 11) * 128 + 128],
                         PWsb[64:128, j * 128:j * 128 + 128], kf,
                         start=(j == 11), stop=True, skip_group_check=True)
            t.matmul(P2[:, 0:128], PWsb[64:128, 15 * 128:15 * 128 + 128], kf,
                     start=True, stop=True, skip_group_check=True)
            for j in range(4):
                t.matmul(P2[:, 512 + j * 128:512 + j * 128 + 128],
                         TQsb[0:64, j * 128:j * 128 + 128], qf,
                         start=(j == 0), stop=True, skip_group_check=True)
            ctx.inc(t.last, s_t, 1)
        tv1 = ctx.val(s_t)

        @blk.vector
        def _(v):
            v.wait_ge(ctx.sem(s_t), tv1)
            v.tensor_copy(WP[:, 0:1024], P0[:])
            v.tensor_copy(WP[:, 1024:1408], P1[:, 0:384])
            v.tensor_copy(WP[:, 1408:1920], P1[:, 512:1024])
            v.tensor_copy(WP[:, 1920:2048], P2[:, 0:128])
            v.tensor_copy(QP[:, 0:512], P2[:, 512:1024])
            ctx.inc(v.last, s_v, 1)
        vv1 = ctx.val(s_v)

        if 2 not in SS:
            return

        @blk.tensor
        def _(t):
            t.wait_ge(ctx.sem(s_v), vv1)
            for j in range(4, 12):
                t.matmul(P0[:, (j - 4) * 128:(j - 4) * 128 + 128],
                         TQsb[0:64, j * 128:j * 128 + 128], qf,
                         start=((j - 4) % 4 == 0), stop=True,
                         skip_group_check=True)
            for j in range(12, 16):
                t.matmul(P1[:, (j - 12) * 128:(j - 12) * 128 + 128],
                         TQsb[0:64, j * 128:j * 128 + 128], qf,
                         start=(j == 12), stop=True, skip_group_check=True)
            us_specs = [("G2", kf0), ("G3", kf0), ("RT0", vf), ("RT1", vf),
                        ("RT2", vf), ("RT3", vf)]
            for i, (nm, src2) in enumerate(us_specs):
                t.matmul(P3[:, 64 + i * 64:64 + i * 64 + 64], src2, trf(nm),
                         start=(i == 0), stop=True, skip_group_check=True)
            t.matmul(P3[0:64, 0:64], ktc, vtc, start=False, stop=True,
                     skip_group_check=True)
            t.matmul(P3[0:64, 448:512], vtc, ktc, start=False, stop=True,
                     skip_group_check=True)
            ctx.inc(t.last, s_t, 1)
        tv2 = ctx.val(s_t)

        @blk.vector
        def _(v):
            v.wait_ge(ctx.sem(s_t), tv2)
            v.tensor_copy(QP[:, 512:1536], P0[:])
            v.tensor_copy(QP[:, 1536:2048], P1[:, 0:512])
            v.tensor_copy(US[:], P3[:, 64:448])
            v.tensor_copy(t1sb, P3[0:64, 0:64])
            v.tensor_copy(t2sb, P3[0:64, 448:512])
            ctx.inc(v.last, s_v, 1)
        vv2 = ctx.val(s_v)

        def score_mm(t, i, ps, col, first):
            wt, ws = comp_slot(i)
            qt_, qs = q_slot(i)
            assert ws == qs
            if wt < 8:
                wcol = wt * 128
            elif wt < 11:
                wcol = 1024 + (wt - 8) * 128
            elif wt < 15:
                wcol = 1408 + (wt - 11) * 128
            else:
                wcol = 1920
            qcol = qt_ * 128
            t.matmul(ps[:, col:col + 128],
                     WP[32 * ws:32 * ws + 32, wcol:wcol + 128],
                     QP[32 * qs:32 * qs + 32, qcol:qcol + 128],
                     start=first, stop=True, skip_group_check=True)

        if 3 not in SS:
            return
        @blk.tensor
        def _(t):
            t.wait_ge(ctx.sem(s_v), vv2)
            for n, i in enumerate(_CLS[0]):
                ps = (P0, P1)[n // 8]
                col = (n % 8) * 128
                score_mm(t, i, ps, col, (n % 4 == 0))
            t.matmul(P2[:, 0:128], kf0, qf, start=True, stop=True,
                     skip_group_check=True)
            ctx.inc(t.last, s_t, 1)
        tv3 = ctx.val(s_t)

        @blk.vector
        def _(v):
            v.wait_ge(ctx.sem(s_t), tv3)
            v.tensor_tensor(SIG[:, 0:1024], P0[:], CAUS[:], ALU.mult)
            v.tensor_tensor(SIG[:, 1024:2048], P1[:], CAUS[:], ALU.mult)
            v.tensor_tensor(SIG[:, 2048:2176], P2[:, 0:128], CAUS[:, 0:128],
                            ALU.mult)
            ctx.inc(v.last, s_v, 1)
        vv3 = ctx.val(s_v)

        if 4 not in SS:
            return
        @blk.tensor
        def _(t):
            t.wait_ge(ctx.sem(s_v), vv3)
            for n, i in enumerate(_CLS[1]):
                ps = (P0, P1)[n // 8]
                col = (n % 8) * 128
                score_mm(t, i, ps, col, (n % 4 == 0))
            ctx.inc(t.last, s_t, 1)
        tv4a = ctx.val(s_t)

        @blk.vector
        def _(v):
            v.wait_ge(ctx.sem(s_t), tv4a)
            n1 = len(_CLS[1])
            v.tensor_tensor(SIG[:, 2176:3200], P0[:], CAUS[:], ALU.mult)
            w = (n1 - 8) * 128
            v.tensor_tensor(SIG[:, 3200:3200 + w], P1[:, 0:w], CAUS[:, 0:w],
                            ALU.mult)
            ctx.inc(v.last, s_v, 1)
        vv4a = ctx.val(s_v)

        @blk.tensor
        def _(t):
            t.wait_ge(ctx.sem(s_v), vv4a)
            for n, i in enumerate(_CLS[2]):
                ps = (P0, P1)[n // 8]
                col = (n % 8) * 128
                score_mm(t, i, ps, col, (n % 4 == 0))
            ctx.inc(t.last, s_t, 1)
        tv4 = ctx.val(s_t)

        @blk.vector
        def _(v):
            v.wait_ge(ctx.sem(s_t), tv4)
            base2 = (17 + len(_CLS[1])) * 128
            n2 = len(_CLS[2])
            v.tensor_tensor(SIG[:, base2:base2 + 1024], P0[:], CAUS[:], ALU.mult)
            w = (n2 - 8) * 128
            v.tensor_tensor(SIG[:, base2 + 1024:base2 + 1024 + w], P1[:, 0:w],
                            CAUS[:, 0:w], ALU.mult)
            ctx.inc(v.last, s_v, 1)
        vv4 = ctx.val(s_v)

        if 5 not in SS:
            return
        @blk.tensor
        def _(t):
            t.wait_ge(ctx.sem(s_v), vv4)
            t.matmul(P3[:, 0:64], qf, SSB[0:64, 0:64], start=True, stop=False,
                     skip_group_check=True)
            for i in range(46):
                c0, c1 = COMPS[i]['ucols']
                sc0 = SIGCOL[i]
                t.matmul(P3[:, c0:c1], SIG[:, sc0:sc0 + 128],
                         ustream_ap(COMPS[i]['ustream'], c0, c1),
                         start=False, stop=True, skip_group_check=True)
            gc = SIGCOL["G"]
            t.matmul(P3[:, 64:65], SIG[:, gc:gc + 128], ONESB[:],
                     start=False, stop=False, skip_group_check=True)
            t.matmul(P3[:, 64:65], qf, SSB[0:64, 64:65], start=False, stop=True,
                     skip_group_check=True)
            t.matmul(P3[0:64, 65:66], ktc, ONESB[:], start=False, stop=True,
                     skip_group_check=True)
            t.matmul(P3[0:64, 128:192], t1sb, trf("G2"), start=False, stop=True,
                     skip_group_check=True)
            t.matmul(P3[0:64, 192:256], t1sb, trf("G3"), start=False, stop=True,
                     skip_group_check=True)
            for lvl in range(4):
                t.matmul(P3[0:64, 256 + lvl * 64:256 + lvl * 64 + 64], t2sb,
                         trf(f"RT{lvl}"), start=False, stop=True,
                         skip_group_check=True)
            ctx.inc(t.last, s_t, 1)
        tv5 = ctx.val(s_t)

        @blk.vector
        def _(v):
            v.wait_ge(ctx.sem(s_t), tv5)
            VC(v, v.tensor_scalar_max(SCL[:, 10:11], P3[:, 64:65], 1e-6))
            VC(v, v.reciprocal(SCL[:, 11:12], SCL[:, 10:11]))
            v.tensor_scalar(ytm[:], P3[:, 0:64], SCL[:, 11:12], None, ALU.mult)
            v.tensor_copy(SAND, P3[0:64, 128:512])
            VC(v, v.tensor_add(SST[0:64, 64:65], SST[0:64, 64:65],
                               P3[0:64, 65:66]))
            ctx.inc(v.last, s_v, 1)
        vv5 = ctx.val(s_v)

        if 6 not in SS:
            return
        @blk.tensor
        def _(t):
            t.wait_ge(ctx.sem(s_v), vv5)
            t.matmul(P0[0:64, 0:64], trf("H2"), SAND[0:64, 0:64], start=True,
                     stop=True, skip_group_check=True)
            t.matmul(P0[0:64, 64:128], trf("H3"), SAND[0:64, 64:128],
                     start=False, stop=True, skip_group_check=True)
            for lvl in range(4):
                t.matmul(P0[0:64, 128 + lvl * 64:128 + lvl * 64 + 64],
                         trf(f"WT{lvl}"), SAND[0:64, 128 + lvl * 64:128 + lvl * 64 + 64],
                         start=False, stop=True, skip_group_check=True)
            ctx.inc(t.last, s_t, 1)
        tv6 = ctx.val(s_t)

        @blk.vector
        def _(v):
            v.wait_ge(ctx.sem(s_t), tv6)
            v.tensor_tensor(ARS[0:64, 0:64], t2sb, M7[:, 0:64], ALU.mult)
            VC(v, v.tensor_tensor(ARS[0:64, 64:448], P0[0:64, 0:384],
                                  M7[:, 64:448], ALU.mult))
            for p in range(7):
                VC(v, v.tensor_add(SST[0:64, 0:64], SST[0:64, 0:64],
                                   ARS[0:64, p * 64:p * 64 + 64]))
            v.tensor_copy(SSB[0:64, 0:64], SST[0:64, 0:64])
            v.tensor_copy(SSB[0:64, 64:65], SST[0:64, 64:65])
            ctx.inc(v.last, s_v, 1)

    def emit_l0(b):
      if "ln" in l0_parts:
        for tt in range(16):
            def fm_sink(v, tt=tt):
                v.tensor_copy(ARM8[:, :, tt * 128:tt * 128 + 128], PT8[:, :, :])
            emit_ln_tile(x_tm, b * T + tt * 128, fm_sink)

      if "proj" in l0_parts:
        for tt in range(4):
            @blk.tensor
            def _(t, tt=tt):
                ctx.wait(t, s_v)
                for cc in range(8):
                    t.matmul(P0[:, 0:512],
                             ctx.sb["L0W"][:, cc * 192:cc * 192 + 128],
                             ARM8[:, cc, tt * 512:tt * 512 + 512],
                             start=(cc == 0), stop=(cc == 7),
                             skip_group_check=True)
                for cc in range(8):
                    t.matmul(P3[0:64, 0:512],
                             ctx.sb["L0W"][:, cc * 192 + 128:cc * 192 + 192],
                             ARM8[:, cc, tt * 512:tt * 512 + 512],
                             start=(cc == 0), stop=(cc == 7),
                             skip_group_check=True)
                ctx.inc(t.last, s_t, 1)
            tv0 = ctx.val(s_t)

            @blk.vector
            def _(v, tt=tt):
                v.wait_ge(ctx.sem(s_t), tv0)
                ctx.wait(v, s_a)
                ctx.inc(v.tensor_scalar_min(MISC[:, 0:512], P0[:, 0:512], 0.0),
                        s_v, 1)
            vv0 = ctx.val(s_v)

            @blk.scalar
            def _(sc, tt=tt):
                sc.wait_ge(ctx.sem(s_v), vv0)
                ctx.inc(sc.activation(MISC2[:, 0:512], MISC[:, 0:512], AF.Exp),
                        s_a, 1)
            av0 = ctx.val(s_a)

            @blk.vector
            def _(v, tt=tt):
                v.wait_ge(ctx.sem(s_a), av0)
                VC(v, v.tensor_scalar_add(MISC[:, 0:512], P0[:, 0:512], 1.0))
                VC(v, v.tensor_tensor(ARK[:, tt * 512:tt * 512 + 512],
                                      MISC[:, 0:512], MISC2[:, 0:512], ALU.max))
                v.tensor_copy(ARL[0:64, 2048 + tt * 512:2048 + tt * 512 + 512],
                              ARK[64:128, tt * 512:tt * 512 + 512])
                ctx.inc(v.tensor_copy(
                    ARK[0:64, 2048 + tt * 512:2048 + tt * 512 + 512],
                    P3[0:64, 0:512]), s_v, 1)

      if "kt" in l0_parts:
        for half in range(2):
            @blk.tensor
            def _(t, half=half):
                ctx.wait(t, s_v)
                for j in range(8):
                    ci = half * 8 + j
                    t.matmul(PT[:, j * 128:j * 128 + 64],
                             ARL[0:64, 2048 + ci * 128:2048 + ci * 128 + 128],
                             ID[0:64, 0:64],
                             start=True, stop=True, is_transpose=True,
                             skip_group_check=True)
                    t.matmul(PT[:, j * 128 + 64:j * 128 + 128],
                             ARK[0:64, 2048 + ci * 128:2048 + ci * 128 + 128],
                             ID[0:64, 0:64], start=True, stop=True,
                             is_transpose=True, skip_group_check=True)
                ctx.inc(t.last, s_t, 1)
            tvh = ctx.val(s_t)

            @blk.vector
            def _(v, half=half):
                v.wait_ge(ctx.sem(s_t), tvh)
                v.tensor_copy(
                    ARK[:, 4096 + half * 512:4096 + half * 512 + 512]
                    .rearrange("p (ci u) -> p ci u", ci=8),
                    PT8[:, :, 0:64])
                ctx.inc(v.tensor_copy(
                    ARK[:, 5120 + half * 512:5120 + half * 512 + 512]
                    .rearrange("p (ci u) -> p ci u", ci=8),
                    PT8[:, :, 64:128]), s_v, 1)

      if "ffn" in l0_parts:
        FFNP = ctx.sb["FFNP"]
        for hb in range(2):
            for t2 in range(2):
                @blk.tensor
                def _(t, hb=hb, t2=t2):
                    ctx.wait(t, s_v)
                    for cc in range(8):
                        t.matmul(P1[hb * 64:hb * 64 + 64,
                                    t2 * 512:t2 * 512 + 512],
                                 ctx.sb["WIN"][:, cc * 64:cc * 64 + 64],
                                 ARM8[:, cc, hb * 1024 + t2 * 512:
                                      hb * 1024 + t2 * 512 + 512],
                                 start=(cc == 0), stop=(cc == 7),
                                 skip_group_check=True)
                    ctx.inc(t.last, s_t, 1)
        tv2 = ctx.val(s_t)

        @blk.vector
        def _(v):
            v.wait_ge(ctx.sem(s_t), tv2)
            ctx.inc(v.tensor_scalar_add(ARS[:, 0:1024], P1[:], FFNP[:, 48:49]),
                    s_v, 1)
        vv2 = ctx.val(s_v)

        for i in range(16):
            @blk.vector
            def _(v, i=i):
                if i == 0:
                    v.wait_ge(ctx.sem(s_v), vv2)
                ctx.wait(v, s_a)
                ctx.inc(v.tensor_scalar(MISC[:], ARS[:, 0:1024],
                                        FFNP[:, i:i + 1], FFNP[:, 16 + i:17 + i],
                                        ALU.mult, ALU.add), s_v, 1)
            vh = ctx.val(s_v)

            @blk.scalar
            def _(sc, i=i):
                sc.wait_ge(ctx.sem(s_v), vh)
                ctx.inc(sc.activation(MISC2[:], MISC[:], AF.Relu), s_a, 1)
            ah = ctx.val(s_a)

            @blk.vector
            def _(v, i=i):
                v.wait_ge(ctx.sem(s_a), ah)
                if i == 0:
                    gi = v.tensor_scalar(ARS[:, 1024:2048], MISC2[:],
                                         FFNP[:, 32 + i:33 + i],
                                         FFNP[:, 49:50], ALU.mult, ALU.add)
                else:
                    gi = v.scalar_tensor_tensor(
                        ARS[:, 1024:2048], MISC2[:], FFNP[:, 32 + i:33 + i],
                        ARS[:, 1024:2048], ALU.mult, ALU.add)
                gv = ctx.inc(gi, s_v, 1)
                v.wait_ge(ctx.sem(s_v), gv)

      if "ffn" in l0_parts:
        @blk.vector
        def _(v):
            ctx.wait(v, s_v)
            ctx.inc(v.tensor_copy(XNT[:], ARS[:, 1024:2048]), s_v, 1)

      if "scan" in l0_parts:
        @blk.vector
        def _(v):
            v.memset(SST[:], 0.0)
            ctx.inc(v.memset(SSB[:], 0.0), s_v, 1)
        for ci in range(scan_chunks):
            emit_scan_chunk(b, ci)

      if "scan" in l0_parts and "d_sig" in debug_outputs and b == 0:
            @blk.vector
            def _(v):
                ctx.wait(v, s_v)
                ctx.inc(v.last, s_v, 1) if False else None
                ctx.inc(v.tensor_copy(MISCB[:, 0:512], ARM[:, 0:512]), s_v, 1)
            sigv = ctx.val(s_v)

            @blk.sync
            def _(s):
                s.wait_ge(ctx.sem(s_v), sigv)
                for sl3 in range(3):
                    ctx.inc(s.dma_start(
                        out=d_natt[sl3 * 128:sl3 * 128 + 128, 0:2048],
                        in_=ARM[:, sl3 * 2048:sl3 * 2048 + 2048]), s_o, 16)
                ctx.inc(s.dma_start(out=d_natt[384:512, 0:2048],
                                    in_=ARM[:, 6144:8192]), s_o, 16)
                ctx.inc(s.dma_start(out=d_natt[512:640, 0:2048],
                                    in_=ARM[:, 8192:10240]), s_o, 16)
                ctx.inc(s.dma_start(out=d_natt[640:768, 0:2048],
                                    in_=ARK[:, 0:2048]), s_o, 16)
                ctx.inc(s.dma_start(out=d_natt[768:896, 0:2048],
                                    in_=ARK[:, 2048:4096]), s_o, 16)
                ctx.inc(s.dma_start(out=d_natt[896:1024, 0:2048],
                                    in_=ARK[:, 4096:6144]), s_o, 16)
                ctx.inc(s.dma_start(out=d_natt[1024:1152, 0:384],
                                    in_=ARM[:, 10240:10624]), s_o, 16)

      if "scan" in l0_parts and "d_y0" in debug_outputs:
            @blk.vector
            def _(v):
                ctx.wait(v, s_v)
                ctx.inc(v.tensor_copy(MISC[:, 0:scan_chunks * 64],
                                      ARK[:, 6144:6144 + scan_chunks * 64]),
                        s_v, 1)
            dvv = ctx.val(s_v)

            @blk.sync
            def _(s):
                s.wait_ge(ctx.sem(s_v), dvv)
                dy = d_y0.rearrange("(a p) u -> a p u", p=128)
                for tt in range(scan_chunks):
                    ctx.inc(s.dma_start(
                        out=dy[b * 16 + tt],
                        in_=MISC[:, tt * 64:tt * 64 + 64]), s_o, 16)

      if "ytr" in l0_parts:
        for half in range(2):
            @blk.tensor
            def _(t, half=half):
                ctx.wait(t, s_v)
                for j in range(8):
                    ci = half * 8 + j
                    t.matmul(PT[0:64, j * 128:j * 128 + 128],
                             ARK[:, 6144 + ci * 64:6144 + ci * 64 + 64], ID[:],
                             start=True, stop=True, is_transpose=True,
                             skip_group_check=True)
                ctx.inc(t.last, s_t, 1)
            tvy = ctx.val(s_t)

            @blk.vector
            def _(v, half=half):
                v.wait_ge(ctx.sem(s_t), tvy)
                ctx.inc(v.tensor_copy(
                    ARL[0:64, half * 1024:half * 1024 + 1024], PT[0:64, :]),
                    s_v, 1)

      if "asm" in l0_parts:
        for tt in range(16):
            @blk.sync
            def _(s, tt=tt):
                ctx.wait(s, s_v)
                ctx.inc(s.dma_start(
                    out=MISC2[:],
                    in_=x_tm[b * T + tt * 128:b * T + tt * 128 + 128, :]),
                    s_ld, 16)
            ldv = ctx.val(s_ld)

            @blk.tensor
            def _(t, tt=tt):
                ctx.wait(t, s_v)
                hb = tt // 8
                for half in range(2):
                    t.matmul(P1[:, half * 512:half * 512 + 512],
                             ARL[0:64, tt * 128:tt * 128 + 128],
                             ctx.sb["L0WO"][0:64, half * 512:half * 512 + 512],
                             start=True, stop=True, skip_group_check=True)
                gsl = XNT
                for half in range(2):
                    t.matmul(P2[:, half * 512:half * 512 + 512],
                             gsl[hb * 64:hb * 64 + 64,
                                 (tt % 8) * 128:(tt % 8) * 128 + 128],
                             ctx.sb["WOUT"][hb * 64:hb * 64 + 64,
                                            half * 512:half * 512 + 512],
                             start=True, stop=True, skip_group_check=True)
                ctx.inc(t.last, s_t, 1)
            tvA = ctx.val(s_t)

            @blk.vector
            def _(v, tt=tt):
                v.wait_ge(ctx.sem(s_t), tvA)
                v.wait_ge(ctx.sem(s_ld), ldv)
                ctx.wait(v, s_o)
                VC(v, v.tensor_add(MISC[:], MISC2[:], P1[:]))
                VC(v, v.tensor_add(MISC[:], MISC[:], P2[:]))
                ctx.inc(v.tensor_add(MISC[:], MISC[:], BOUTBC[:]), s_v, 1)
            vvA = ctx.val(s_v)

            @blk.sync
            def _(s, tt=tt):
                s.wait_ge(ctx.sem(s_v), vvA)
                ctx.inc(s.dma_start(
                    out=d_x1[b * T + tt * 128:b * T + tt * 128 + 128, :],
                    in_=MISC[:]), s_o, 16)

    def emit_ln1(b):
        for tt in range(16):
            gtile = b * 16 + tt

            def after_tm(vv2, tt=tt, gtile=gtile):
                @blk.scalar
                def _(sc):
                    sc.wait_ge(ctx.sem(s_v), vv2)
                    ctx.inc(sc.dma_start(
                        out=d_xn1tm[gtile * 128:gtile * 128 + 128, :],
                        in_=XNT[:]), s_so, 16)

            def fm_sink(v, tt=tt, gtile=gtile):
                v.tensor_copy(ARM8[:, :, tt * 128:tt * 128 + 128], PT8[:, :, :])
            emit_ln_tile(d_x1, b * T + tt * 128, fm_sink, after_tm=after_tm)
            vfm = ctx.val(s_v)

            @blk.tensor
            def _(t, tt=tt, gtile=gtile):
                t.wait_ge(ctx.sem(s_v), vfm)
                for cc in range(8):
                    t.matmul(P3[:, 0:8],
                             ARM8[:, cc, tt * 128:tt * 128 + 128],
                             ctx.sb["RWT"][:, cc * 8:cc * 8 + 8],
                             start=(cc == 0), stop=(cc == 7),
                             skip_group_check=True)
                ctx.inc(t.last, s_t, 1)
            tvr = ctx.val(s_t)

            @blk.vector
            def _(v, tt=tt, gtile=gtile):
                v.wait_ge(ctx.sem(s_t), tvr)
                ctx.inc(v.tensor_copy(RWIDE[:, gtile * 8:gtile * 8 + 8],
                                      P3[:, 0:8]), s_v, 1)

        @blk.sync
        def _(s):
            ctx.wait(s, s_v)
            fmv = d_xn1fm.rearrange("(bb c p) u -> bb c p u", bb=B_CORE, p=128)
            for cc in range(8):
                ctx.inc(s.dma_start(out=fmv[b, cc],
                                    in_=ARM8[:, cc, :]), s_o, 16)

    def emit_route():
        TRIU = ctx.sb["TRIU"]
        ONESM = ctx.sb["ONESM"]
        IOTAE = ctx.sb["IOTAE"]
        for g in range(32):
            @blk.vector
            def _(v, g=g):
                if g == 0:
                    ctx.wait(v, s_v)
                lg = RWIDE[:, g * 8:g * 8 + 8]
                VC(v, v.max(MISC[:, 0:8], lg))
                v.tensor_copy(RTOP[:, g:g + 1], MISC[:, 0:1])
                v.tensor_copy(RTOP[:, 32 + g:32 + g + 1], MISC[:, 1:2])
                VC(v, v.max_index(IDXS[:, 0:8], lg, MISC[:, 0:8]))
                ctx.inc(v.tensor_copy(RANKW[:, g * 8:g * 8 + 8],
                                      IDXS[:, 0:8]), s_v, 1)

        @blk.vector
        def _(v):
            ctx.wait(v, s_v)
            VC(v, v.tensor_single_scalar(OH1W[:], RANKW[:], 0.0, ALU.is_equal))
            VC(v, v.tensor_single_scalar(OH2W[:], RANKW[:], 1.0, ALU.is_equal))
            ctx.inc(v.tensor_sub(W12[:, 0:32], RTOP[:, 0:32], RTOP[:, 32:64]),
                    s_v, 1)
        vv = ctx.val(s_v)

        @blk.scalar
        def _(sc):
            sc.wait_ge(ctx.sem(s_v), vv)
            ctx.inc(sc.activation(W12[:, 0:32], W12[:, 0:32], AF.Sigmoid),
                    s_a, 1)
        av = ctx.val(s_a)

        @blk.vector
        def _(v):
            v.wait_ge(ctx.sem(s_a), av)
            VC(v, v.tensor_scalar(W12[:, 32:64], W12[:, 0:32], -1.0, 1.0,
                                  ALU.mult, ALU.add))
            for g in range(32):
                VC(v, v.tensor_tensor(MISC[:, 0:8], OH1W[:, g * 8:g * 8 + 8],
                                      IOTAE[:], ALU.mult))
                VC(v, v.reduce_sum(EW[:, 2 * g:2 * g + 1], MISC[:, 0:8],
                                   axis=AX.X))
                VC(v, v.tensor_tensor(MISC[:, 0:8], OH2W[:, g * 8:g * 8 + 8],
                                      IOTAE[:], ALU.mult))
                VC(v, v.reduce_sum(EW[:, 2 * g + 1:2 * g + 2], MISC[:, 0:8],
                                   axis=AX.X))
            v.tensor_copy(OHB[:, 0:256], OH1W[:])
            ctx.inc(v.tensor_copy(OHB[:, 256:512], OH2W[:]), s_v, 1)
        vv2 = ctx.val(s_v)

        @blk.tensor
        def _(t):
            t.wait_ge(ctx.sem(s_v), vv2)
            for g in range(32):
                for k in range(2):
                    t.matmul(P3[0:8, (2 * g + k):(2 * g + k) + 1],
                             OHB[:, k * 256 + g * 8:k * 256 + g * 8 + 8],
                             ONESB[:], start=(g == 0 and k == 0), stop=True,
                             skip_group_check=True)
            ctx.inc(t.last, s_t, 1)
        tv = ctx.val(s_t)

        @blk.vector
        def _(v):
            v.wait_ge(ctx.sem(s_t), tv)
            ctx.inc(v.tensor_copy(MISCB[0:8, 0:64], P3[0:8, 0:64]), s_v, 1)
        vv3 = ctx.val(s_v)

        @blk.tensor
        def _(t):
            t.wait_ge(ctx.sem(s_v), vv3)
            t.matmul(PT[0:64, 0:8], MISCB[0:8, 0:64], ID[0:8, 0:8],
                     start=True, stop=True, is_transpose=True,
                     skip_group_check=True)
            ctx.inc(t.last, s_t, 1)
        tv2 = ctx.val(s_t)

        @blk.vector
        def _(v):
            v.wait_ge(ctx.sem(s_t), tv2)
            ctx.inc(v.tensor_copy(MISCB[0:64, 64:72], PT[0:64, 0:8]), s_v, 1)
        vv4 = ctx.val(s_v)

        @blk.tensor
        def _(t):
            t.wait_ge(ctx.sem(s_v), vv4)
            t.matmul(P3[0:64, 64:72], ctx.sb["UTRI64"][:], MISCB[0:64, 64:72],
                     start=True, stop=True, skip_group_check=True)
            ctx.inc(t.last, s_t, 1)
        tv3 = ctx.val(s_t)

        @blk.vector
        def _(v):
            v.wait_ge(ctx.sem(s_t), tv3)
            ctx.inc(v.tensor_copy(MISC[0:64, 256:264], P3[0:64, 64:72]), s_v, 1)
        vv5 = ctx.val(s_v)

        @blk.tensor
        def _(t):
            t.wait_ge(ctx.sem(s_v), vv5)
            t.matmul(P0[0:8, 0:64], MISC[0:64, 256:264],
                     ctx.sb["IDF32"][0:64, 0:64],
                     start=True, stop=True, is_transpose=True,
                     skip_group_check=True)
            ctx.inc(t.last, s_t, 1)
        tv4 = ctx.val(s_t)

        @blk.vector
        def _(v):
            v.wait_ge(ctx.sem(s_t), tv4)
            ctx.inc(v.tensor_copy(BASESTF[0:8, 0:64], P0[0:8, 0:64]), s_v, 1)
        vv6 = ctx.val(s_v)

        for g in range(32):
            for k in range(2):
                col = 2 * g + k

                @blk.tensor
                def _(t, g=g, k=k, col=col):
                    if g == 0 and k == 0:
                        t.wait_ge(ctx.sem(s_v), vv6)
                    else:
                        ctx.wait(t, s_v)
                    ohk = OHB[:, k * 256 + g * 8:k * 256 + g * 8 + 8]
                    t.matmul(P3[:, 128:136], TRIU[:], ohk, start=True,
                             stop=True, skip_group_check=True)
                    bb = BASESTF[0:8, col:col + 1].broadcast_to([8, 128])
                    t.matmul(P2[:, 0:8], bb, ctx.sb["IDF32"][0:8, 0:8],
                             start=True, stop=True, skip_group_check=True)
                    ctx.inc(t.last, s_t, 1)
                tvk = ctx.val(s_t)

                @blk.vector
                def _(v, g=g, k=k, col=col):
                    v.wait_ge(ctx.sem(s_t), tvk)
                    VC(v, v.tensor_copy(SLW[:, col * 8:col * 8 + 8],
                                        P3[:, 128:136]))
                    ctx.inc(VC(v, v.tensor_add(SLW[:, col * 8:col * 8 + 8],
                                               SLW[:, col * 8:col * 8 + 8],
                                               P2[:, 0:8])) if False else
                            v.tensor_add(SLW[:, col * 8:col * 8 + 8],
                                         SLW[:, col * 8:col * 8 + 8],
                                         P2[:, 0:8]), s_v, 1)

        @blk.vector
        def _(v):
            ctx.wait(v, s_v)
            for g in range(32):
                for k in range(2):
                    col = 2 * g + k
                    ohk = OHB[:, k * 256 + g * 8:k * 256 + g * 8 + 8]
                    VC(v, v.tensor_tensor(MISC[:, 0:8],
                                          SLW[:, col * 8:col * 8 + 8],
                                          ohk, ALU.mult))
                    VC(v, v.reduce_sum(SELW[:, col:col + 1], MISC[:, 0:8],
                                       axis=AX.X))
            VC(v, v.tensor_single_scalar(KEEPW[:], SELW[:], float(LCAP),
                                         ALU.is_lt))
            VC(v, v.tensor_scalar(MISC[:, 0:64], EW[:], float(LCAP), None,
                                  ALU.mult))
            VC(v, v.tensor_add(MISC[:, 0:64], MISC[:, 0:64], SELW[:]))
            VC(v, v.tensor_tensor(MISC[:, 0:64], MISC[:, 0:64], KEEPW[:],
                                  ALU.mult))
            VC(v, v.tensor_scalar(MISC2[:, 0:64], KEEPW[:], -float(NE * LCAP),
                                  float(NE * LCAP), ALU.mult, ALU.add))
            VC(v, v.tensor_add(MISC[:, 0:64], MISC[:, 0:64], MISC2[:, 0:64]))
            VC(v, v.tensor_copy(RROWS[:], MISC[:, 0:64]))
            ctx.inc(v.last, s_v, 1)

        @blk.vector
        def _(v):
            KE = KEEPW.rearrange("p (g k) -> p g k", k=2)
            kw1 = MISC[:, 128:160]
            kw2 = MISC[:, 160:192]
            VC(v, v.tensor_tensor(kw1, W12[:, 0:32], KE[:, :, 0], ALU.mult))
            VC(v, v.tensor_tensor(kw2, W12[:, 32:64], KE[:, :, 1], ALU.mult))
            VC(v, v.tensor_add(MISC[:, 64:96], kw1, kw2))
            VC(v, v.tensor_scalar_max(MISC[:, 64:96], MISC[:, 64:96], EPSM))
            VC(v, v.reciprocal(MISC[:, 96:128], MISC[:, 64:96]))
            RF = RFW.rearrange("p (g k) -> p g k", k=2)
            VC(v, v.tensor_tensor(RF[:, :, 0], kw1, MISC[:, 96:128], ALU.mult))
            VC(v, v.tensor_tensor(RF[:, :, 1], kw2, MISC[:, 96:128], ALU.mult))
            ctx.inc(v.last, s_v, 1)
        vfin = ctx.val(s_v)

        if "d_rt" in debug_outputs:
            @blk.sync
            def _(s):
                s.wait_ge(ctx.sem(s_v), vfin)
                ctx.inc(s.dma_start(out=d_rt[:, 0:64], in_=SELW[:]), s_o, 16)
                ctx.inc(s.dma_start(out=d_rt[:, 64:128], in_=EW[:]), s_o, 16)
                ctx.inc(s.dma_start(out=d_rt[:, 128:192], in_=RFW[:]), s_o, 16)
                ctx.inc(s.dma_start(out=d_rt[:, 192:256],
                                    in_=MISC[:, 0:64]), s_o, 16)

        @blk.gpsimd
        def _(g_):
            g_.wait_ge(ctx.sem(s_v), vfin)
            ctx.wait(g_, s_so)
            g0 = ctx.val(s_g)
            for g in range(32):
                sl = 1 + (g % 4)
                if g >= 4:
                    g_.wait_ge(ctx.sem(s_g), g0 + (g - 3) * 32)
                g_.iota(IDXS[:, sl:sl + 1], pattern=[[0, 1]], base=g * 128,
                        channel_multiplier=1)
                for k in range(2):
                    col = 2 * g + k
                    ctx.inc(g_.indirect_dma_start(
                        d_idx[:, :], IndirectOffsetOnAxis(RROWS[:, col:col + 1], 0),
                        IDXS[:, sl:sl + 1], None), s_g, 16)

    def emit_moe():
        CT = 256
        ncts = LCAP // CT
        for e in range(moe_experts):
            @blk.sync
            def _(s, e=e):
                ctx.wait(s, s_g)
                ctx.wait(s, s_t)
                ctx.wait(s, s_v)
                for j in range(12):
                    ctx.inc(s.dma_start(
                        out=IDXS[:, 4 + j:5 + j] if j < 12 else None,
                        in_=d_idx[e * LCAP + j * 128:e * LCAP + j * 128 + 128, :]),
                        s_ld, 16)
            ldv = ctx.val(s_ld)
            tpos = {}
            for j in range(12):
                @blk.gpsimd
                def _(g_, e=e, j=j):
                    if j == 0:
                        g_.wait_ge(ctx.sem(s_ld), ldv)
                    if j >= 4:
                        g_.wait_ge(ctx.sem(s_t), tpos[j - 4])
                    ctx.inc(g_.indirect_dma_start(
                        ARK[:, (j % 4) * 1024:(j % 4) * 1024 + 1024], None,
                        d_xn1tm[:, :],
                        IndirectOffsetOnAxis(IDXS[:, 4 + j:5 + j], 0)),
                        s_g, 16)
                gvj = ctx.val(s_g)

                @blk.tensor
                def _(t, e=e, j=j):
                    t.wait_ge(ctx.sem(s_g), gvj)
                    ctx.wait(t, s_v)
                    for cc in range(8):
                        t.matmul(PT[:, cc * 128:cc * 128 + 128],
                                 ARK[:, (j % 4) * 1024 + cc * 128:(j % 4) * 1024 + cc * 128 + 128],
                                 ID[:], start=True, stop=True,
                                 is_transpose=True, skip_group_check=True)
                    ctx.inc(t.last, s_t, 1)
                tpos[j] = ctx.val(s_t)

                @blk.vector
                def _(v, e=e, j=j):
                    v.wait_ge(ctx.sem(s_t), tpos[j])
                    ctx.inc(v.tensor_copy(
                        ARM[:, 0:12288].rearrange("p (c u) -> p c u", c=8)
                        [:, :, j * 128:j * 128 + 128],
                        PT8[:, :, :]), s_v, 1)
            eiv = ctx.val(s_v)

            WBUFS = (ARW, ARW2, ctx.sb["ARW3"], ctx.sb["ARW4"])
            buf_free = {0: 0, 1: 0, 2: 0, 3: 0}
            NPAIR = 16
            for pj in range(NPAIR):
                bufs = (WBUFS[(pj % 2) * 2], WBUFS[(pj % 2) * 2 + 1])
                gstart = ctx.val(s_g)

                @blk.gpsimd
                def _(g_, e=e, pj=pj, bufs=bufs):
                    for half in range(2):
                        fs = 2 * pj + half
                        wbuf = bufs[half]
                        bi = (pj % 2) * 2 + half
                        if buf_free[bi] > 0:
                            g_.wait_ge(ctx.sem(s_t), buf_free[bi])
                        if fs == 0:
                            g_.wait_ge(ctx.sem(s_v), eiv)
                        wp = w_preT[e].rearrange("(c p) f -> p c f", p=128)
                        wg = w_gateT[e].rearrange("(c p) f -> p c f", p=128)
                        ctx.inc(g_.dma_start(
                            out=wbuf[:, 0:1024].rearrange("p (c f) -> p c f", c=8),
                            in_=wp[:, :, fs * 128:fs * 128 + 128]), s_g, 16)
                        ctx.inc(g_.dma_start(
                            out=wbuf[:, 1024:2048].rearrange("p (c f) -> p c f", c=8),
                            in_=wg[:, :, fs * 128:fs * 128 + 128]), s_g, 16)
                        wpo = w_postT[e].rearrange("(c p) u -> c p u", p=128)
                        ctx.inc(g_.dma_start(
                            out=wbuf[:, 2048:3072],
                            in_=wpo[fs]), s_g, 16)
                gw = [gstart + 48, gstart + 96]

                a0 = ctx.val(s_a)
                v0 = ctx.val(s_v)
                t0 = ctx.val(s_t)

                @blk.tensor
                def _(t, e=e, pj=pj, bufs=bufs):
                    for half in range(2):
                        wbuf = bufs[half]
                        t.wait_ge(ctx.sem(s_g), gw[half])
                        if half > 0 or pj > 0:
                            t.wait_ge(ctx.sem(s_v), v0 + 2 * half)
                        for cc in range(8):
                            for cp in range(3):
                                ps, col = ((P0, 0), (P0, 512), (P3, 0))[cp]
                                t.matmul(ps[:, col:col + 512],
                                         wbuf[:, 1024 + cc * 128:1024 + cc * 128 + 128],
                                         ARM[:, cc * 1536 + cp * 512:cc * 1536 + cp * 512 + 512],
                                         start=(cc == 0), stop=(cc == 7),
                                         skip_group_check=True)
                        ctx.inc(t.last, s_t, 1)
                        t.wait_ge(ctx.sem(s_a), a0 + 2 * (half + 1))
                        for cc in range(8):
                            for cp in range(3):
                                ps, col = ((P0, 0), (P0, 512), (P3, 0))[cp]
                                t.matmul(ps[:, col:col + 512],
                                         wbuf[:, cc * 128:cc * 128 + 128],
                                         ARM[:, cc * 1536 + cp * 512:cc * 1536 + cp * 512 + 512],
                                         start=(cc == 0), stop=(cc == 7),
                                         skip_group_check=True)
                        ctx.inc(t.last, s_t, 1)
                    for ct in range(ncts):
                        t.wait_ge(ctx.sem(s_v), v0 + 4 + ct)
                        for half in range(2):
                            wbuf = bufs[half]
                            act = ARL[:, half * 1536 + ct * 256:half * 1536 + ct * 256 + 256]
                            for cs in range(2):
                                ps = P1 if cs == 0 else P2
                                for dh in range(2):
                                    t.matmul(ps[:, dh * 512:dh * 512 + 512],
                                             act[:, cs * 128:cs * 128 + 128],
                                             wbuf[:, 2048 + dh * 512:2048 + dh * 512 + 512],
                                             start=(half == 0),
                                             stop=(half == 1),
                                             skip_group_check=True)
                        ctx.inc(t.last, s_t, 1)

                @blk.scalar
                def _(sc, pj=pj):
                    for half in range(2):
                        sc.wait_ge(ctx.sem(s_t), t0 + 2 * half + 1)
                        ctx.inc(sc.activation(MISC[:, 0:1024], P0[:], AF.Silu),
                                s_a, 1)
                        ctx.inc(sc.activation(MISC2[:, 0:512], P3[:], AF.Silu),
                                s_a, 1)

                @blk.vector
                def _(v, pj=pj):
                    for half in range(2):
                        v.wait_ge(ctx.sem(s_t), t0 + 2 * half + 2)
                        ctx.inc(v.tensor_tensor(
                            ARL[:, half * 1536:half * 1536 + 1024],
                            P0[:], MISC[:, 0:1024], ALU.mult), s_v, 1)
                        ctx.inc(v.tensor_tensor(
                            ARL[:, half * 1536 + 1024:half * 1536 + 1536],
                            P3[:], MISC2[:, 0:512], ALU.mult), s_v, 1)
                    for ct in range(ncts):
                        v.wait_ge(ctx.sem(s_t), t0 + 4 + ct + 1)
                        v.wait_ge(ctx.sem(s_g), gw[1] + 32 * ct)
                        v.tensor_copy(ctx.sb["MOEO"][:, 0:1024], P1[:])
                        ctx.inc(v.tensor_copy(ctx.sb["MOEO2"][:, 0:1024], P2[:]),
                                s_v, 1)

                @blk.gpsimd
                def _(g_, pj=pj, e=e):
                    for ct in range(ncts):
                        r0 = e * LCAP + ct * CT
                        g_.wait_ge(ctx.sem(s_v), v0 + 4 + ct + 1)
                        op = ALU.bypass if pj == 0 else ALU.add
                        ctx.inc(g_.dma_start(out=d_eo[r0:r0 + 128, :],
                                             in_=ctx.sb["MOEO"][:, 0:1024],
                                             accum_op=op), s_g, 16)
                        ctx.inc(g_.dma_start(out=d_eo[r0 + 128:r0 + 256, :],
                                             in_=ctx.sb["MOEO2"][:, 0:1024],
                                             accum_op=op), s_g, 16)
                tend = ctx.val(s_t)
                buf_free[(pj % 2) * 2] = tend
                buf_free[(pj % 2) * 2 + 1] = tend

    def emit_attn(b):
        s_a_alias = s_v
        fmv = d_xn1fm.rearrange("(bb c p) u -> bb c p u", bb=B_CORE, p=128)

        @blk.sync
        def _(s):
            ctx.wait(s, s_v)
            ctx.wait(s, s_o)
            for cc in range(8):
                ctx.inc(s.dma_start(out=ARM8[:, cc, :], in_=fmv[b, cc]),
                        s_ld, 16)
        ldx = ctx.val(s_ld)

        qkd = d_qk.rearrange("(bb two c p) u -> bb two c p u", bb=B_CORE,
                             two=2, p=128)
        for oc in range(8):
            for part, wcol in ((0, oc * 128), (1, (24 + oc) * 128)):
                wbuf = ARW if part == 0 else ARW2

                @blk.gpsimd
                def _(g_, oc=oc, wcol=wcol, wbuf=wbuf):
                    ctx.wait(g_, s_t)
                    wv = wqkv.rearrange("(c p) u -> c p u", p=128)
                    for cc in range(8):
                        ctx.inc(g_.dma_start(
                            out=wbuf[:, cc * 128:cc * 128 + 128],
                            in_=wv[cc, :, wcol:wcol + 128]), s_g, 16)
            gv = ctx.val(s_g)

            for tt in range(4):
                @blk.tensor
                def _(t, oc=oc, tt=tt):
                    t.wait_ge(ctx.sem(s_g), gv)
                    t.wait_ge(ctx.sem(s_ld), ldx)
                    ctx.wait(t, s_v)
                    for cc in range(8):
                        t.matmul(P0[:, 0:512],
                                 ARW[:, cc * 128:cc * 128 + 128],
                                 ARM8[:, cc, tt * 512:tt * 512 + 512],
                                 start=(cc == 0), stop=(cc == 7),
                                 skip_group_check=True)
                    for cc in range(8):
                        t.matmul(P1[:, 0:512],
                                 ARW2[:, cc * 128:cc * 128 + 128],
                                 ARM8[:, cc, tt * 512:tt * 512 + 512],
                                 start=(cc == 0), stop=(cc == 7),
                                 skip_group_check=True)
                    ctx.inc(t.last, s_t, 1)
                tvq = ctx.val(s_t)

                @blk.vector
                def _(v, oc=oc, tt=tt):
                    v.wait_ge(ctx.sem(s_t), tvq)
                    cs = slice(tt * 512, tt * 512 + 512)
                    VC(v, v.tensor_tensor(MISC[:, 0:512], P0[:, 0:512],
                                          ctx.sb["COSQ"][:, cs], ALU.mult))
                    VC(v, v.tensor_tensor(MISC2[:, 0:512], P1[:, 0:512],
                                          ctx.sb["SINQ"][:, cs], ALU.mult))
                    ctx.inc(v.tensor_add(ARK[:, tt * 512:tt * 512 + 512],
                                         MISC[:, 0:512], MISC2[:, 0:512]),
                            s_v, 1)
            vvq = ctx.val(s_v)

            @blk.sync
            def _(s, oc=oc):
                s.wait_ge(ctx.sem(s_v), vvq)
                ctx.inc(s.dma_start(out=qkd[b, 0, oc], in_=ARK[:, 0:2048]),
                        s_o, 16)

        for oc in range(8):
            for part, wcol in ((0, (8 + oc) * 128), (1, (32 + oc) * 128)):
                wbuf = ARW if part == 0 else ARW2

                @blk.gpsimd
                def _(g_, oc=oc, wcol=wcol, wbuf=wbuf):
                    ctx.wait(g_, s_t)
                    wv = wqkv.rearrange("(c p) u -> c p u", p=128)
                    for cc in range(8):
                        ctx.inc(g_.dma_start(
                            out=wbuf[:, cc * 128:cc * 128 + 128],
                            in_=wv[cc, :, wcol:wcol + 128]), s_g, 16)
            gv = ctx.val(s_g)

            for tt in range(4):
                @blk.tensor
                def _(t, oc=oc, tt=tt):
                    t.wait_ge(ctx.sem(s_g), gv)
                    ctx.wait(t, s_v)
                    for cc in range(8):
                        t.matmul(P0[:, 0:512],
                                 ARW[:, cc * 128:cc * 128 + 128],
                                 ARM8[:, cc, tt * 512:tt * 512 + 512],
                                 start=(cc == 0), stop=(cc == 7),
                                 skip_group_check=True)
                    for cc in range(8):
                        t.matmul(P1[:, 0:512],
                                 ARW2[:, cc * 128:cc * 128 + 128],
                                 ARM8[:, cc, tt * 512:tt * 512 + 512],
                                 start=(cc == 0), stop=(cc == 7),
                                 skip_group_check=True)
                    ctx.inc(t.last, s_t, 1)
                tvq = ctx.val(s_t)

                @blk.vector
                def _(v, oc=oc, tt=tt):
                    v.wait_ge(ctx.sem(s_t), tvq)
                    cs = slice(tt * 512, tt * 512 + 512)
                    VC(v, v.tensor_tensor(MISC[:, 0:512], P0[:, 0:512],
                                          ctx.sb["COSK"][:, cs], ALU.mult))
                    VC(v, v.tensor_tensor(MISC2[:, 0:512], P1[:, 0:512],
                                          ctx.sb["SINK"][:, cs], ALU.mult))
                    ctx.inc(v.tensor_add(ARK[:, tt * 512:tt * 512 + 512],
                                         MISC[:, 0:512], MISC2[:, 0:512]),
                            s_v, 1)
            vvq = ctx.val(s_v)

            @blk.sync
            def _(s, oc=oc):
                s.wait_ge(ctx.sem(s_v), vvq)
                ctx.inc(s.dma_start(out=qkd[b, 1, oc], in_=ARK[:, 0:2048]),
                        s_o, 16)

        def wvap(cc):
            if cc < 6:
                return ARW[:, cc * 1024:cc * 1024 + 1024]
            return ARW2[:, (cc - 6) * 1024:(cc - 6) * 1024 + 1024]

        @blk.gpsimd
        def _(g_):
            ctx.wait(g_, s_t)
            wv = wqkv.rearrange("(c p) u -> c p u", p=128)
            for cc in range(8):
                ctx.inc(g_.dma_start(out=wvap(cc),
                                     in_=wv[cc, :, 2048:3072]), s_g, 16)
        gvv = ctx.val(s_g)

        for tt in range(16):
            @blk.tensor
            def _(t, tt=tt):
                t.wait_ge(ctx.sem(s_g), gvv)
                ctx.wait(t, s_v)
                for half in range(2):
                    for cc in range(8):
                        t.matmul(P2[:, half * 512:half * 512 + 512],
                                 ARM8[:, cc, tt * 128:tt * 128 + 128],
                                 wvap(cc)[:, half * 512:half * 512 + 512],
                                 start=(cc == 0), stop=(cc == 7),
                                 skip_group_check=True)
                ctx.inc(t.last, s_t, 1)
            tvv = ctx.val(s_t)

            @blk.vector
            def _(v, tt=tt):
                v.wait_ge(ctx.sem(s_t), tvv)
                ctx.inc(v.tensor_copy(XNT[:], P2[:]), s_v, 1)
            vvv = ctx.val(s_v)

            @blk.sync
            def _(s, tt=tt):
                s.wait_ge(ctx.sem(s_v), vvv)
                ctx.inc(s.dma_start(
                    out=d_v[b * T + tt * 128:b * T + tt * 128 + 128, :],
                    in_=XNT[:]), s_o, 16)

        for h in range(attn_heads):
            cc = h // 2
            ro = (h % 2) * 64
            @blk.sync
            def _(s, h=h, cc=cc, ro=ro):
                ctx.wait(s, s_a)
                ctx.wait(s, s_t)
                ctx.inc(s.dma_start(out=ARK[0:64, 0:2048],
                                    in_=qkd[b, 0, cc][ro:ro + 64, :]), s_ld, 16)
                ctx.inc(s.dma_start(out=ARK[0:64, 2048:4096],
                                    in_=qkd[b, 1, cc][ro:ro + 64, :]), s_ld, 16)
                vv_ = d_v.rearrange("(g p) u -> g p u", p=128)
                for g in range(16):
                    ctx.inc(s.dma_start(
                        out=ARK[:, 4096 + g * 64:4096 + g * 64 + 64],
                        in_=vv_[b * 16 + g][:, h * 64:h * 64 + 64]), s_ld, 16)
            ldh = ctx.val(s_ld)

            for which in range(2):
                @blk.vector
                def _(v, h=h, which=which):
                    if which == 0:
                        v.wait_ge(ctx.sem(s_ld), ldh)
                    else:
                        ctx.wait(v, s_t)
                    qs = ARK[0:64, which * 2048:which * 2048 + 2048]
                    ctx.inc(v.tensor_tensor(ARL[0:64, 0:2048], qs, qs,
                                            ALU.mult), s_a_alias, 1)
                avh = ctx.val(s_a_alias)

                @blk.tensor
                def _(t, h=h, which=which):
                    t.wait_ge(ctx.sem(s_a_alias), avh)
                    for i4 in range(4):
                        ps = (P0, P0, P1, P1)[i4]
                        col = (i4 % 2) * 512
                        t.matmul(ps[0:1, col:col + 512], ONESB[0:64, :],
                                 ARL[0:64, i4 * 512:i4 * 512 + 512],
                                 start=True, stop=True, skip_group_check=True)
                    ctx.inc(t.last, s_t, 1)
                tvn = ctx.val(s_t)

                @blk.vector
                def _(v, h=h, which=which):
                    v.wait_ge(ctx.sem(s_t), tvn)
                    VC(v, v.reduce_max(SCL[0:1, 20:21], P0[0:1, :], axis=AX.X))
                    VC(v, v.reduce_max(SCL[0:1, 21:22], P1[0:1, :], axis=AX.X))
                    v.tensor_max(SCL[0:1, 12 + which:13 + which],
                                 SCL[0:1, 20:21], SCL[0:1, 21:22])
                    ctx.inc(v.last, s_v, 1)

            @blk.vector
            def _(v, h=h):
                ctx.wait(v, s_v)
                ctx.inc(v.tensor_tensor(SCL[0:1, 14:15], SCL[0:1, 12:13],
                                        SCL[0:1, 13:14], ALU.mult), s_v, 1)
            vvc = ctx.val(s_v)

            @blk.scalar
            def _(sc, h=h):
                sc.wait_ge(ctx.sem(s_v), vvc)
                ctx.inc(sc.activation(SCL[0:1, 15:16], SCL[0:1, 14:15],
                                      AF.Sqrt), s_a, 1)
            avc = ctx.val(s_a)

            @blk.vector
            def _(v, h=h):
                v.wait_ge(ctx.sem(s_a), avc)
                ctx.inc(v.tensor_scalar_mul(SCL[0:1, 16:17], SCL[0:1, 15:16],
                                            -1.0), s_v, 1)
            vvc2 = ctx.val(s_v)

            @blk.tensor
            def _(t, h=h):
                t.wait_ge(ctx.sem(s_v), vvc2)
                t.matmul(P3[:, 508:509], ONESRF[0:1, :], SCL[0:1, 16:17],
                         start=True, stop=True, skip_group_check=True)
                ctx.inc(t.last, s_t, 1)
            tvc = ctx.val(s_t)

            @blk.vector
            def _(v, h=h):
                v.wait_ge(ctx.sem(s_t), tvc)
                ctx.inc(v.tensor_copy(SCL[:, 17:18], P3[:, 508:509]), s_v, 1)
            vnegc = ctx.val(s_v)

            T0 = ctx.val(s_t)
            A0 = ctx.val(s_a)
            V0 = ctx.val(s_v)

            @blk.tensor
            def _(t, h=h):
                for qt in range(4):
                    Tq = T0 + 10 * qt
                    Aq = A0 + 16 * qt
                    for rd in range(8):
                        ps = P0 if rd % 2 == 0 else P1
                        if qt == 0 and rd == 0:
                            t.wait_ge(ctx.sem(s_v), vnegc)
                        if rd >= 2:
                            t.wait_ge(ctx.sem(s_a), Aq + 2 * (rd - 1))
                        if qt > 0 and rd < 2:
                            t.wait_ge(ctx.sem(s_a), Aq)
                            t.wait_ge(ctx.sem(s_v), V0 + 2 * qt)
                        for j in range(2):
                            kc = rd * 2 + j
                            t.matmul(ps[:, j * 512:j * 512 + 512],
                                     ARK[0:64, 2048 + kc * 128:2048 + kc * 128 + 128],
                                     ARK[0:64, qt * 512:qt * 512 + 512],
                                     start=True, stop=True,
                                     skip_group_check=True)
                        ctx.inc(t.last, s_t, 1)
                    t.wait_ge(ctx.sem(s_a), Aq + 16)
                    if qt > 0:
                        t.wait_ge(ctx.sem(s_v), V0 + 2 * qt)
                    for kc in range(16):
                        t.matmul(P3[0:1, 0:512], ONESB[:],
                                 ARM[:, kc * 512:kc * 512 + 512],
                                 start=(kc == 0), stop=(kc == 15),
                                 skip_group_check=True)
                    for kc in range(16):
                        t.matmul(P2[0:64, 0:512],
                                 ARK[:, 4096 + kc * 64:4096 + kc * 64 + 64],
                                 ARM[:, kc * 512:kc * 512 + 512],
                                 start=(kc == 0), stop=(kc == 15),
                                 skip_group_check=True)
                    ctx.inc(t.last, s_t, 1)
                    t.wait_ge(ctx.sem(s_v), V0 + 2 * qt + 1)
                    t.matmul(P0[0:64, 0:512], ONESRF[0:1, 0:64],
                             ARS[0:1, 0:512], start=True, stop=True,
                             skip_group_check=True)
                    ctx.inc(t.last, s_t, 1)
            assert ctx.val(s_t) == T0 + 40

            @blk.scalar
            def _(sc, h=h):
                for qt in range(4):
                    Tq = T0 + 10 * qt
                    for rd in range(8):
                        sc.wait_ge(ctx.sem(s_t), Tq + rd + 1)
                        ps = P0 if rd % 2 == 0 else P1
                        for j in range(2):
                            kc = rd * 2 + j
                            ctx.inc(sc.activation(
                                ARM[:, kc * 512:kc * 512 + 512],
                                ps[:, j * 512:j * 512 + 512], AF.Exp,
                                bias=SCL[:, 17:18]), s_a, 1)
            assert ctx.val(s_a) == A0 + 64

            @blk.vector
            def _(v, h=h):
                for qt in range(4):
                    Tq = T0 + 10 * qt
                    v.wait_ge(ctx.sem(s_t), Tq + 9)
                    ctx.inc(v.reciprocal(ARS[0:1, 0:512], P3[0:1, 0:512]),
                            s_v, 1)
                    v.wait_ge(ctx.sem(s_t), Tq + 10)
                    VC(v, v.tensor_copy(MISC[0:64, 0:512], P0[0:64, 0:512]))
                    ctx.inc(v.tensor_tensor(
                        ARL[0:64, 2048 + qt * 512:2048 + qt * 512 + 512],
                        P2[0:64, 0:512], MISC[0:64, 0:512], ALU.mult),
                        s_v, 1)
            assert ctx.val(s_v) == V0 + 8
            vnat = ctx.val(s_v)

            @blk.sync
            def _(s, h=h):
                s.wait_ge(ctx.sem(s_v), vnat)
                ctx.inc(s.dma_start(
                    out=d_natt[(b * NH + h) * 64:(b * NH + h) * 64 + 64, :],
                    in_=ARL[0:64, 2048:4096]), s_o, 16)

    def emit_fin(b):
        natv = d_natt.rearrange("(bb c p) u -> bb c p u", bb=B_CORE, p=128)

        @blk.sync
        def _(s):
            ctx.wait(s, s_o)
            ctx.wait(s, s_v)
            for cc in range(8):
                ctx.inc(s.dma_start(out=ARM8[:, cc, :], in_=natv[b, cc]),
                        s_ld, 16)
        ldn = ctx.val(s_ld)

        @blk.gpsimd
        def _(g_):
            ctx.wait(g_, s_t)
            wv = w_od.rearrange("(c p) u -> c p u", p=128)
            for cc in range(8):
                dst = (ARW[:, cc * 1024:cc * 1024 + 1024] if cc < 6 else
                       ARW2[:, (cc - 6) * 1024:(cc - 6) * 1024 + 1024])
                ctx.inc(g_.dma_start(out=dst, in_=wv[cc]), s_g, 16)
        gvo = ctx.val(s_g)

        def woap(cc):
            if cc < 6:
                return ARW[:, cc * 1024:cc * 1024 + 1024]
            return ARW2[:, (cc - 6) * 1024:(cc - 6) * 1024 + 1024]

        for tt in range(16):
            gtile = b * 16 + tt

            @blk.sync
            def _(s, tt=tt):
                ctx.wait(s, s_v)
                ctx.inc(s.dma_start(
                    out=MISC[:],
                    in_=d_x1[b * T + tt * 128:b * T + tt * 128 + 128, :]),
                    s_ld, 16)
            ldt = ctx.val(s_ld)

            @blk.gpsimd
            def _(g_, tt=tt, gtile=gtile):
                ctx.wait(g_, s_v)
                for k in range(2):
                    col = 2 * gtile + k
                    ctx.inc(g_.indirect_dma_start(
                        (MISC2 if k == 0 else ARS)[:, 0:1024], None,
                        d_eo[:, :],
                        IndirectOffsetOnAxis(RROWS[:, col:col + 1], 0)),
                        s_g, 16)
            gvt = ctx.val(s_g)

            @blk.tensor
            def _(t, tt=tt):
                t.wait_ge(ctx.sem(s_ld), ldn)
                t.wait_ge(ctx.sem(s_g), gvo)
                ctx.wait(t, s_v)
                for half in range(2):
                    for cc in range(8):
                        t.matmul(P2[:, half * 512:half * 512 + 512],
                                 ARM8[:, cc, tt * 128:tt * 128 + 128],
                                 woap(cc)[:, half * 512:half * 512 + 512],
                                 start=(cc == 0), stop=(cc == 7),
                                 skip_group_check=True)
                ctx.inc(t.last, s_t, 1)
            tvo = ctx.val(s_t)

            @blk.vector
            def _(v, tt=tt, gtile=gtile):
                v.wait_ge(ctx.sem(s_t), tvo)
                v.wait_ge(ctx.sem(s_ld), ldt)
                v.wait_ge(ctx.sem(s_g), gvt)
                VC(v, v.tensor_add(MISC[:], MISC[:], P2[:]))
                VC(v, v.scalar_tensor_tensor(MISC[:], MISC2[:],
                                             RFW[:, 2 * gtile:2 * gtile + 1],
                                             MISC[:], ALU.mult, ALU.add))
                ctx.inc(v.scalar_tensor_tensor(
                    MISC[:], ARS[:, 0:1024],
                    RFW[:, 2 * gtile + 1:2 * gtile + 2], MISC[:],
                    ALU.mult, ALU.add), s_v, 1)
            vvo = ctx.val(s_v)

            @blk.sync
            def _(s, tt=tt):
                s.wait_ge(ctx.sem(s_v), vvo)
                ctx.inc(s.dma_start(
                    out=out_t[b * T + tt * 128:b * T + tt * 128 + 128, :],
                    in_=MISC[:]), s_o, 16)

    if "l0" in phases:
        for b in range(B_CORE):
            with nc.named_scope(f"l0_{b}"):
                emit_l0(b)
            barrier()
    if "ln1" in phases:
        for b in range(B_CORE):
            with nc.named_scope(f"ln1_{b}"):
                emit_ln1(b)
            barrier()
    if "route" in phases:
        with nc.named_scope("route"):
            emit_route()
        barrier()
    if "moe" in phases:
        with nc.named_scope("moe"):
            emit_moe()
        barrier()
    if "attn" in phases:
        for b in range(B_CORE):
            with nc.named_scope(f"attn_{b}"):
                emit_attn(b)
            barrier()
    if "fin" in phases:
        for b in range(B_CORE):
            with nc.named_scope(f"fin_{b}"):
                emit_fin(b)
            barrier()

    @blk.sync
    def _(s):
        ctx.wait(s, s_o)
        ctx.wait(s, s_so)
        ctx.wait(s, s_g)
        ctx.wait(s, s_ld)

    st.close()
    return nc



_CACHE = {}


def kernel(**inp):
    x = f32(inp["L0_attn_x"])
    cns = host_constants(inp)
    if "nc" not in _CACHE:
        _CACHE["nc"] = build()
    nc = _CACHE["nc"]
    in_maps = []
    base = {
        "w_qkv_aug": np.concatenate(
            [f32(inp['L1_attn_w_qkv']),
             rope_perm_cols(f32(inp['L1_attn_w_qkv'])[:, :1024]),
             rope_perm_cols(f32(inp['L1_attn_w_qkv'])[:, 1024:2048])], axis=1),
        "w_o": f32(inp['L1_attn_w_o']),
        "w_preT": np.ascontiguousarray(
            f32(inp['L1_ffn_ff_pre_act_weight']).transpose(0, 2, 1)),
        "w_gateT": np.ascontiguousarray(
            f32(inp['L1_ffn_gate_weight']).transpose(0, 2, 1)),
        "w_postT": np.ascontiguousarray(
            f32(inp['L1_ffn_ff_post_act_weight']).transpose(0, 2, 1)),
    }
    for nm in CONST_NAMES:
        shp, dt = CONST_SHAPES[nm]
        arr = cns[nm]
        if dt == BF16:
            arr = arr.astype(ml_dtypes.bfloat16)
        elif dt == U32:
            arr = arr.astype(np.uint32)
        else:
            arr = arr.astype(np.float32)
        assert list(arr.shape) == shp, (nm, arr.shape, shp)
        base["c_" + nm] = arr
    for core in range(8):
        m = dict(base)
        m["x_tm"] = np.ascontiguousarray(
            x[2 * core:2 * core + 2].reshape(NTOK, D))
        in_maps.append(m)
    res = run_bass_kernel_spmd(nc, in_maps, core_ids=list(range(8)))
    out = np.concatenate(
        [res.results[c]["out"].reshape(2, T, D) for c in range(8)], axis=0)
    return out


# version 05
# speedup vs baseline: 1.2713x; 1.2713x over previous
import numpy as np
import ml_dtypes
import concourse.bass as bass
import concourse.mybir as mybir
from concourse.bass import IndirectOffsetOnAxis
from concourse.bass_utils import run_bass_kernel_spmd
from contextlib import ExitStack

F32 = mybir.dt.float32
BF16 = mybir.dt.bfloat16
U32 = mybir.dt.uint32
AF = mybir.ActivationFunctionType
ALU = mybir.AluOpType
AX = mybir.AxisListType

B_CORE = 2
T = 2048
D = 1024
NTOK = B_CORE * T
NH, DH = 16, 64
C = 128
NCH = T // C
LCAP = 1536
NE = 8
EOROWS = NE * LCAP + 1
NIDX = NE * LCAP + 256
BLOCK_SIZES = [4, 8, 16, 32]
RANKS = [4, 8, 8, 8]
NUM_NODES = [8, 4, 2, 1]
EPSM = 1.1920929e-07
SPT = 3


def f32(x):
    return np.ascontiguousarray(np.asarray(x), dtype=np.float32)



def _comp_list():
    comps = []
    for n in range(16):
        sl = (4 * n, 4 * n + 4)
        comps.append(dict(wsrc='v', lvl=-1, wrows=sl, qrows=sl,
                          ustream='ktm', ucols=sl, rank=4, kind='leaf'))
    for lvl in range(4):
        bs = BLOCK_SIZES[lvl]
        for n in range(NUM_NODES[lvl]):
            top = (2 * bs * n, 2 * bs * n + bs)
            bot = (2 * bs * n + bs, 2 * bs * n + 2 * bs)
            comps.append(dict(wsrc='v', lvl=lvl, wrows=bot, qrows=bot,
                              ustream=('ktm' if lvl < 2 else f"G{lvl}k"),
                              ucols=top, rank=RANKS[lvl], kind='top'))
    for lvl in range(4):
        bs = BLOCK_SIZES[lvl]
        for n in range(NUM_NODES[lvl]):
            top = (2 * bs * n, 2 * bs * n + bs)
            bot = (2 * bs * n + bs, 2 * bs * n + 2 * bs)
            comps.append(dict(wsrc='k', lvl=lvl, wrows=top, qrows=top,
                              ustream=f"R{lvl}v", ucols=bot,
                              rank=RANKS[lvl], kind='bot'))
    assert len(comps) == 46
    return comps


COMPS = _comp_list()
NVC = sum(1 for c in COMPS if c['wsrc'] == 'v')
NKC = 46 - NVC
NWT_V = (NVC + SPT - 1) // SPT
NWT_K = (NKC + SPT - 1) // SPT
NWT = NWT_V + NWT_K
NQT = (46 + SPT - 1) // SPT


def comp_slot(i):
    if COMPS[i]['wsrc'] == 'v':
        j = i
        return j // SPT, j % SPT
    j = i - NVC
    return NWT_V + j // SPT, j % SPT


def q_slot(i):
    return comp_slot(i)



_CLS = {0: [], 1: [], 2: []}
for _i in range(46):
    _CLS[comp_slot(_i)[1]].append(_i)
SIG_ORDER = _CLS[0] + ["G"] + _CLS[1] + _CLS[2]
SIGCOL = {c: j * 128 for j, c in enumerate(SIG_ORDER)}


def build_packers(UL, VR, VL):
    PW = np.zeros((128, NWT * 128), np.float32)
    TQ = np.zeros((64, NQT * 128), np.float32)
    for i, c in enumerate(COMPS):
        lvl = c['lvl']
        if c['kind'] == 'leaf' or (c['kind'] == 'top' and lvl < 2):
            wmat = np.eye(c['wrows'][1] - c['wrows'][0], dtype=np.float32)
            qmat = wmat
        elif c['kind'] == 'top':
            wmat = VR[lvl]
            qmat = VR[lvl]
        else:
            wmat = UL[lvl]
            qmat = VL[lvl]
        r = wmat.shape[1]
        wt, ws = comp_slot(i)
        rbase = (64 if c['wsrc'] == 'k' else 0)
        rows = np.arange(c['wrows'][0], c['wrows'][1])
        PW[rbase + rows, wt * 128 + 32 * ws: wt * 128 + 32 * ws + r] = wmat
        qt_, qs = q_slot(i)
        qrows = np.arange(c['qrows'][0], c['qrows'][1])
        TQ[qrows, qt_ * 128 + 32 * qs: qt_ * 128 + 32 * qs + r] = qmat
    return PW, TQ


def blockdiag(mat, nn, bs, side):
    r = mat.shape[1]
    out = np.zeros((64, nn * r), np.float32)
    for n in range(nn):
        off = 2 * bs * n + (bs if side == 'bot' else 0)
        out[off:off + bs, n * r:(n + 1) * r] = mat
    return out


def build_mask7():
    ml = np.zeros((64, 64), np.float32)
    for n in range(16):
        ml[4 * n:4 * n + 4, 4 * n:4 * n + 4] = 1
    trT, blT = [], []
    for lvl in range(4):
        bs = BLOCK_SIZES[lvl]
        mt = np.zeros((64, 64), np.float32)
        mb = np.zeros((64, 64), np.float32)
        for n in range(NUM_NODES[lvl]):
            top = slice(2 * bs * n, 2 * bs * n + bs)
            bot = slice(2 * bs * n + bs, 2 * bs * n + 2 * bs)
            mt[top, bot] = 1
            mb[bot, top] = 1
        trT.append(mt.T.copy())
        blT.append(mb.T.copy())
    p0 = ml + trT[0] + trT[1]
    return np.concatenate([p0, trT[2], trT[3]] + blT, axis=1)


def build_rope_tables():
    inv_freq = 1.0 / (10000.0 ** (np.arange(0, DH, 2, dtype=np.float32) / DH))
    tt = np.arange(T, dtype=np.float32)
    freqs = np.outer(tt, inv_freq)
    emb = np.concatenate([freqs, freqs], -1)
    cos = np.cos(emb).T
    sin = np.sin(emb).T
    sgn = np.concatenate([-np.ones(32), np.ones(32)]).astype(np.float32)
    sinw = sin * sgn[:, None]
    return (np.concatenate([cos, cos], 0).astype(np.float32),
            np.concatenate([sinw, sinw], 0).astype(np.float32))


def rope_perm_cols(w):
    perm = np.concatenate([np.arange(32, 64), np.arange(0, 32)])
    out = np.empty_like(w)
    for h in range(w.shape[1] // 64):
        out[:, h * 64:(h + 1) * 64] = w[:, h * 64:(h + 1) * 64][:, perm]
    return out


CONST_SHAPES = dict(
    ID128=([128, 128], BF16), IDF32=([128, 128], F32),
    CAUSREP=([128, 1024], BF16),
    TRIU=([128, 128], BF16), UTRI64=([64, 64], BF16),
    IOTAE=([128, 8], F32), ONESB=([128, 1], BF16), ONESM=([128, 128], BF16),
    ONESRF=([1, 128], F32),
    PW=([128, NWT * 128], BF16), TQ=([64, NQT * 128], BF16),
    BD=([32, 1024], BF16), MASK7=([64, 448], F32),
    COSQ=([128, 2048], F32), SINQ=([128, 2048], F32),
    COSK=([128, 2048], F32), SINK=([128, 2048], F32),
    L0W=([128, 1536], BF16), L0WO=([64, 1024], BF16), WIN=([128, 512], BF16),
    FFNP=([128, 50], F32), WOUT=([128, 1024], BF16), BROW=([128, 1024], F32),
    RWT=([128, 64], BF16),
)
CONST_NAMES = list(CONST_SHAPES)


def host_constants(inp):
    UL = [f32(inp[f'L0_attn_UL{l}']) for l in range(4)]
    VR = [f32(inp[f'L0_attn_VR{l}']) for l in range(4)]
    UR = [f32(inp[f'L0_attn_UR{l}']) for l in range(4)]
    VL = [f32(inp[f'L0_attn_VL{l}']) for l in range(4)]
    PW, TQ = build_packers(UL, VR, VL)
    cns = {}
    cns['ID128'] = np.eye(128, dtype=np.float32)
    cns['IDF32'] = np.eye(128, dtype=np.float32)
    caus = np.triu(np.ones((C, C), np.float32))
    cns['CAUSREP'] = np.tile(caus, (1, 8))
    tri = np.zeros((128, 128), np.float32)
    tri[np.triu_indices(128, 1)] = 1
    cns['TRIU'] = tri
    u64 = np.zeros((64, 64), np.float32)
    u64[np.triu_indices(64, 1)] = 1
    cns['UTRI64'] = u64
    cns['IOTAE'] = np.tile(np.arange(8, dtype=np.float32), (128, 1))
    cns['ONESB'] = np.ones((128, 1), np.float32)
    cns['ONESM'] = np.ones((128, 128), np.float32)
    cns['ONESRF'] = np.ones((1, 128), np.float32)
    cns['PW'] = PW
    cns['TQ'] = TQ
    bd = np.zeros((32, 16 * 64), np.float32)
    for lvl in range(4):
        for wi, W in enumerate([VR[lvl], UR[lvl], UL[lvl], VL[lvl]]):
            side = 'bot' if wi < 2 else 'top'
            m = blockdiag(W, NUM_NODES[lvl], BLOCK_SIZES[lvl], side)
            kr = NUM_NODES[lvl] * RANKS[lvl]
            bd[0:kr, (lvl * 4 + wi) * 64:(lvl * 4 + wi) * 64 + 64] = m.T
    cns['BD'] = bd
    cns['MASK7'] = build_mask7()
    cq, sq = build_rope_tables()
    cns['COSQ'] = cq / 8.0
    cns['SINQ'] = sq / 8.0
    cns['COSK'] = cq
    cns['SINK'] = sq
    wqkv0 = np.concatenate([f32(inp['L0_attn_Wq']), f32(inp['L0_attn_Wk']),
                            f32(inp['L0_attn_Wv'])], axis=1)
    cns['L0W'] = np.ascontiguousarray(
        wqkv0.reshape(8, 128, 192).transpose(1, 0, 2).reshape(128, 8 * 192))
    cns['L0WO'] = f32(inp['L0_attn_Wo'])
    cns['WIN'] = np.ascontiguousarray(
        f32(inp['L0_ffn_Win']).reshape(8, 128, 64).transpose(1, 0, 2).reshape(128, 512))
    ffnp = np.concatenate([f32(inp['L0_ffn_W1']), f32(inp['L0_ffn_b1']),
                           f32(inp['L0_ffn_W2']),
                           f32(inp['L0_ffn_bin_'])[:, None],
                           f32(inp['L0_ffn_b2'])[:, None]], axis=1)
    cns['FFNP'] = np.concatenate([ffnp, ffnp], axis=0)
    wout = f32(inp['L0_ffn_Wout'])
    cns['WOUT'] = np.concatenate([wout, wout], axis=0)
    brow = np.zeros((128, 1024), np.float32)
    brow[0] = f32(inp['L0_ffn_bout'])
    cns['BROW'] = brow
    cns['RWT'] = np.ascontiguousarray(
        f32(inp['L1_ffn_router_weight']).T.reshape(8, 128, 8)
        .transpose(1, 0, 2).reshape(128, 64))
    return cns



class Ctx:
    def __init__(self, nc, st):
        self.nc = nc
        self.st = st
        self._sems = {}
        self.sb = {}

    def sem(self, name):
        if name not in self._sems:
            h = self.st.enter_context(self.nc.semaphore(name))
            self._sems[name] = [h, 0]
        return self._sems[name][0]

    def inc(self, inst, name, k=1):
        self.sem(name)
        s = self._sems[name]
        s[1] += k
        inst.then_inc(s[0], k)
        return s[1]

    def val(self, name):
        self.sem(name)
        return self._sems[name][1]

    def wait(self, eng, name, v=None):
        self.sem(name)
        s = self._sems[name]
        eng.wait_ge(s[0], s[1] if v is None else v)

    def alloc(self, name, shape, dtype):
        h = self.st.enter_context(self.nc.sbuf_tensor(name, shape, dtype))
        self.sb[name] = h
        return h


class _EngWrap:

    def __init__(self, eng):
        object.__setattr__(self, "_e", eng)
        object.__setattr__(self, "last", None)

    def __getattr__(self, k):
        f = getattr(self._e, k)
        if callable(k) or callable(f):
            def g(*a, **kw):
                r = f(*a, **kw)
                try:
                    if hasattr(r, "then_inc"):
                        object.__setattr__(self, "last", r)
                except Exception:
                    pass
                return r
            return g
        return f


class _BlkWrap:
    def __init__(self, blk):
        self._b = blk

    def _mk(self, name):
        def deco(f):
            return getattr(self._b, name)(lambda e: f(_EngWrap(e)))
        return deco

    @property
    def tensor(self):
        return self._mk("tensor")

    @property
    def vector(self):
        return self._mk("vector")

    @property
    def scalar(self):
        return self._mk("scalar")

    @property
    def gpsimd(self):
        return self._mk("gpsimd")

    @property
    def sync(self):
        return self._mk("sync")


TRF_IDX = dict(G2=0, G3=1, H2=2, H3=3, RT0=4, RT1=5, RT2=6, RT3=7,
               WT0=8, WT1=9, WT2=10, WT3=11)


def build(debug_outputs=(), phases=("l0", "ln1", "route", "moe", "attn", "fin"),
          l0_parts=("ln", "proj", "kt", "ffn", "scan", "ytr", "asm"), scan_chunks=NCH,
          scan_steps=(1, 2, 3, 4, 5, 6, 7), moe_experts=NE, attn_heads=NH):
    nc = bass.Bass("TRN2", target_bir_lowering=False)
    st = ExitStack()
    ctx = Ctx(nc, st)

    x_tm = nc.dram_tensor("x_tm", [NTOK, D], F32, kind="ExternalInput").ap()
    wqkv = nc.dram_tensor("w_qkv_aug", [D, 5120], F32, kind="ExternalInput").ap()
    w_od = nc.dram_tensor("w_o", [D, D], F32, kind="ExternalInput").ap()
    w_preT = nc.dram_tensor("w_preT", [NE, D, 4096], F32, kind="ExternalInput").ap()
    w_gateT = nc.dram_tensor("w_gateT", [NE, D, 4096], F32, kind="ExternalInput").ap()
    w_postT = nc.dram_tensor("w_postT", [NE, 4096, D], F32, kind="ExternalInput").ap()
    cdram = {}
    for nm in CONST_NAMES:
        shp, dt = CONST_SHAPES[nm]
        cdram[nm] = nc.dram_tensor("c_" + nm, shp, dt, kind="ExternalInput").ap()
    out_t = nc.dram_tensor("out", [NTOK, D], F32, kind="ExternalOutput").ap()

    dbg = {}

    def idram(name, shape, dt):
        kind = "ExternalOutput" if name in debug_outputs else "Internal"
        h = nc.dram_tensor(name, shape, dt, kind=kind).ap()
        dbg[name] = h
        return h

    d_x1 = idram("d_x1", [NTOK, D], F32)
    d_xn1tm = idram("d_xn1tm", [NTOK, D], BF16)
    d_xn1fm = idram("d_xn1fm", [B_CORE * D, T], BF16)
    d_eo = idram("d_eo", [EOROWS, D], F32)
    d_idx = idram("d_idx", [NIDX, 1], U32)
    d_natt = idram("d_natt", [B_CORE * NH * DH, T], BF16)
    d_qk = idram("d_qk", [B_CORE * 2 * D, T], BF16)
    d_v = idram("d_v", [NTOK, D], BF16)
    d_rt = idram("d_rt", [128, 512], F32)
    d_y0 = idram("d_y0", [NTOK, 64], F32)

    for nm in CONST_NAMES:
        shp, dt = CONST_SHAPES[nm]
        ctx.alloc(nm, shp, dt)
    TRFSB = ctx.alloc("TRFSB", [64, 768], BF16)
    BOUTBC = ctx.alloc("BOUTBC", [128, 1024], F32)
    ARM = ctx.alloc("ARM", [128, 16384], BF16)
    ARK = ctx.alloc("ARK", [128, 8192], BF16)
    ARL = ctx.alloc("ARL", [128, 4096], BF16)
    ARS = ctx.alloc("ARS", [128, 2048], F32)
    ARW = ctx.alloc("ARW", [128, 6144], BF16)
    ARW2 = ctx.alloc("ARW2", [128, 6144], BF16)
    ARW3 = ctx.alloc("ARW3", [128, 6144], BF16)
    ARW4 = ctx.alloc("ARW4", [128, 6144], BF16)
    MOEO = ctx.alloc("MOEO", [128, 1024], F32)
    MOEO2 = ctx.alloc("MOEO2", [128, 1024], F32)
    RWIDE = ctx.alloc("RWIDE", [128, 256], F32)
    RTOP = ctx.alloc("RTOP", [128, 64], F32)
    RANKW = ctx.alloc("RANKW", [128, 256], F32)
    OH1W = ctx.alloc("OH1W", [128, 256], F32)
    OH2W = ctx.alloc("OH2W", [128, 256], F32)
    OHB = ctx.alloc("OHB", [128, 512], BF16)
    SLW = ctx.alloc("SLW", [128, 512], F32)
    EW = ctx.alloc("EW", [128, 64], F32)
    SELW = ctx.alloc("SELW", [128, 64], F32)
    KEEPW = ctx.alloc("KEEPW", [128, 64], F32)
    W12 = ctx.alloc("W12", [128, 64], F32)
    RROWS = ctx.alloc("RROWS", [128, 64], U32)
    RFW = ctx.alloc("RFW", [128, 64], F32)
    BASEST = ctx.alloc("BASEST", [8, 64], BF16)
    BASESTF = ctx.alloc("BASESTF", [8, 64], F32)
    SST = ctx.alloc("SST", [64, 66], F32)
    SSB = ctx.alloc("SSB", [64, 66], BF16)
    MISC = ctx.alloc("MISC", [128, 1024], F32)
    MISC2 = ctx.alloc("MISC2", [128, 1024], F32)
    MISCB = ctx.alloc("MISCB", [128, 512], BF16)
    IDXS = ctx.alloc("IDXS", [128, 16], U32)
    SCL = ctx.alloc("SCL", [128, 32], F32)
    XNT = ctx.alloc("XNT", [128, 1024], BF16)

    P0 = st.enter_context(nc.psum_tensor("P0", [128, 1024], F32))
    P1 = st.enter_context(nc.psum_tensor("P1", [128, 1024], F32))
    P2 = st.enter_context(nc.psum_tensor("P2", [128, 1024], F32))
    P3 = st.enter_context(nc.psum_tensor("P3", [128, 512], F32))
    PT = st.enter_context(nc.psum_tensor("PT", [128, 1024], BF16))

    blk = _BlkWrap(st.enter_context(nc.Block()))
    for s in ("s_ld", "s_g", "s_t", "s_v", "s_a", "s_o", "s_so", "s_bar"):
        ctx.sem(s)
    s_ld, s_g, s_t, s_v, s_a, s_o, s_so = ("s_ld", "s_g", "s_t", "s_v",
                                           "s_a", "s_o", "s_so")

    ID = ctx.sb["ID128"]
    ONESB = ctx.sb["ONESB"]
    ONESRF = ctx.sb["ONESRF"]
    ARM8 = ARM.rearrange("p (c u) -> p c u", c=8)
    PT8 = PT.rearrange("p (c u) -> p c u", c=8)

    def trf(name):
        i = TRF_IDX[name]
        return TRFSB[0:64, i * 64:(i + 1) * 64]

    def VC(eng, inst):
        val = ctx.inc(inst, "s_vc", 1)
        eng.wait_ge(ctx.sem("s_vc"), val)
        return inst

    def barrier():
        dma_sems = [(s, ctx.val(s)) for s in (s_ld, s_o, s_so, s_g)]
        for name in ("tensor", "vector", "scalar", "gpsimd", "sync"):
            def mark(eng):
                for s, v in dma_sems:
                    eng.wait_ge(ctx.sem(s), v)
                ctx.inc(eng.nop(), "s_bar", 1)
            getattr(blk, name)(mark)
        tgt = ctx.val("s_bar")
        for name in ("tensor", "vector", "scalar", "gpsimd", "sync"):
            def wt(eng):
                eng.wait_ge(ctx.sem("s_bar"), tgt)
            getattr(blk, name)(wt)

    @blk.sync
    def _(s):
        for nm in CONST_NAMES:
            ctx.inc(s.dma_start(out=ctx.sb[nm][:], in_=cdram[nm][:]), s_ld, 16)

    @blk.vector
    def _(v):
        v.memset(MISC[:], 0.0)
        ctx.inc(v.memset(IDXS[:], 0), s_v, 1)

    @blk.scalar
    def _(s):
        ctx.wait(s, s_v)
        for i0 in range(0, NIDX, 128):
            ctx.inc(s.dma_start(out=d_idx[i0:i0 + 128, :], in_=IDXS[:, 0:1]),
                    s_so, 16)
        ctx.inc(s.dma_start(out=d_eo[EOROWS - 1:EOROWS, :],
                            in_=MISC[0:1, 0:1024]), s_so, 16)

    @blk.tensor
    def _(t):
        ctx.wait(t, s_ld)
        bd = ctx.sb["BD"]

        def bdap(lvl, which):
            off = (lvl * 4 + which) * 64
            kr = NUM_NODES[lvl] * RANKS[lvl]
            return bd[0:kr, off:off + 64]
        specs = [(bdap(2, 2), bdap(2, 2)), (bdap(3, 2), bdap(3, 2)),
                 (bdap(2, 0), bdap(2, 0)), (bdap(3, 0), bdap(3, 0))]
        for lvl in range(4):
            specs.append((bdap(lvl, 0), bdap(lvl, 1)))
        for lvl in range(4):
            specs.append((bdap(lvl, 2), bdap(lvl, 3)))
        for i, (lhsT, rhs) in enumerate(specs):
            t.matmul(P0[0:64, i * 64:(i + 1) * 64], lhsT, rhs,
                     start=(i % 8 == 0), stop=True, skip_group_check=True)
        t.matmul(P1[:, 0:512], ONESRF[0:1, :], ctx.sb["BROW"][0:1, 0:512],
                 start=True, stop=True, skip_group_check=True)
        t.matmul(P1[:, 512:1024], ONESRF[0:1, :], ctx.sb["BROW"][0:1, 512:1024],
                 start=True, stop=True, skip_group_check=True)
        ctx.inc(t.last, s_t, 1)

    @blk.vector
    def _(v):
        ctx.wait(v, s_t)
        v.tensor_copy(TRFSB[0:64, :], P0[0:64, 0:768])
        ctx.inc(v.tensor_copy(BOUTBC[:], P1[:]), s_v, 1)

    barrier()

    def emit_ln_tile(dram_src, row0, fm_sink, after_tm=None):
        @blk.sync
        def _(s):
            ctx.wait(s, s_v)
            ctx.wait(s, s_a)
            ctx.inc(s.dma_start(out=MISC[:], in_=dram_src[row0:row0 + 128, :]),
                    s_ld, 16)
        ldv = ctx.val(s_ld)

        @blk.scalar
        def _(sc):
            sc.wait_ge(ctx.sem(s_ld), ldv)
            ctx.wait(sc, s_v)
            ctx.inc(sc.activation(MISC2[:], MISC[:], AF.Square,
                                  accum_out=SCL[:, 1:2]), s_a, 1)
        av = ctx.val(s_a)

        @blk.vector
        def _(v):
            v.wait_ge(ctx.sem(s_ld), ldv)
            VC(v, v.reduce_sum(SCL[:, 0:1], MISC[:], axis=AX.X))
            v.wait_ge(ctx.sem(s_a), av)
            VC(v, v.tensor_scalar_mul(SCL[:, 2:3], SCL[:, 0:1], 1.0 / D))
            VC(v, v.tensor_scalar_mul(SCL[:, 3:4], SCL[:, 1:2], 1.0 / D))
            VC(v, v.tensor_tensor(SCL[:, 4:5], SCL[:, 2:3], SCL[:, 2:3], ALU.mult))
            VC(v, v.tensor_sub(SCL[:, 5:6], SCL[:, 3:4], SCL[:, 4:5]))
            ctx.inc(v.tensor_scalar_add(SCL[:, 6:7], SCL[:, 5:6], 1e-5), s_v, 1)
        vv = ctx.val(s_v)

        @blk.scalar
        def _(sc):
            sc.wait_ge(ctx.sem(s_v), vv)
            ctx.inc(sc.activation(SCL[:, 7:8], SCL[:, 6:7], AF.Sqrt), s_a, 1)
        av2 = ctx.val(s_a)

        @blk.vector
        def _(v):
            v.wait_ge(ctx.sem(s_a), av2)
            VC(v, v.reciprocal(SCL[:, 8:9], SCL[:, 7:8]))
            ctx.wait(v, s_so)
            ctx.inc(v.tensor_scalar(XNT[:], MISC[:], SCL[:, 2:3], SCL[:, 8:9],
                                    ALU.subtract, ALU.mult), s_v, 1)
        vv2 = ctx.val(s_v)
        if after_tm is not None:
            after_tm(vv2)

        @blk.tensor
        def _(t):
            t.wait_ge(ctx.sem(s_v), vv2)
            ctx.wait(t, s_v)
            for j in range(8):
                t.matmul(PT[:, j * 128:(j + 1) * 128],
                         XNT[:, j * 128:(j + 1) * 128], ID[:],
                         start=True, stop=True, is_transpose=True,
                         skip_group_check=True)
            ctx.inc(t.last, s_t, 1)
        tv = ctx.val(s_t)

        @blk.vector
        def _(v):
            v.wait_ge(ctx.sem(s_t), tv)
            fm_sink(v)
            ctx.inc(v.last, s_v, 1)

    def emit_scan_chunk(b, ci):
        SS = scan_steps
        PWsb = ctx.sb["PW"]
        TQsb = ctx.sb["TQ"]
        CAUS = ctx.sb["CAUSREP"]
        M7 = ctx.sb["MASK7"]
        qf = ARK[0:64, ci * 128:ci * 128 + 128]
        kf = ARK[64:128, ci * 128:ci * 128 + 128]
        vf = ARK[0:64, 2048 + ci * 128:2048 + ci * 128 + 128]
        kf0 = ARL[0:64, 2048 + ci * 128:2048 + ci * 128 + 128]
        ktc = ARK[:, 4096 + ci * 64:4096 + ci * 64 + 64]
        vtc = ARK[:, 5120 + ci * 64:5120 + ci * 64 + 64]
        ytm = ARK[:, 6144 + ci * 64:6144 + ci * 64 + 64]
        SIG = ARM[:, 0:6016]
        WP = ARM[:, 6144:8192]
        QP = ARM[:, 8192:10240]
        US = ARM[:, 10240:10624]
        t1sb = MISCB[0:64, 0:64]
        t2sb = MISCB[0:64, 64:128]
        SAND = MISCB[0:64, 128:512]

        def ustream_ap(name, c0, c1):
            if name == 'ktm':
                return ktc[:, c0:c1]
            i = dict(G2k=0, G3k=1, R0v=2, R1v=3, R2v=4, R3v=5)[name]
            return US[:, i * 64 + c0:i * 64 + c1]

        if 1 not in SS:
            return

        @blk.tensor
        def _(t):
            ctx.wait(t, s_v)
            for j in range(8):
                t.matmul(P0[:, (j % 8) * 128:(j % 8) * 128 + 128],
                         PWsb[0:64, j * 128:j * 128 + 128], vf,
                         start=(j % 4 == 0), stop=True, skip_group_check=True)
            for j in range(8, 11):
                t.matmul(P1[:, (j - 8) * 128:(j - 8) * 128 + 128],
                         PWsb[0:64, j * 128:j * 128 + 128], vf,
                         start=(j == 8), stop=True, skip_group_check=True)
            for j in range(11, 15):
                t.matmul(P1[:, 512 + (j - 11) * 128:512 + (j - 11) * 128 + 128],
                         PWsb[64:128, j * 128:j * 128 + 128], kf,
                         start=(j == 11), stop=True, skip_group_check=True)
            t.matmul(P2[:, 0:128], PWsb[64:128, 15 * 128:15 * 128 + 128], kf,
                     start=True, stop=True, skip_group_check=True)
            for j in range(4):
                t.matmul(P2[:, 512 + j * 128:512 + j * 128 + 128],
                         TQsb[0:64, j * 128:j * 128 + 128], qf,
                         start=(j == 0), stop=True, skip_group_check=True)
            ctx.inc(t.last, s_t, 1)
        tv1 = ctx.val(s_t)

        @blk.vector
        def _(v):
            v.wait_ge(ctx.sem(s_t), tv1)
            v.tensor_copy(WP[:, 0:1024], P0[:])
            v.tensor_copy(WP[:, 1024:1408], P1[:, 0:384])
            v.tensor_copy(WP[:, 1408:1920], P1[:, 512:1024])
            v.tensor_copy(WP[:, 1920:2048], P2[:, 0:128])
            v.tensor_copy(QP[:, 0:512], P2[:, 512:1024])
            ctx.inc(v.last, s_v, 1)
        vv1 = ctx.val(s_v)

        if 2 not in SS:
            return

        @blk.tensor
        def _(t):
            t.wait_ge(ctx.sem(s_v), vv1)
            for j in range(4, 12):
                t.matmul(P0[:, (j - 4) * 128:(j - 4) * 128 + 128],
                         TQsb[0:64, j * 128:j * 128 + 128], qf,
                         start=((j - 4) % 4 == 0), stop=True,
                         skip_group_check=True)
            for j in range(12, 16):
                t.matmul(P1[:, (j - 12) * 128:(j - 12) * 128 + 128],
                         TQsb[0:64, j * 128:j * 128 + 128], qf,
                         start=(j == 12), stop=True, skip_group_check=True)
            us_specs = [("G2", kf0), ("G3", kf0), ("RT0", vf), ("RT1", vf),
                        ("RT2", vf), ("RT3", vf)]
            for i, (nm, src2) in enumerate(us_specs):
                t.matmul(P3[:, 64 + i * 64:64 + i * 64 + 64], src2, trf(nm),
                         start=(i == 0), stop=True, skip_group_check=True)
            t.matmul(P3[0:64, 0:64], ktc, vtc, start=False, stop=True,
                     skip_group_check=True)
            t.matmul(P3[0:64, 448:512], vtc, ktc, start=False, stop=True,
                     skip_group_check=True)
            ctx.inc(t.last, s_t, 1)
        tv2 = ctx.val(s_t)

        @blk.vector
        def _(v):
            v.wait_ge(ctx.sem(s_t), tv2)
            v.tensor_copy(QP[:, 512:1536], P0[:])
            v.tensor_copy(QP[:, 1536:2048], P1[:, 0:512])
            v.tensor_copy(US[:], P3[:, 64:448])
            v.tensor_copy(t1sb, P3[0:64, 0:64])
            v.tensor_copy(t2sb, P3[0:64, 448:512])
            ctx.inc(v.last, s_v, 1)
        vv2 = ctx.val(s_v)

        def score_mm(t, i, ps, col, first):
            wt, ws = comp_slot(i)
            qt_, qs = q_slot(i)
            assert ws == qs
            if wt < 8:
                wcol = wt * 128
            elif wt < 11:
                wcol = 1024 + (wt - 8) * 128
            elif wt < 15:
                wcol = 1408 + (wt - 11) * 128
            else:
                wcol = 1920
            qcol = qt_ * 128
            t.matmul(ps[:, col:col + 128],
                     WP[32 * ws:32 * ws + 32, wcol:wcol + 128],
                     QP[32 * qs:32 * qs + 32, qcol:qcol + 128],
                     start=first, stop=True, skip_group_check=True)

        if 3 not in SS:
            return
        @blk.tensor
        def _(t):
            t.wait_ge(ctx.sem(s_v), vv2)
            for n, i in enumerate(_CLS[0]):
                ps = (P0, P1)[n // 8]
                col = (n % 8) * 128
                score_mm(t, i, ps, col, (n % 4 == 0))
            t.matmul(P2[:, 0:128], kf0, qf, start=True, stop=True,
                     skip_group_check=True)
            ctx.inc(t.last, s_t, 1)
        tv3 = ctx.val(s_t)

        @blk.vector
        def _(v):
            v.wait_ge(ctx.sem(s_t), tv3)
            v.tensor_tensor(SIG[:, 0:1024], P0[:], CAUS[:], ALU.mult)
            v.tensor_tensor(SIG[:, 1024:2048], P1[:], CAUS[:], ALU.mult)
            v.tensor_tensor(SIG[:, 2048:2176], P2[:, 0:128], CAUS[:, 0:128],
                            ALU.mult)
            ctx.inc(v.last, s_v, 1)
        vv3 = ctx.val(s_v)

        if 4 not in SS:
            return
        @blk.tensor
        def _(t):
            t.wait_ge(ctx.sem(s_v), vv3)
            for n, i in enumerate(_CLS[1]):
                ps = (P0, P1)[n // 8]
                col = (n % 8) * 128
                score_mm(t, i, ps, col, (n % 4 == 0))
            ctx.inc(t.last, s_t, 1)
        tv4a = ctx.val(s_t)

        @blk.vector
        def _(v):
            v.wait_ge(ctx.sem(s_t), tv4a)
            n1 = len(_CLS[1])
            v.tensor_tensor(SIG[:, 2176:3200], P0[:], CAUS[:], ALU.mult)
            w = (n1 - 8) * 128
            v.tensor_tensor(SIG[:, 3200:3200 + w], P1[:, 0:w], CAUS[:, 0:w],
                            ALU.mult)
            ctx.inc(v.last, s_v, 1)
        vv4a = ctx.val(s_v)

        @blk.tensor
        def _(t):
            t.wait_ge(ctx.sem(s_v), vv4a)
            for n, i in enumerate(_CLS[2]):
                ps = (P0, P1)[n // 8]
                col = (n % 8) * 128
                score_mm(t, i, ps, col, (n % 4 == 0))
            ctx.inc(t.last, s_t, 1)
        tv4 = ctx.val(s_t)

        @blk.vector
        def _(v):
            v.wait_ge(ctx.sem(s_t), tv4)
            base2 = (17 + len(_CLS[1])) * 128
            n2 = len(_CLS[2])
            v.tensor_tensor(SIG[:, base2:base2 + 1024], P0[:], CAUS[:], ALU.mult)
            w = (n2 - 8) * 128
            v.tensor_tensor(SIG[:, base2 + 1024:base2 + 1024 + w], P1[:, 0:w],
                            CAUS[:, 0:w], ALU.mult)
            ctx.inc(v.last, s_v, 1)
        vv4 = ctx.val(s_v)

        if 5 not in SS:
            return
        @blk.tensor
        def _(t):
            t.wait_ge(ctx.sem(s_v), vv4)
            t.matmul(P3[:, 0:64], qf, SSB[0:64, 0:64], start=True, stop=False,
                     skip_group_check=True)
            for i in range(46):
                c0, c1 = COMPS[i]['ucols']
                sc0 = SIGCOL[i]
                t.matmul(P3[:, c0:c1], SIG[:, sc0:sc0 + 128],
                         ustream_ap(COMPS[i]['ustream'], c0, c1),
                         start=False, stop=True, skip_group_check=True)
            gc = SIGCOL["G"]
            t.matmul(P3[:, 64:65], SIG[:, gc:gc + 128], ONESB[:],
                     start=False, stop=False, skip_group_check=True)
            t.matmul(P3[:, 64:65], qf, SSB[0:64, 64:65], start=False, stop=True,
                     skip_group_check=True)
            t.matmul(P3[0:64, 65:66], ktc, ONESB[:], start=False, stop=True,
                     skip_group_check=True)
            t.matmul(P3[0:64, 128:192], t1sb, trf("G2"), start=False, stop=True,
                     skip_group_check=True)
            t.matmul(P3[0:64, 192:256], t1sb, trf("G3"), start=False, stop=True,
                     skip_group_check=True)
            for lvl in range(4):
                t.matmul(P3[0:64, 256 + lvl * 64:256 + lvl * 64 + 64], t2sb,
                         trf(f"RT{lvl}"), start=False, stop=True,
                         skip_group_check=True)
            ctx.inc(t.last, s_t, 1)
        tv5 = ctx.val(s_t)

        @blk.vector
        def _(v):
            v.wait_ge(ctx.sem(s_t), tv5)
            VC(v, v.tensor_scalar_max(SCL[:, 10:11], P3[:, 64:65], 1e-6))
            VC(v, v.reciprocal(SCL[:, 11:12], SCL[:, 10:11]))
            v.tensor_scalar(ytm[:], P3[:, 0:64], SCL[:, 11:12], None, ALU.mult)
            v.tensor_copy(SAND, P3[0:64, 128:512])
            VC(v, v.tensor_add(SST[0:64, 64:65], SST[0:64, 64:65],
                               P3[0:64, 65:66]))
            ctx.inc(v.last, s_v, 1)
        vv5 = ctx.val(s_v)

        if 6 not in SS:
            return
        @blk.tensor
        def _(t):
            t.wait_ge(ctx.sem(s_v), vv5)
            t.matmul(P0[0:64, 0:64], trf("H2"), SAND[0:64, 0:64], start=True,
                     stop=True, skip_group_check=True)
            t.matmul(P0[0:64, 64:128], trf("H3"), SAND[0:64, 64:128],
                     start=False, stop=True, skip_group_check=True)
            for lvl in range(4):
                t.matmul(P0[0:64, 128 + lvl * 64:128 + lvl * 64 + 64],
                         trf(f"WT{lvl}"), SAND[0:64, 128 + lvl * 64:128 + lvl * 64 + 64],
                         start=False, stop=True, skip_group_check=True)
            ctx.inc(t.last, s_t, 1)
        tv6 = ctx.val(s_t)

        @blk.vector
        def _(v):
            v.wait_ge(ctx.sem(s_t), tv6)
            v.tensor_tensor(ARS[0:64, 0:64], t2sb, M7[:, 0:64], ALU.mult)
            VC(v, v.tensor_tensor(ARS[0:64, 64:448], P0[0:64, 0:384],
                                  M7[:, 64:448], ALU.mult))
            for p in range(7):
                VC(v, v.tensor_add(SST[0:64, 0:64], SST[0:64, 0:64],
                                   ARS[0:64, p * 64:p * 64 + 64]))
            v.tensor_copy(SSB[0:64, 0:64], SST[0:64, 0:64])
            v.tensor_copy(SSB[0:64, 64:65], SST[0:64, 64:65])
            ctx.inc(v.last, s_v, 1)

    def emit_l0(b):
      if "ln" in l0_parts:
        for tt in range(16):
            def fm_sink(v, tt=tt):
                v.tensor_copy(ARM8[:, :, tt * 128:tt * 128 + 128], PT8[:, :, :])
            emit_ln_tile(x_tm, b * T + tt * 128, fm_sink)

      if "proj" in l0_parts:
        for tt in range(4):
            @blk.tensor
            def _(t, tt=tt):
                ctx.wait(t, s_v)
                for cc in range(8):
                    t.matmul(P0[:, 0:512],
                             ctx.sb["L0W"][:, cc * 192:cc * 192 + 128],
                             ARM8[:, cc, tt * 512:tt * 512 + 512],
                             start=(cc == 0), stop=(cc == 7),
                             skip_group_check=True)
                for cc in range(8):
                    t.matmul(P3[0:64, 0:512],
                             ctx.sb["L0W"][:, cc * 192 + 128:cc * 192 + 192],
                             ARM8[:, cc, tt * 512:tt * 512 + 512],
                             start=(cc == 0), stop=(cc == 7),
                             skip_group_check=True)
                ctx.inc(t.last, s_t, 1)
            tv0 = ctx.val(s_t)

            @blk.vector
            def _(v, tt=tt):
                v.wait_ge(ctx.sem(s_t), tv0)
                ctx.wait(v, s_a)
                ctx.inc(v.tensor_scalar_min(MISC[:, 0:512], P0[:, 0:512], 0.0),
                        s_v, 1)
            vv0 = ctx.val(s_v)

            @blk.scalar
            def _(sc, tt=tt):
                sc.wait_ge(ctx.sem(s_v), vv0)
                ctx.inc(sc.activation(MISC2[:, 0:512], MISC[:, 0:512], AF.Exp),
                        s_a, 1)
            av0 = ctx.val(s_a)

            @blk.vector
            def _(v, tt=tt):
                v.wait_ge(ctx.sem(s_a), av0)
                VC(v, v.tensor_scalar_add(MISC[:, 0:512], P0[:, 0:512], 1.0))
                VC(v, v.tensor_tensor(ARK[:, tt * 512:tt * 512 + 512],
                                      MISC[:, 0:512], MISC2[:, 0:512], ALU.max))
                v.tensor_copy(ARL[0:64, 2048 + tt * 512:2048 + tt * 512 + 512],
                              ARK[64:128, tt * 512:tt * 512 + 512])
                ctx.inc(v.tensor_copy(
                    ARK[0:64, 2048 + tt * 512:2048 + tt * 512 + 512],
                    P3[0:64, 0:512]), s_v, 1)

      if "kt" in l0_parts:
        for half in range(2):
            @blk.tensor
            def _(t, half=half):
                ctx.wait(t, s_v)
                for j in range(8):
                    ci = half * 8 + j
                    t.matmul(PT[:, j * 128:j * 128 + 64],
                             ARL[0:64, 2048 + ci * 128:2048 + ci * 128 + 128],
                             ID[0:64, 0:64],
                             start=True, stop=True, is_transpose=True,
                             skip_group_check=True)
                    t.matmul(PT[:, j * 128 + 64:j * 128 + 128],
                             ARK[0:64, 2048 + ci * 128:2048 + ci * 128 + 128],
                             ID[0:64, 0:64], start=True, stop=True,
                             is_transpose=True, skip_group_check=True)
                ctx.inc(t.last, s_t, 1)
            tvh = ctx.val(s_t)

            @blk.vector
            def _(v, half=half):
                v.wait_ge(ctx.sem(s_t), tvh)
                v.tensor_copy(
                    ARK[:, 4096 + half * 512:4096 + half * 512 + 512]
                    .rearrange("p (ci u) -> p ci u", ci=8),
                    PT8[:, :, 0:64])
                ctx.inc(v.tensor_copy(
                    ARK[:, 5120 + half * 512:5120 + half * 512 + 512]
                    .rearrange("p (ci u) -> p ci u", ci=8),
                    PT8[:, :, 64:128]), s_v, 1)

      if "ffn" in l0_parts:
        FFNP = ctx.sb["FFNP"]
        for hb in range(2):
            for t2 in range(2):
                @blk.tensor
                def _(t, hb=hb, t2=t2):
                    ctx.wait(t, s_v)
                    for cc in range(8):
                        t.matmul(P1[hb * 64:hb * 64 + 64,
                                    t2 * 512:t2 * 512 + 512],
                                 ctx.sb["WIN"][:, cc * 64:cc * 64 + 64],
                                 ARM8[:, cc, hb * 1024 + t2 * 512:
                                      hb * 1024 + t2 * 512 + 512],
                                 start=(cc == 0), stop=(cc == 7),
                                 skip_group_check=True)
                    ctx.inc(t.last, s_t, 1)
        tv2 = ctx.val(s_t)

        @blk.vector
        def _(v):
            v.wait_ge(ctx.sem(s_t), tv2)
            ctx.inc(v.tensor_scalar_add(ARS[:, 0:1024], P1[:], FFNP[:, 48:49]),
                    s_v, 1)
        vv2 = ctx.val(s_v)

        for i in range(16):
            @blk.vector
            def _(v, i=i):
                if i == 0:
                    v.wait_ge(ctx.sem(s_v), vv2)
                ctx.wait(v, s_a)
                ctx.inc(v.tensor_scalar(MISC[:], ARS[:, 0:1024],
                                        FFNP[:, i:i + 1], FFNP[:, 16 + i:17 + i],
                                        ALU.mult, ALU.add), s_v, 1)
            vh = ctx.val(s_v)

            @blk.scalar
            def _(sc, i=i):
                sc.wait_ge(ctx.sem(s_v), vh)
                ctx.inc(sc.activation(MISC2[:], MISC[:], AF.Relu), s_a, 1)
            ah = ctx.val(s_a)

            @blk.vector
            def _(v, i=i):
                v.wait_ge(ctx.sem(s_a), ah)
                if i == 0:
                    gi = v.tensor_scalar(ARS[:, 1024:2048], MISC2[:],
                                         FFNP[:, 32 + i:33 + i],
                                         FFNP[:, 49:50], ALU.mult, ALU.add)
                else:
                    gi = v.scalar_tensor_tensor(
                        ARS[:, 1024:2048], MISC2[:], FFNP[:, 32 + i:33 + i],
                        ARS[:, 1024:2048], ALU.mult, ALU.add)
                gv = ctx.inc(gi, s_v, 1)
                v.wait_ge(ctx.sem(s_v), gv)

      if "ffn" in l0_parts:
        @blk.vector
        def _(v):
            ctx.wait(v, s_v)
            ctx.inc(v.tensor_copy(XNT[:], ARS[:, 1024:2048]), s_v, 1)

      if "scan" in l0_parts:
        @blk.vector
        def _(v):
            v.memset(SST[:], 0.0)
            ctx.inc(v.memset(SSB[:], 0.0), s_v, 1)
        for ci in range(scan_chunks):
            emit_scan_chunk(b, ci)

      if "scan" in l0_parts and "d_sig" in debug_outputs and b == 0:
            @blk.vector
            def _(v):
                ctx.wait(v, s_v)
                ctx.inc(v.last, s_v, 1) if False else None
                ctx.inc(v.tensor_copy(MISCB[:, 0:512], ARM[:, 0:512]), s_v, 1)
            sigv = ctx.val(s_v)

            @blk.sync
            def _(s):
                s.wait_ge(ctx.sem(s_v), sigv)
                for sl3 in range(3):
                    ctx.inc(s.dma_start(
                        out=d_natt[sl3 * 128:sl3 * 128 + 128, 0:2048],
                        in_=ARM[:, sl3 * 2048:sl3 * 2048 + 2048]), s_o, 16)
                ctx.inc(s.dma_start(out=d_natt[384:512, 0:2048],
                                    in_=ARM[:, 6144:8192]), s_o, 16)
                ctx.inc(s.dma_start(out=d_natt[512:640, 0:2048],
                                    in_=ARM[:, 8192:10240]), s_o, 16)
                ctx.inc(s.dma_start(out=d_natt[640:768, 0:2048],
                                    in_=ARK[:, 0:2048]), s_o, 16)
                ctx.inc(s.dma_start(out=d_natt[768:896, 0:2048],
                                    in_=ARK[:, 2048:4096]), s_o, 16)
                ctx.inc(s.dma_start(out=d_natt[896:1024, 0:2048],
                                    in_=ARK[:, 4096:6144]), s_o, 16)
                ctx.inc(s.dma_start(out=d_natt[1024:1152, 0:384],
                                    in_=ARM[:, 10240:10624]), s_o, 16)

      if "scan" in l0_parts and "d_y0" in debug_outputs:
            @blk.vector
            def _(v):
                ctx.wait(v, s_v)
                ctx.inc(v.tensor_copy(MISC[:, 0:scan_chunks * 64],
                                      ARK[:, 6144:6144 + scan_chunks * 64]),
                        s_v, 1)
            dvv = ctx.val(s_v)

            @blk.sync
            def _(s):
                s.wait_ge(ctx.sem(s_v), dvv)
                dy = d_y0.rearrange("(a p) u -> a p u", p=128)
                for tt in range(scan_chunks):
                    ctx.inc(s.dma_start(
                        out=dy[b * 16 + tt],
                        in_=MISC[:, tt * 64:tt * 64 + 64]), s_o, 16)

      if "ytr" in l0_parts:
        for half in range(2):
            @blk.tensor
            def _(t, half=half):
                ctx.wait(t, s_v)
                for j in range(8):
                    ci = half * 8 + j
                    t.matmul(PT[0:64, j * 128:j * 128 + 128],
                             ARK[:, 6144 + ci * 64:6144 + ci * 64 + 64], ID[:],
                             start=True, stop=True, is_transpose=True,
                             skip_group_check=True)
                ctx.inc(t.last, s_t, 1)
            tvy = ctx.val(s_t)

            @blk.vector
            def _(v, half=half):
                v.wait_ge(ctx.sem(s_t), tvy)
                ctx.inc(v.tensor_copy(
                    ARL[0:64, half * 1024:half * 1024 + 1024], PT[0:64, :]),
                    s_v, 1)

      if "asm" in l0_parts:
        for tt in range(16):
            @blk.sync
            def _(s, tt=tt):
                ctx.wait(s, s_v)
                ctx.inc(s.dma_start(
                    out=MISC2[:],
                    in_=x_tm[b * T + tt * 128:b * T + tt * 128 + 128, :]),
                    s_ld, 16)
            ldv = ctx.val(s_ld)

            @blk.tensor
            def _(t, tt=tt):
                ctx.wait(t, s_v)
                hb = tt // 8
                for half in range(2):
                    t.matmul(P1[:, half * 512:half * 512 + 512],
                             ARL[0:64, tt * 128:tt * 128 + 128],
                             ctx.sb["L0WO"][0:64, half * 512:half * 512 + 512],
                             start=True, stop=True, skip_group_check=True)
                gsl = XNT
                for half in range(2):
                    t.matmul(P2[:, half * 512:half * 512 + 512],
                             gsl[hb * 64:hb * 64 + 64,
                                 (tt % 8) * 128:(tt % 8) * 128 + 128],
                             ctx.sb["WOUT"][hb * 64:hb * 64 + 64,
                                            half * 512:half * 512 + 512],
                             start=True, stop=True, skip_group_check=True)
                ctx.inc(t.last, s_t, 1)
            tvA = ctx.val(s_t)

            @blk.vector
            def _(v, tt=tt):
                v.wait_ge(ctx.sem(s_t), tvA)
                v.wait_ge(ctx.sem(s_ld), ldv)
                ctx.wait(v, s_o)
                VC(v, v.tensor_add(MISC[:], MISC2[:], P1[:]))
                VC(v, v.tensor_add(MISC[:], MISC[:], P2[:]))
                ctx.inc(v.tensor_add(MISC[:], MISC[:], BOUTBC[:]), s_v, 1)
            vvA = ctx.val(s_v)

            @blk.sync
            def _(s, tt=tt):
                s.wait_ge(ctx.sem(s_v), vvA)
                ctx.inc(s.dma_start(
                    out=d_x1[b * T + tt * 128:b * T + tt * 128 + 128, :],
                    in_=MISC[:]), s_o, 16)

    def emit_ln1(b):
        for tt in range(16):
            gtile = b * 16 + tt

            def after_tm(vv2, tt=tt, gtile=gtile):
                @blk.scalar
                def _(sc):
                    sc.wait_ge(ctx.sem(s_v), vv2)
                    ctx.inc(sc.dma_start(
                        out=d_xn1tm[gtile * 128:gtile * 128 + 128, :],
                        in_=XNT[:]), s_so, 16)

            def fm_sink(v, tt=tt, gtile=gtile):
                v.tensor_copy(ARM8[:, :, tt * 128:tt * 128 + 128], PT8[:, :, :])
            emit_ln_tile(d_x1, b * T + tt * 128, fm_sink, after_tm=after_tm)
            vfm = ctx.val(s_v)

            @blk.tensor
            def _(t, tt=tt, gtile=gtile):
                t.wait_ge(ctx.sem(s_v), vfm)
                for cc in range(8):
                    t.matmul(P3[:, 0:8],
                             ARM8[:, cc, tt * 128:tt * 128 + 128],
                             ctx.sb["RWT"][:, cc * 8:cc * 8 + 8],
                             start=(cc == 0), stop=(cc == 7),
                             skip_group_check=True)
                ctx.inc(t.last, s_t, 1)
            tvr = ctx.val(s_t)

            @blk.vector
            def _(v, tt=tt, gtile=gtile):
                v.wait_ge(ctx.sem(s_t), tvr)
                ctx.inc(v.tensor_copy(RWIDE[:, gtile * 8:gtile * 8 + 8],
                                      P3[:, 0:8]), s_v, 1)

        @blk.sync
        def _(s):
            ctx.wait(s, s_v)
            fmv = d_xn1fm.rearrange("(bb c p) u -> bb c p u", bb=B_CORE, p=128)
            for cc in range(8):
                ctx.inc(s.dma_start(out=fmv[b, cc],
                                    in_=ARM8[:, cc, :]), s_o, 16)

    def emit_route():
        TRIU = ctx.sb["TRIU"]
        ONESM = ctx.sb["ONESM"]
        IOTAE = ctx.sb["IOTAE"]
        for g in range(32):
            @blk.vector
            def _(v, g=g):
                if g == 0:
                    ctx.wait(v, s_v)
                lg = RWIDE[:, g * 8:g * 8 + 8]
                VC(v, v.max(MISC[:, 0:8], lg))
                v.tensor_copy(RTOP[:, g:g + 1], MISC[:, 0:1])
                v.tensor_copy(RTOP[:, 32 + g:32 + g + 1], MISC[:, 1:2])
                VC(v, v.max_index(IDXS[:, 0:8], lg, MISC[:, 0:8]))
                ctx.inc(v.tensor_copy(RANKW[:, g * 8:g * 8 + 8],
                                      IDXS[:, 0:8]), s_v, 1)

        @blk.vector
        def _(v):
            ctx.wait(v, s_v)
            VC(v, v.tensor_single_scalar(OH1W[:], RANKW[:], 0.0, ALU.is_equal))
            VC(v, v.tensor_single_scalar(OH2W[:], RANKW[:], 1.0, ALU.is_equal))
            ctx.inc(v.tensor_sub(W12[:, 0:32], RTOP[:, 0:32], RTOP[:, 32:64]),
                    s_v, 1)
        vv = ctx.val(s_v)

        @blk.scalar
        def _(sc):
            sc.wait_ge(ctx.sem(s_v), vv)
            ctx.inc(sc.activation(W12[:, 0:32], W12[:, 0:32], AF.Sigmoid),
                    s_a, 1)
        av = ctx.val(s_a)

        @blk.vector
        def _(v):
            v.wait_ge(ctx.sem(s_a), av)
            VC(v, v.tensor_scalar(W12[:, 32:64], W12[:, 0:32], -1.0, 1.0,
                                  ALU.mult, ALU.add))
            for g in range(32):
                VC(v, v.tensor_tensor(MISC[:, 0:8], OH1W[:, g * 8:g * 8 + 8],
                                      IOTAE[:], ALU.mult))
                VC(v, v.reduce_sum(EW[:, 2 * g:2 * g + 1], MISC[:, 0:8],
                                   axis=AX.X))
                VC(v, v.tensor_tensor(MISC[:, 0:8], OH2W[:, g * 8:g * 8 + 8],
                                      IOTAE[:], ALU.mult))
                VC(v, v.reduce_sum(EW[:, 2 * g + 1:2 * g + 2], MISC[:, 0:8],
                                   axis=AX.X))
            v.tensor_copy(OHB[:, 0:256], OH1W[:])
            ctx.inc(v.tensor_copy(OHB[:, 256:512], OH2W[:]), s_v, 1)
        vv2 = ctx.val(s_v)

        @blk.tensor
        def _(t):
            t.wait_ge(ctx.sem(s_v), vv2)
            for g in range(32):
                for k in range(2):
                    t.matmul(P3[0:8, (2 * g + k):(2 * g + k) + 1],
                             OHB[:, k * 256 + g * 8:k * 256 + g * 8 + 8],
                             ONESB[:], start=(g == 0 and k == 0), stop=True,
                             skip_group_check=True)
            ctx.inc(t.last, s_t, 1)
        tv = ctx.val(s_t)

        @blk.vector
        def _(v):
            v.wait_ge(ctx.sem(s_t), tv)
            ctx.inc(v.tensor_copy(MISCB[0:8, 0:64], P3[0:8, 0:64]), s_v, 1)
        vv3 = ctx.val(s_v)

        @blk.tensor
        def _(t):
            t.wait_ge(ctx.sem(s_v), vv3)
            t.matmul(PT[0:64, 0:8], MISCB[0:8, 0:64], ID[0:8, 0:8],
                     start=True, stop=True, is_transpose=True,
                     skip_group_check=True)
            ctx.inc(t.last, s_t, 1)
        tv2 = ctx.val(s_t)

        @blk.vector
        def _(v):
            v.wait_ge(ctx.sem(s_t), tv2)
            ctx.inc(v.tensor_copy(MISCB[0:64, 64:72], PT[0:64, 0:8]), s_v, 1)
        vv4 = ctx.val(s_v)

        @blk.tensor
        def _(t):
            t.wait_ge(ctx.sem(s_v), vv4)
            t.matmul(P3[0:64, 64:72], ctx.sb["UTRI64"][:], MISCB[0:64, 64:72],
                     start=True, stop=True, skip_group_check=True)
            ctx.inc(t.last, s_t, 1)
        tv3 = ctx.val(s_t)

        @blk.vector
        def _(v):
            v.wait_ge(ctx.sem(s_t), tv3)
            ctx.inc(v.tensor_copy(MISC[0:64, 256:264], P3[0:64, 64:72]), s_v, 1)
        vv5 = ctx.val(s_v)

        @blk.tensor
        def _(t):
            t.wait_ge(ctx.sem(s_v), vv5)
            t.matmul(P0[0:8, 0:64], MISC[0:64, 256:264],
                     ctx.sb["IDF32"][0:64, 0:64],
                     start=True, stop=True, is_transpose=True,
                     skip_group_check=True)
            ctx.inc(t.last, s_t, 1)
        tv4 = ctx.val(s_t)

        @blk.vector
        def _(v):
            v.wait_ge(ctx.sem(s_t), tv4)
            ctx.inc(v.tensor_copy(BASESTF[0:8, 0:64], P0[0:8, 0:64]), s_v, 1)
        vv6 = ctx.val(s_v)

        for g in range(32):
            for k in range(2):
                col = 2 * g + k

                @blk.tensor
                def _(t, g=g, k=k, col=col):
                    if g == 0 and k == 0:
                        t.wait_ge(ctx.sem(s_v), vv6)
                    else:
                        ctx.wait(t, s_v)
                    ohk = OHB[:, k * 256 + g * 8:k * 256 + g * 8 + 8]
                    t.matmul(P3[:, 128:136], TRIU[:], ohk, start=True,
                             stop=True, skip_group_check=True)
                    bb = BASESTF[0:8, col:col + 1].broadcast_to([8, 128])
                    t.matmul(P2[:, 0:8], bb, ctx.sb["IDF32"][0:8, 0:8],
                             start=True, stop=True, skip_group_check=True)
                    ctx.inc(t.last, s_t, 1)
                tvk = ctx.val(s_t)

                @blk.vector
                def _(v, g=g, k=k, col=col):
                    v.wait_ge(ctx.sem(s_t), tvk)
                    VC(v, v.tensor_copy(SLW[:, col * 8:col * 8 + 8],
                                        P3[:, 128:136]))
                    ctx.inc(VC(v, v.tensor_add(SLW[:, col * 8:col * 8 + 8],
                                               SLW[:, col * 8:col * 8 + 8],
                                               P2[:, 0:8])) if False else
                            v.tensor_add(SLW[:, col * 8:col * 8 + 8],
                                         SLW[:, col * 8:col * 8 + 8],
                                         P2[:, 0:8]), s_v, 1)

        @blk.vector
        def _(v):
            ctx.wait(v, s_v)
            for g in range(32):
                for k in range(2):
                    col = 2 * g + k
                    ohk = OHB[:, k * 256 + g * 8:k * 256 + g * 8 + 8]
                    VC(v, v.tensor_tensor(MISC[:, 0:8],
                                          SLW[:, col * 8:col * 8 + 8],
                                          ohk, ALU.mult))
                    VC(v, v.reduce_sum(SELW[:, col:col + 1], MISC[:, 0:8],
                                       axis=AX.X))
            VC(v, v.tensor_single_scalar(KEEPW[:], SELW[:], float(LCAP),
                                         ALU.is_lt))
            VC(v, v.tensor_scalar(MISC[:, 0:64], EW[:], float(LCAP), None,
                                  ALU.mult))
            VC(v, v.tensor_add(MISC[:, 0:64], MISC[:, 0:64], SELW[:]))
            VC(v, v.tensor_tensor(MISC[:, 0:64], MISC[:, 0:64], KEEPW[:],
                                  ALU.mult))
            VC(v, v.tensor_scalar(MISC2[:, 0:64], KEEPW[:], -float(NE * LCAP),
                                  float(NE * LCAP), ALU.mult, ALU.add))
            VC(v, v.tensor_add(MISC[:, 0:64], MISC[:, 0:64], MISC2[:, 0:64]))
            VC(v, v.tensor_copy(RROWS[:], MISC[:, 0:64]))
            ctx.inc(v.last, s_v, 1)

        @blk.vector
        def _(v):
            KE = KEEPW.rearrange("p (g k) -> p g k", k=2)
            kw1 = MISC[:, 128:160]
            kw2 = MISC[:, 160:192]
            VC(v, v.tensor_tensor(kw1, W12[:, 0:32], KE[:, :, 0], ALU.mult))
            VC(v, v.tensor_tensor(kw2, W12[:, 32:64], KE[:, :, 1], ALU.mult))
            VC(v, v.tensor_add(MISC[:, 64:96], kw1, kw2))
            VC(v, v.tensor_scalar_max(MISC[:, 64:96], MISC[:, 64:96], EPSM))
            VC(v, v.reciprocal(MISC[:, 96:128], MISC[:, 64:96]))
            RF = RFW.rearrange("p (g k) -> p g k", k=2)
            VC(v, v.tensor_tensor(RF[:, :, 0], kw1, MISC[:, 96:128], ALU.mult))
            VC(v, v.tensor_tensor(RF[:, :, 1], kw2, MISC[:, 96:128], ALU.mult))
            ctx.inc(v.last, s_v, 1)
        vfin = ctx.val(s_v)

        if "d_rt" in debug_outputs:
            @blk.sync
            def _(s):
                s.wait_ge(ctx.sem(s_v), vfin)
                ctx.inc(s.dma_start(out=d_rt[:, 0:64], in_=SELW[:]), s_o, 16)
                ctx.inc(s.dma_start(out=d_rt[:, 64:128], in_=EW[:]), s_o, 16)
                ctx.inc(s.dma_start(out=d_rt[:, 128:192], in_=RFW[:]), s_o, 16)
                ctx.inc(s.dma_start(out=d_rt[:, 192:256],
                                    in_=MISC[:, 0:64]), s_o, 16)

        @blk.gpsimd
        def _(g_):
            g_.wait_ge(ctx.sem(s_v), vfin)
            ctx.wait(g_, s_so)
            g0 = ctx.val(s_g)
            for g in range(32):
                sl = 1 + (g % 4)
                if g >= 4:
                    g_.wait_ge(ctx.sem(s_g), g0 + (g - 3) * 32)
                g_.iota(IDXS[:, sl:sl + 1], pattern=[[0, 1]], base=g * 128,
                        channel_multiplier=1)
                for k in range(2):
                    col = 2 * g + k
                    ctx.inc(g_.indirect_dma_start(
                        d_idx[:, :], IndirectOffsetOnAxis(RROWS[:, col:col + 1], 0),
                        IDXS[:, sl:sl + 1], None), s_g, 16)

    def emit_moe():
        CT = 256
        ncts = LCAP // CT
        for e in range(moe_experts):
            @blk.sync
            def _(s, e=e):
                ctx.wait(s, s_g)
                ctx.wait(s, s_t)
                ctx.wait(s, s_v)
                for j in range(12):
                    ctx.inc(s.dma_start(
                        out=IDXS[:, 4 + j:5 + j] if j < 12 else None,
                        in_=d_idx[e * LCAP + j * 128:e * LCAP + j * 128 + 128, :]),
                        s_ld, 16)
            ldv = ctx.val(s_ld)
            tpos = {}
            for j in range(12):
                @blk.gpsimd
                def _(g_, e=e, j=j):
                    if j == 0:
                        g_.wait_ge(ctx.sem(s_ld), ldv)
                    if j >= 4:
                        g_.wait_ge(ctx.sem(s_t), tpos[j - 4])
                    ctx.inc(g_.indirect_dma_start(
                        ARK[:, (j % 4) * 1024:(j % 4) * 1024 + 1024], None,
                        d_xn1tm[:, :],
                        IndirectOffsetOnAxis(IDXS[:, 4 + j:5 + j], 0)),
                        s_g, 16)
                gvj = ctx.val(s_g)

                @blk.tensor
                def _(t, e=e, j=j):
                    t.wait_ge(ctx.sem(s_g), gvj)
                    ctx.wait(t, s_v)
                    for cc in range(8):
                        t.matmul(PT[:, cc * 128:cc * 128 + 128],
                                 ARK[:, (j % 4) * 1024 + cc * 128:(j % 4) * 1024 + cc * 128 + 128],
                                 ID[:], start=True, stop=True,
                                 is_transpose=True, skip_group_check=True)
                    ctx.inc(t.last, s_t, 1)
                tpos[j] = ctx.val(s_t)

                @blk.vector
                def _(v, e=e, j=j):
                    v.wait_ge(ctx.sem(s_t), tpos[j])
                    ctx.inc(v.tensor_copy(
                        ARM[:, 0:12288].rearrange("p (c u) -> p c u", c=8)
                        [:, :, j * 128:j * 128 + 128],
                        PT8[:, :, :]), s_v, 1)
            eiv = ctx.val(s_v)

            WBUFS = (ARW, ARW2, ctx.sb["ARW3"], ctx.sb["ARW4"])
            ACTB = (ARL, ctx.sb["ARK"])
            buf_free = {0: 0, 1: 0, 2: 0, 3: 0}
            for pj in range(8):
                bufs = (WBUFS[(pj % 2) * 2], WBUFS[(pj % 2) * 2 + 1])
                gstart = ctx.val(s_g)

                @blk.gpsimd
                def _(g_, e=e, pj=pj, bufs=bufs):
                    for half in range(2):
                        fs = 2 * pj + half
                        wbuf = bufs[half]
                        bi = (pj % 2) * 2 + half
                        if buf_free[bi] > 0:
                            g_.wait_ge(ctx.sem(s_t), buf_free[bi])
                        if fs == 0:
                            g_.wait_ge(ctx.sem(s_v), eiv)
                        wp = w_preT[e].rearrange("(c p) f -> p c f", p=128)
                        wg = w_gateT[e].rearrange("(c p) f -> p c f", p=128)
                        ctx.inc(g_.dma_start(
                            out=wbuf[:, 0:2048].rearrange("p (c f) -> p c f", c=8),
                            in_=wp[:, :, fs * 256:fs * 256 + 256]), s_g, 16)
                        ctx.inc(g_.dma_start(
                            out=wbuf[:, 2048:4096].rearrange("p (c f) -> p c f", c=8),
                            in_=wg[:, :, fs * 256:fs * 256 + 256]), s_g, 16)
                        wpo = w_postT[e].rearrange("(c p) u -> p c u", p=128)
                        ctx.inc(g_.dma_start(
                            out=wbuf[:, 4096:6144].rearrange("p (c u) -> p c u", c=2),
                            in_=wpo[:, fs * 2:fs * 2 + 2, :]), s_g, 16)
                gw = [gstart + 48, gstart + 96]

                a0 = ctx.val(s_a)
                v0 = ctx.val(s_v)
                t0 = ctx.val(s_t)

                @blk.tensor
                def _(t, e=e, pj=pj, bufs=bufs):
                    for ci, (half, fc) in enumerate(
                            [(0, 0), (0, 1), (1, 0), (1, 1)]):
                        wbuf = bufs[half]
                        t.wait_ge(ctx.sem(s_g), gw[half])
                        if ci > 0 or pj > 0:
                            t.wait_ge(ctx.sem(s_v), v0 + 2 * ci)
                        for cc in range(8):
                            for cp in range(3):
                                ps, col = ((P0, 0), (P0, 512), (P3, 0))[cp]
                                t.matmul(ps[:, col:col + 512],
                                         wbuf[:, 2048 + cc * 256 + fc * 128:
                                              2048 + cc * 256 + fc * 128 + 128],
                                         ARM[:, cc * 1536 + cp * 512:cc * 1536 + cp * 512 + 512],
                                         start=(cc == 0), stop=(cc == 7),
                                         skip_group_check=True)
                        ctx.inc(t.last, s_t, 1)
                        t.wait_ge(ctx.sem(s_a), a0 + 2 * (ci + 1))
                        for cc in range(8):
                            for cp in range(3):
                                ps, col = ((P0, 0), (P0, 512), (P3, 0))[cp]
                                t.matmul(ps[:, col:col + 512],
                                         wbuf[:, cc * 256 + fc * 128:
                                              cc * 256 + fc * 128 + 128],
                                         ARM[:, cc * 1536 + cp * 512:cc * 1536 + cp * 512 + 512],
                                         start=(cc == 0), stop=(cc == 7),
                                         skip_group_check=True)
                        ctx.inc(t.last, s_t, 1)
                    for ct in range(ncts):
                        t.wait_ge(ctx.sem(s_v), v0 + 8 + ct)
                        for half in range(2):
                            wbuf = bufs[half]
                            for fc in range(2):
                                act = ACTB[half][:, fc * 1536 + ct * 256:
                                                 fc * 1536 + ct * 256 + 256]
                                for cs in range(2):
                                    ps = P1 if cs == 0 else P2
                                    for dh in range(2):
                                        t.matmul(
                                            ps[:, dh * 512:dh * 512 + 512],
                                            act[:, cs * 128:cs * 128 + 128],
                                            wbuf[:, 4096 + fc * 1024 + dh * 512:
                                                 4096 + fc * 1024 + dh * 512 + 512],
                                            start=(half == 0 and fc == 0),
                                            stop=(half == 1 and fc == 1),
                                            skip_group_check=True)
                        ctx.inc(t.last, s_t, 1)

                @blk.scalar
                def _(sc, pj=pj):
                    for ci in range(4):
                        sc.wait_ge(ctx.sem(s_t), t0 + 2 * ci + 1)
                        ctx.inc(sc.activation(MISC[:, 0:1024], P0[:], AF.Silu),
                                s_a, 1)
                        ctx.inc(sc.activation(MISC2[:, 0:512], P3[:], AF.Silu),
                                s_a, 1)

                @blk.vector
                def _(v, pj=pj):
                    for ci, (half, fc) in enumerate(
                            [(0, 0), (0, 1), (1, 0), (1, 1)]):
                        v.wait_ge(ctx.sem(s_t), t0 + 2 * ci + 2)
                        dst = ACTB[half]
                        ctx.inc(v.tensor_tensor(
                            dst[:, fc * 1536:fc * 1536 + 1024],
                            P0[:], MISC[:, 0:1024], ALU.mult), s_v, 1)
                        ctx.inc(v.tensor_tensor(
                            dst[:, fc * 1536 + 1024:fc * 1536 + 1536],
                            P3[:], MISC2[:, 0:512], ALU.mult), s_v, 1)
                    for ct in range(ncts):
                        v.wait_ge(ctx.sem(s_t), t0 + 8 + ct + 1)
                        v.wait_ge(ctx.sem(s_g), gw[1] + 32 * ct)
                        v.tensor_copy(ctx.sb["MOEO"][:, 0:1024], P1[:])
                        ctx.inc(v.tensor_copy(ctx.sb["MOEO2"][:, 0:1024], P2[:]),
                                s_v, 1)

                @blk.gpsimd
                def _(g_, pj=pj, e=e):
                    for ct in range(ncts):
                        r0 = e * LCAP + ct * CT
                        g_.wait_ge(ctx.sem(s_v), v0 + 8 + ct + 1)
                        op = ALU.bypass if pj == 0 else ALU.add
                        ctx.inc(g_.dma_start(out=d_eo[r0:r0 + 128, :],
                                             in_=ctx.sb["MOEO"][:, 0:1024],
                                             accum_op=op), s_g, 16)
                        ctx.inc(g_.dma_start(out=d_eo[r0 + 128:r0 + 256, :],
                                             in_=ctx.sb["MOEO2"][:, 0:1024],
                                             accum_op=op), s_g, 16)
                tend = ctx.val(s_t)
                buf_free[(pj % 2) * 2] = tend
                buf_free[(pj % 2) * 2 + 1] = tend

    def emit_attn(b):
        s_a_alias = s_v
        fmv = d_xn1fm.rearrange("(bb c p) u -> bb c p u", bb=B_CORE, p=128)

        @blk.sync
        def _(s):
            ctx.wait(s, s_v)
            ctx.wait(s, s_o)
            for cc in range(8):
                ctx.inc(s.dma_start(out=ARM8[:, cc, :], in_=fmv[b, cc]),
                        s_ld, 16)
        ldx = ctx.val(s_ld)

        qkd = d_qk.rearrange("(bb two c p) u -> bb two c p u", bb=B_CORE,
                             two=2, p=128)
        for oc in range(8):
            for part, wcol in ((0, oc * 128), (1, (24 + oc) * 128)):
                wbuf = ARW if part == 0 else ARW2

                @blk.gpsimd
                def _(g_, oc=oc, wcol=wcol, wbuf=wbuf):
                    ctx.wait(g_, s_t)
                    wv = wqkv.rearrange("(c p) u -> c p u", p=128)
                    for cc in range(8):
                        ctx.inc(g_.dma_start(
                            out=wbuf[:, cc * 128:cc * 128 + 128],
                            in_=wv[cc, :, wcol:wcol + 128]), s_g, 16)
            gv = ctx.val(s_g)

            for tt in range(4):
                @blk.tensor
                def _(t, oc=oc, tt=tt):
                    t.wait_ge(ctx.sem(s_g), gv)
                    t.wait_ge(ctx.sem(s_ld), ldx)
                    ctx.wait(t, s_v)
                    for cc in range(8):
                        t.matmul(P0[:, 0:512],
                                 ARW[:, cc * 128:cc * 128 + 128],
                                 ARM8[:, cc, tt * 512:tt * 512 + 512],
                                 start=(cc == 0), stop=(cc == 7),
                                 skip_group_check=True)
                    for cc in range(8):
                        t.matmul(P1[:, 0:512],
                                 ARW2[:, cc * 128:cc * 128 + 128],
                                 ARM8[:, cc, tt * 512:tt * 512 + 512],
                                 start=(cc == 0), stop=(cc == 7),
                                 skip_group_check=True)
                    ctx.inc(t.last, s_t, 1)
                tvq = ctx.val(s_t)

                @blk.vector
                def _(v, oc=oc, tt=tt):
                    v.wait_ge(ctx.sem(s_t), tvq)
                    cs = slice(tt * 512, tt * 512 + 512)
                    VC(v, v.tensor_tensor(MISC[:, 0:512], P0[:, 0:512],
                                          ctx.sb["COSQ"][:, cs], ALU.mult))
                    VC(v, v.tensor_tensor(MISC2[:, 0:512], P1[:, 0:512],
                                          ctx.sb["SINQ"][:, cs], ALU.mult))
                    ctx.inc(v.tensor_add(ARK[:, tt * 512:tt * 512 + 512],
                                         MISC[:, 0:512], MISC2[:, 0:512]),
                            s_v, 1)
            vvq = ctx.val(s_v)

            @blk.sync
            def _(s, oc=oc):
                s.wait_ge(ctx.sem(s_v), vvq)
                ctx.inc(s.dma_start(out=qkd[b, 0, oc], in_=ARK[:, 0:2048]),
                        s_o, 16)

        for oc in range(8):
            for part, wcol in ((0, (8 + oc) * 128), (1, (32 + oc) * 128)):
                wbuf = ARW if part == 0 else ARW2

                @blk.gpsimd
                def _(g_, oc=oc, wcol=wcol, wbuf=wbuf):
                    ctx.wait(g_, s_t)
                    wv = wqkv.rearrange("(c p) u -> c p u", p=128)
                    for cc in range(8):
                        ctx.inc(g_.dma_start(
                            out=wbuf[:, cc * 128:cc * 128 + 128],
                            in_=wv[cc, :, wcol:wcol + 128]), s_g, 16)
            gv = ctx.val(s_g)

            for tt in range(4):
                @blk.tensor
                def _(t, oc=oc, tt=tt):
                    t.wait_ge(ctx.sem(s_g), gv)
                    ctx.wait(t, s_v)
                    for cc in range(8):
                        t.matmul(P0[:, 0:512],
                                 ARW[:, cc * 128:cc * 128 + 128],
                                 ARM8[:, cc, tt * 512:tt * 512 + 512],
                                 start=(cc == 0), stop=(cc == 7),
                                 skip_group_check=True)
                    for cc in range(8):
                        t.matmul(P1[:, 0:512],
                                 ARW2[:, cc * 128:cc * 128 + 128],
                                 ARM8[:, cc, tt * 512:tt * 512 + 512],
                                 start=(cc == 0), stop=(cc == 7),
                                 skip_group_check=True)
                    ctx.inc(t.last, s_t, 1)
                tvq = ctx.val(s_t)

                @blk.vector
                def _(v, oc=oc, tt=tt):
                    v.wait_ge(ctx.sem(s_t), tvq)
                    cs = slice(tt * 512, tt * 512 + 512)
                    VC(v, v.tensor_tensor(MISC[:, 0:512], P0[:, 0:512],
                                          ctx.sb["COSK"][:, cs], ALU.mult))
                    VC(v, v.tensor_tensor(MISC2[:, 0:512], P1[:, 0:512],
                                          ctx.sb["SINK"][:, cs], ALU.mult))
                    ctx.inc(v.tensor_add(ARK[:, tt * 512:tt * 512 + 512],
                                         MISC[:, 0:512], MISC2[:, 0:512]),
                            s_v, 1)
            vvq = ctx.val(s_v)

            @blk.sync
            def _(s, oc=oc):
                s.wait_ge(ctx.sem(s_v), vvq)
                ctx.inc(s.dma_start(out=qkd[b, 1, oc], in_=ARK[:, 0:2048]),
                        s_o, 16)

        def wvap(cc):
            if cc < 6:
                return ARW[:, cc * 1024:cc * 1024 + 1024]
            return ARW2[:, (cc - 6) * 1024:(cc - 6) * 1024 + 1024]

        @blk.gpsimd
        def _(g_):
            ctx.wait(g_, s_t)
            wv = wqkv.rearrange("(c p) u -> c p u", p=128)
            for cc in range(8):
                ctx.inc(g_.dma_start(out=wvap(cc),
                                     in_=wv[cc, :, 2048:3072]), s_g, 16)
        gvv = ctx.val(s_g)

        for tt in range(16):
            @blk.tensor
            def _(t, tt=tt):
                t.wait_ge(ctx.sem(s_g), gvv)
                ctx.wait(t, s_v)
                for half in range(2):
                    for cc in range(8):
                        t.matmul(P2[:, half * 512:half * 512 + 512],
                                 ARM8[:, cc, tt * 128:tt * 128 + 128],
                                 wvap(cc)[:, half * 512:half * 512 + 512],
                                 start=(cc == 0), stop=(cc == 7),
                                 skip_group_check=True)
                ctx.inc(t.last, s_t, 1)
            tvv = ctx.val(s_t)

            @blk.vector
            def _(v, tt=tt):
                v.wait_ge(ctx.sem(s_t), tvv)
                ctx.inc(v.tensor_copy(XNT[:], P2[:]), s_v, 1)
            vvv = ctx.val(s_v)

            @blk.sync
            def _(s, tt=tt):
                s.wait_ge(ctx.sem(s_v), vvv)
                ctx.inc(s.dma_start(
                    out=d_v[b * T + tt * 128:b * T + tt * 128 + 128, :],
                    in_=XNT[:]), s_o, 16)

        for h in range(attn_heads):
            cc = h // 2
            ro = (h % 2) * 64
            @blk.sync
            def _(s, h=h, cc=cc, ro=ro):
                ctx.wait(s, s_a)
                ctx.wait(s, s_t)
                ctx.inc(s.dma_start(out=ARK[0:64, 0:2048],
                                    in_=qkd[b, 0, cc][ro:ro + 64, :]), s_ld, 16)
                ctx.inc(s.dma_start(out=ARK[0:64, 2048:4096],
                                    in_=qkd[b, 1, cc][ro:ro + 64, :]), s_ld, 16)
                vv_ = d_v.rearrange("(g p) u -> g p u", p=128)
                for g in range(16):
                    ctx.inc(s.dma_start(
                        out=ARK[:, 4096 + g * 64:4096 + g * 64 + 64],
                        in_=vv_[b * 16 + g][:, h * 64:h * 64 + 64]), s_ld, 16)
            ldh = ctx.val(s_ld)

            for which in range(2):
                @blk.vector
                def _(v, h=h, which=which):
                    if which == 0:
                        v.wait_ge(ctx.sem(s_ld), ldh)
                    else:
                        ctx.wait(v, s_t)
                    qs = ARK[0:64, which * 2048:which * 2048 + 2048]
                    ctx.inc(v.tensor_tensor(ARL[0:64, 0:2048], qs, qs,
                                            ALU.mult), s_a_alias, 1)
                avh = ctx.val(s_a_alias)

                @blk.tensor
                def _(t, h=h, which=which):
                    t.wait_ge(ctx.sem(s_a_alias), avh)
                    for i4 in range(4):
                        ps = (P0, P0, P1, P1)[i4]
                        col = (i4 % 2) * 512
                        t.matmul(ps[0:1, col:col + 512], ONESB[0:64, :],
                                 ARL[0:64, i4 * 512:i4 * 512 + 512],
                                 start=True, stop=True, skip_group_check=True)
                    ctx.inc(t.last, s_t, 1)
                tvn = ctx.val(s_t)

                @blk.vector
                def _(v, h=h, which=which):
                    v.wait_ge(ctx.sem(s_t), tvn)
                    VC(v, v.reduce_max(SCL[0:1, 20:21], P0[0:1, :], axis=AX.X))
                    VC(v, v.reduce_max(SCL[0:1, 21:22], P1[0:1, :], axis=AX.X))
                    v.tensor_max(SCL[0:1, 12 + which:13 + which],
                                 SCL[0:1, 20:21], SCL[0:1, 21:22])
                    ctx.inc(v.last, s_v, 1)

            @blk.vector
            def _(v, h=h):
                ctx.wait(v, s_v)
                ctx.inc(v.tensor_tensor(SCL[0:1, 14:15], SCL[0:1, 12:13],
                                        SCL[0:1, 13:14], ALU.mult), s_v, 1)
            vvc = ctx.val(s_v)

            @blk.scalar
            def _(sc, h=h):
                sc.wait_ge(ctx.sem(s_v), vvc)
                ctx.inc(sc.activation(SCL[0:1, 15:16], SCL[0:1, 14:15],
                                      AF.Sqrt), s_a, 1)
            avc = ctx.val(s_a)

            @blk.vector
            def _(v, h=h):
                v.wait_ge(ctx.sem(s_a), avc)
                ctx.inc(v.tensor_scalar_mul(SCL[0:1, 16:17], SCL[0:1, 15:16],
                                            -1.0), s_v, 1)
            vvc2 = ctx.val(s_v)

            @blk.tensor
            def _(t, h=h):
                t.wait_ge(ctx.sem(s_v), vvc2)
                t.matmul(P3[:, 508:509], ONESRF[0:1, :], SCL[0:1, 16:17],
                         start=True, stop=True, skip_group_check=True)
                ctx.inc(t.last, s_t, 1)
            tvc = ctx.val(s_t)

            @blk.vector
            def _(v, h=h):
                v.wait_ge(ctx.sem(s_t), tvc)
                ctx.inc(v.tensor_copy(SCL[:, 17:18], P3[:, 508:509]), s_v, 1)
            vnegc = ctx.val(s_v)

            T0 = ctx.val(s_t)
            A0 = ctx.val(s_a)
            V0 = ctx.val(s_v)

            @blk.tensor
            def _(t, h=h):
                for qt in range(4):
                    Tq = T0 + 10 * qt
                    Aq = A0 + 16 * qt
                    for rd in range(8):
                        ps = P0 if rd % 2 == 0 else P1
                        if qt == 0 and rd == 0:
                            t.wait_ge(ctx.sem(s_v), vnegc)
                        if rd >= 2:
                            t.wait_ge(ctx.sem(s_a), Aq + 2 * (rd - 1))
                        if qt > 0 and rd < 2:
                            t.wait_ge(ctx.sem(s_a), Aq)
                            t.wait_ge(ctx.sem(s_v), V0 + 2 * qt)
                        for j in range(2):
                            kc = rd * 2 + j
                            t.matmul(ps[:, j * 512:j * 512 + 512],
                                     ARK[0:64, 2048 + kc * 128:2048 + kc * 128 + 128],
                                     ARK[0:64, qt * 512:qt * 512 + 512],
                                     start=True, stop=True,
                                     skip_group_check=True)
                        ctx.inc(t.last, s_t, 1)
                    t.wait_ge(ctx.sem(s_a), Aq + 16)
                    if qt > 0:
                        t.wait_ge(ctx.sem(s_v), V0 + 2 * qt)
                    for kc in range(16):
                        t.matmul(P3[0:1, 0:512], ONESB[:],
                                 ARM[:, kc * 512:kc * 512 + 512],
                                 start=(kc == 0), stop=(kc == 15),
                                 skip_group_check=True)
                    for kc in range(16):
                        t.matmul(P2[0:64, 0:512],
                                 ARK[:, 4096 + kc * 64:4096 + kc * 64 + 64],
                                 ARM[:, kc * 512:kc * 512 + 512],
                                 start=(kc == 0), stop=(kc == 15),
                                 skip_group_check=True)
                    ctx.inc(t.last, s_t, 1)
                    t.wait_ge(ctx.sem(s_v), V0 + 2 * qt + 1)
                    t.matmul(P0[0:64, 0:512], ONESRF[0:1, 0:64],
                             ARS[0:1, 0:512], start=True, stop=True,
                             skip_group_check=True)
                    ctx.inc(t.last, s_t, 1)
            assert ctx.val(s_t) == T0 + 40

            @blk.scalar
            def _(sc, h=h):
                for qt in range(4):
                    Tq = T0 + 10 * qt
                    for rd in range(8):
                        sc.wait_ge(ctx.sem(s_t), Tq + rd + 1)
                        ps = P0 if rd % 2 == 0 else P1
                        for j in range(2):
                            kc = rd * 2 + j
                            ctx.inc(sc.activation(
                                ARM[:, kc * 512:kc * 512 + 512],
                                ps[:, j * 512:j * 512 + 512], AF.Exp,
                                bias=SCL[:, 17:18]), s_a, 1)
            assert ctx.val(s_a) == A0 + 64

            @blk.vector
            def _(v, h=h):
                for qt in range(4):
                    Tq = T0 + 10 * qt
                    v.wait_ge(ctx.sem(s_t), Tq + 9)
                    ctx.inc(v.reciprocal(ARS[0:1, 0:512], P3[0:1, 0:512]),
                            s_v, 1)
                    v.wait_ge(ctx.sem(s_t), Tq + 10)
                    VC(v, v.tensor_copy(MISC[0:64, 0:512], P0[0:64, 0:512]))
                    ctx.inc(v.tensor_tensor(
                        ARL[0:64, 2048 + qt * 512:2048 + qt * 512 + 512],
                        P2[0:64, 0:512], MISC[0:64, 0:512], ALU.mult),
                        s_v, 1)
            assert ctx.val(s_v) == V0 + 8
            vnat = ctx.val(s_v)

            @blk.sync
            def _(s, h=h):
                s.wait_ge(ctx.sem(s_v), vnat)
                ctx.inc(s.dma_start(
                    out=d_natt[(b * NH + h) * 64:(b * NH + h) * 64 + 64, :],
                    in_=ARL[0:64, 2048:4096]), s_o, 16)

    def emit_fin(b):
        natv = d_natt.rearrange("(bb c p) u -> bb c p u", bb=B_CORE, p=128)

        @blk.sync
        def _(s):
            ctx.wait(s, s_o)
            ctx.wait(s, s_v)
            for cc in range(8):
                ctx.inc(s.dma_start(out=ARM8[:, cc, :], in_=natv[b, cc]),
                        s_ld, 16)
        ldn = ctx.val(s_ld)

        @blk.gpsimd
        def _(g_):
            ctx.wait(g_, s_t)
            wv = w_od.rearrange("(c p) u -> c p u", p=128)
            for cc in range(8):
                dst = (ARW[:, cc * 1024:cc * 1024 + 1024] if cc < 6 else
                       ARW2[:, (cc - 6) * 1024:(cc - 6) * 1024 + 1024])
                ctx.inc(g_.dma_start(out=dst, in_=wv[cc]), s_g, 16)
        gvo = ctx.val(s_g)

        def woap(cc):
            if cc < 6:
                return ARW[:, cc * 1024:cc * 1024 + 1024]
            return ARW2[:, (cc - 6) * 1024:(cc - 6) * 1024 + 1024]

        for tt in range(16):
            gtile = b * 16 + tt

            @blk.sync
            def _(s, tt=tt):
                ctx.wait(s, s_v)
                ctx.inc(s.dma_start(
                    out=MISC[:],
                    in_=d_x1[b * T + tt * 128:b * T + tt * 128 + 128, :]),
                    s_ld, 16)
            ldt = ctx.val(s_ld)

            @blk.gpsimd
            def _(g_, tt=tt, gtile=gtile):
                ctx.wait(g_, s_v)
                for k in range(2):
                    col = 2 * gtile + k
                    ctx.inc(g_.indirect_dma_start(
                        (MISC2 if k == 0 else ARS)[:, 0:1024], None,
                        d_eo[:, :],
                        IndirectOffsetOnAxis(RROWS[:, col:col + 1], 0)),
                        s_g, 16)
            gvt = ctx.val(s_g)

            @blk.tensor
            def _(t, tt=tt):
                t.wait_ge(ctx.sem(s_ld), ldn)
                t.wait_ge(ctx.sem(s_g), gvo)
                ctx.wait(t, s_v)
                for half in range(2):
                    for cc in range(8):
                        t.matmul(P2[:, half * 512:half * 512 + 512],
                                 ARM8[:, cc, tt * 128:tt * 128 + 128],
                                 woap(cc)[:, half * 512:half * 512 + 512],
                                 start=(cc == 0), stop=(cc == 7),
                                 skip_group_check=True)
                ctx.inc(t.last, s_t, 1)
            tvo = ctx.val(s_t)

            @blk.vector
            def _(v, tt=tt, gtile=gtile):
                v.wait_ge(ctx.sem(s_t), tvo)
                v.wait_ge(ctx.sem(s_ld), ldt)
                v.wait_ge(ctx.sem(s_g), gvt)
                VC(v, v.tensor_add(MISC[:], MISC[:], P2[:]))
                VC(v, v.scalar_tensor_tensor(MISC[:], MISC2[:],
                                             RFW[:, 2 * gtile:2 * gtile + 1],
                                             MISC[:], ALU.mult, ALU.add))
                ctx.inc(v.scalar_tensor_tensor(
                    MISC[:], ARS[:, 0:1024],
                    RFW[:, 2 * gtile + 1:2 * gtile + 2], MISC[:],
                    ALU.mult, ALU.add), s_v, 1)
            vvo = ctx.val(s_v)

            @blk.sync
            def _(s, tt=tt):
                s.wait_ge(ctx.sem(s_v), vvo)
                ctx.inc(s.dma_start(
                    out=out_t[b * T + tt * 128:b * T + tt * 128 + 128, :],
                    in_=MISC[:]), s_o, 16)

    if "l0" in phases:
        for b in range(B_CORE):
            with nc.named_scope(f"l0_{b}"):
                emit_l0(b)
            barrier()
    if "ln1" in phases:
        for b in range(B_CORE):
            with nc.named_scope(f"ln1_{b}"):
                emit_ln1(b)
            barrier()
    if "route" in phases:
        with nc.named_scope("route"):
            emit_route()
        barrier()
    if "moe" in phases:
        with nc.named_scope("moe"):
            emit_moe()
        barrier()
    if "attn" in phases:
        for b in range(B_CORE):
            with nc.named_scope(f"attn_{b}"):
                emit_attn(b)
            barrier()
    if "fin" in phases:
        for b in range(B_CORE):
            with nc.named_scope(f"fin_{b}"):
                emit_fin(b)
            barrier()

    @blk.sync
    def _(s):
        ctx.wait(s, s_o)
        ctx.wait(s, s_so)
        ctx.wait(s, s_g)
        ctx.wait(s, s_ld)

    st.close()
    return nc



_CACHE = {}


def kernel(**inp):
    x = f32(inp["L0_attn_x"])
    cns = host_constants(inp)
    if "nc" not in _CACHE:
        _CACHE["nc"] = build()
    nc = _CACHE["nc"]
    in_maps = []
    base = {
        "w_qkv_aug": np.concatenate(
            [f32(inp['L1_attn_w_qkv']),
             rope_perm_cols(f32(inp['L1_attn_w_qkv'])[:, :1024]),
             rope_perm_cols(f32(inp['L1_attn_w_qkv'])[:, 1024:2048])], axis=1),
        "w_o": f32(inp['L1_attn_w_o']),
        "w_preT": np.ascontiguousarray(
            f32(inp['L1_ffn_ff_pre_act_weight']).transpose(0, 2, 1)),
        "w_gateT": np.ascontiguousarray(
            f32(inp['L1_ffn_gate_weight']).transpose(0, 2, 1)),
        "w_postT": np.ascontiguousarray(
            f32(inp['L1_ffn_ff_post_act_weight']).transpose(0, 2, 1)),
    }
    for nm in CONST_NAMES:
        shp, dt = CONST_SHAPES[nm]
        arr = cns[nm]
        if dt == BF16:
            arr = arr.astype(ml_dtypes.bfloat16)
        elif dt == U32:
            arr = arr.astype(np.uint32)
        else:
            arr = arr.astype(np.float32)
        assert list(arr.shape) == shp, (nm, arr.shape, shp)
        base["c_" + nm] = arr
    for core in range(8):
        m = dict(base)
        m["x_tm"] = np.ascontiguousarray(
            x[2 * core:2 * core + 2].reshape(NTOK, D))
        in_maps.append(m)
    res = run_bass_kernel_spmd(nc, in_maps, core_ids=list(range(8)))
    out = np.concatenate(
        [res.results[c]["out"].reshape(2, T, D) for c in range(8)], axis=0)
    return out


# version 06
# speedup vs baseline: 1.3494x; 1.0614x over previous
import numpy as np
import ml_dtypes
import concourse.bass as bass
import concourse.mybir as mybir
from concourse.bass import IndirectOffsetOnAxis
from concourse.bass_utils import run_bass_kernel_spmd
from contextlib import ExitStack

F32 = mybir.dt.float32
BF16 = mybir.dt.bfloat16
U32 = mybir.dt.uint32
AF = mybir.ActivationFunctionType
ALU = mybir.AluOpType
AX = mybir.AxisListType

B_CORE = 2
T = 2048
D = 1024
NTOK = B_CORE * T
NH, DH = 16, 64
C = 128
NCH = T // C
LCAP = 1536
NE = 8
EOROWS = NE * LCAP + 1
NIDX = NE * LCAP + 256
BLOCK_SIZES = [4, 8, 16, 32]
RANKS = [4, 8, 8, 8]
NUM_NODES = [8, 4, 2, 1]
EPSM = 1.1920929e-07
SPT = 3


def f32(x):
    return np.ascontiguousarray(np.asarray(x), dtype=np.float32)



def _comp_list():
    comps = []
    for n in range(16):
        sl = (4 * n, 4 * n + 4)
        comps.append(dict(wsrc='v', lvl=-1, wrows=sl, qrows=sl,
                          ustream='ktm', ucols=sl, rank=4, kind='leaf'))
    for lvl in range(4):
        bs = BLOCK_SIZES[lvl]
        for n in range(NUM_NODES[lvl]):
            top = (2 * bs * n, 2 * bs * n + bs)
            bot = (2 * bs * n + bs, 2 * bs * n + 2 * bs)
            comps.append(dict(wsrc='v', lvl=lvl, wrows=bot, qrows=bot,
                              ustream=('ktm' if lvl < 2 else f"G{lvl}k"),
                              ucols=top, rank=RANKS[lvl], kind='top'))
    for lvl in range(4):
        bs = BLOCK_SIZES[lvl]
        for n in range(NUM_NODES[lvl]):
            top = (2 * bs * n, 2 * bs * n + bs)
            bot = (2 * bs * n + bs, 2 * bs * n + 2 * bs)
            comps.append(dict(wsrc='k', lvl=lvl, wrows=top, qrows=top,
                              ustream=f"R{lvl}v", ucols=bot,
                              rank=RANKS[lvl], kind='bot'))
    assert len(comps) == 46
    return comps


COMPS = _comp_list()
NVC = sum(1 for c in COMPS if c['wsrc'] == 'v')
NKC = 46 - NVC
NWT_V = (NVC + SPT - 1) // SPT
NWT_K = (NKC + SPT - 1) // SPT
NWT = NWT_V + NWT_K
NQT = (46 + SPT - 1) // SPT


def comp_slot(i):
    if COMPS[i]['wsrc'] == 'v':
        j = i
        return j // SPT, j % SPT
    j = i - NVC
    return NWT_V + j // SPT, j % SPT


def q_slot(i):
    return comp_slot(i)



_CLS = {0: [], 1: [], 2: []}
for _i in range(46):
    _CLS[comp_slot(_i)[1]].append(_i)
SIG_ORDER = _CLS[0] + ["G"] + _CLS[1] + _CLS[2]
SIGCOL = {c: j * 128 for j, c in enumerate(SIG_ORDER)}


def build_packers(UL, VR, VL):
    PW = np.zeros((128, NWT * 128), np.float32)
    TQ = np.zeros((64, NQT * 128), np.float32)
    for i, c in enumerate(COMPS):
        lvl = c['lvl']
        if c['kind'] == 'leaf' or (c['kind'] == 'top' and lvl < 2):
            wmat = np.eye(c['wrows'][1] - c['wrows'][0], dtype=np.float32)
            qmat = wmat
        elif c['kind'] == 'top':
            wmat = VR[lvl]
            qmat = VR[lvl]
        else:
            wmat = UL[lvl]
            qmat = VL[lvl]
        r = wmat.shape[1]
        wt, ws = comp_slot(i)
        rbase = (64 if c['wsrc'] == 'k' else 0)
        rows = np.arange(c['wrows'][0], c['wrows'][1])
        PW[rbase + rows, wt * 128 + 32 * ws: wt * 128 + 32 * ws + r] = wmat
        qt_, qs = q_slot(i)
        qrows = np.arange(c['qrows'][0], c['qrows'][1])
        TQ[qrows, qt_ * 128 + 32 * qs: qt_ * 128 + 32 * qs + r] = qmat
    return PW, TQ


def blockdiag(mat, nn, bs, side):
    r = mat.shape[1]
    out = np.zeros((64, nn * r), np.float32)
    for n in range(nn):
        off = 2 * bs * n + (bs if side == 'bot' else 0)
        out[off:off + bs, n * r:(n + 1) * r] = mat
    return out


def build_mask7():
    ml = np.zeros((64, 64), np.float32)
    for n in range(16):
        ml[4 * n:4 * n + 4, 4 * n:4 * n + 4] = 1
    trT, blT = [], []
    for lvl in range(4):
        bs = BLOCK_SIZES[lvl]
        mt = np.zeros((64, 64), np.float32)
        mb = np.zeros((64, 64), np.float32)
        for n in range(NUM_NODES[lvl]):
            top = slice(2 * bs * n, 2 * bs * n + bs)
            bot = slice(2 * bs * n + bs, 2 * bs * n + 2 * bs)
            mt[top, bot] = 1
            mb[bot, top] = 1
        trT.append(mt.T.copy())
        blT.append(mb.T.copy())
    p0 = ml + trT[0] + trT[1]
    return np.concatenate([p0, trT[2], trT[3]] + blT, axis=1)


def build_rope_tables():
    inv_freq = 1.0 / (10000.0 ** (np.arange(0, DH, 2, dtype=np.float32) / DH))
    tt = np.arange(T, dtype=np.float32)
    freqs = np.outer(tt, inv_freq)
    emb = np.concatenate([freqs, freqs], -1)
    cos = np.cos(emb).T
    sin = np.sin(emb).T
    sgn = np.concatenate([-np.ones(32), np.ones(32)]).astype(np.float32)
    sinw = sin * sgn[:, None]
    return (np.concatenate([cos, cos], 0).astype(np.float32),
            np.concatenate([sinw, sinw], 0).astype(np.float32))


def rope_perm_cols(w):
    perm = np.concatenate([np.arange(32, 64), np.arange(0, 32)])
    out = np.empty_like(w)
    for h in range(w.shape[1] // 64):
        out[:, h * 64:(h + 1) * 64] = w[:, h * 64:(h + 1) * 64][:, perm]
    return out


CONST_SHAPES = dict(
    ID128=([128, 128], BF16), IDF32=([128, 128], F32),
    CAUSREP=([128, 1024], BF16),
    TRIU=([128, 128], BF16), UTRI64=([64, 64], BF16),
    IOTAE=([128, 8], F32), ONESB=([128, 1], BF16), ONESM=([128, 128], BF16),
    ONESRF=([1, 128], F32),
    PW=([128, NWT * 128], BF16), TQ=([64, NQT * 128], BF16),
    BD=([32, 1024], BF16), MASK7=([64, 448], F32),
    COSQ=([128, 2048], F32), SINQ=([128, 2048], F32),
    COSK=([128, 2048], F32), SINK=([128, 2048], F32),
    L0W=([128, 1536], BF16), L0WO=([64, 1024], BF16), WIN=([128, 512], BF16),
    FFNP=([128, 50], F32), WOUT=([128, 1024], BF16), BROW=([128, 1024], F32),
    RWT=([128, 64], BF16),
)
CONST_NAMES = list(CONST_SHAPES)


def host_constants(inp):
    UL = [f32(inp[f'L0_attn_UL{l}']) for l in range(4)]
    VR = [f32(inp[f'L0_attn_VR{l}']) for l in range(4)]
    UR = [f32(inp[f'L0_attn_UR{l}']) for l in range(4)]
    VL = [f32(inp[f'L0_attn_VL{l}']) for l in range(4)]
    PW, TQ = build_packers(UL, VR, VL)
    cns = {}
    cns['ID128'] = np.eye(128, dtype=np.float32)
    cns['IDF32'] = np.eye(128, dtype=np.float32)
    caus = np.triu(np.ones((C, C), np.float32))
    cns['CAUSREP'] = np.tile(caus, (1, 8))
    tri = np.zeros((128, 128), np.float32)
    tri[np.triu_indices(128, 1)] = 1
    cns['TRIU'] = tri
    u64 = np.zeros((64, 64), np.float32)
    u64[np.triu_indices(64, 1)] = 1
    cns['UTRI64'] = u64
    cns['IOTAE'] = np.tile(np.arange(8, dtype=np.float32), (128, 1))
    cns['ONESB'] = np.ones((128, 1), np.float32)
    cns['ONESM'] = np.ones((128, 128), np.float32)
    cns['ONESRF'] = np.ones((1, 128), np.float32)
    cns['PW'] = PW
    cns['TQ'] = TQ
    bd = np.zeros((32, 16 * 64), np.float32)
    for lvl in range(4):
        for wi, W in enumerate([VR[lvl], UR[lvl], UL[lvl], VL[lvl]]):
            side = 'bot' if wi < 2 else 'top'
            m = blockdiag(W, NUM_NODES[lvl], BLOCK_SIZES[lvl], side)
            kr = NUM_NODES[lvl] * RANKS[lvl]
            bd[0:kr, (lvl * 4 + wi) * 64:(lvl * 4 + wi) * 64 + 64] = m.T
    cns['BD'] = bd
    cns['MASK7'] = build_mask7()
    cq, sq = build_rope_tables()
    cns['COSQ'] = cq / 8.0
    cns['SINQ'] = sq / 8.0
    cns['COSK'] = cq
    cns['SINK'] = sq
    wqkv0 = np.concatenate([f32(inp['L0_attn_Wq']), f32(inp['L0_attn_Wk']),
                            f32(inp['L0_attn_Wv'])], axis=1)
    cns['L0W'] = np.ascontiguousarray(
        wqkv0.reshape(8, 128, 192).transpose(1, 0, 2).reshape(128, 8 * 192))
    cns['L0WO'] = f32(inp['L0_attn_Wo'])
    cns['WIN'] = np.ascontiguousarray(
        f32(inp['L0_ffn_Win']).reshape(8, 128, 64).transpose(1, 0, 2).reshape(128, 512))
    ffnp = np.concatenate([f32(inp['L0_ffn_W1']), f32(inp['L0_ffn_b1']),
                           f32(inp['L0_ffn_W2']),
                           f32(inp['L0_ffn_bin_'])[:, None],
                           f32(inp['L0_ffn_b2'])[:, None]], axis=1)
    cns['FFNP'] = np.concatenate([ffnp, ffnp], axis=0)
    wout = f32(inp['L0_ffn_Wout'])
    cns['WOUT'] = np.concatenate([wout, wout], axis=0)
    brow = np.zeros((128, 1024), np.float32)
    brow[0] = f32(inp['L0_ffn_bout'])
    cns['BROW'] = brow
    cns['RWT'] = np.ascontiguousarray(
        f32(inp['L1_ffn_router_weight']).T.reshape(8, 128, 8)
        .transpose(1, 0, 2).reshape(128, 64))
    return cns



class Ctx:
    def __init__(self, nc, st):
        self.nc = nc
        self.st = st
        self._sems = {}
        self.sb = {}

    def sem(self, name):
        if name not in self._sems:
            h = self.st.enter_context(self.nc.semaphore(name))
            self._sems[name] = [h, 0]
        return self._sems[name][0]

    def inc(self, inst, name, k=1):
        self.sem(name)
        s = self._sems[name]
        s[1] += k
        inst.then_inc(s[0], k)
        return s[1]

    def val(self, name):
        self.sem(name)
        return self._sems[name][1]

    def wait(self, eng, name, v=None):
        self.sem(name)
        s = self._sems[name]
        eng.wait_ge(s[0], s[1] if v is None else v)

    def alloc(self, name, shape, dtype):
        h = self.st.enter_context(self.nc.sbuf_tensor(name, shape, dtype))
        self.sb[name] = h
        return h


class _EngWrap:

    def __init__(self, eng):
        object.__setattr__(self, "_e", eng)
        object.__setattr__(self, "last", None)

    def __getattr__(self, k):
        f = getattr(self._e, k)
        if callable(k) or callable(f):
            def g(*a, **kw):
                r = f(*a, **kw)
                try:
                    if hasattr(r, "then_inc"):
                        object.__setattr__(self, "last", r)
                except Exception:
                    pass
                return r
            return g
        return f


class _BlkWrap:
    def __init__(self, blk):
        self._b = blk

    def _mk(self, name):
        def deco(f):
            return getattr(self._b, name)(lambda e: f(_EngWrap(e)))
        return deco

    @property
    def tensor(self):
        return self._mk("tensor")

    @property
    def vector(self):
        return self._mk("vector")

    @property
    def scalar(self):
        return self._mk("scalar")

    @property
    def gpsimd(self):
        return self._mk("gpsimd")

    @property
    def sync(self):
        return self._mk("sync")


TRF_IDX = dict(G2=0, G3=1, H2=2, H3=3, RT0=4, RT1=5, RT2=6, RT3=7,
               WT0=8, WT1=9, WT2=10, WT3=11)


def build(debug_outputs=(), phases=("l0", "ln1", "route", "moe", "attn", "fin"),
          l0_parts=("ln", "proj", "kt", "ffn", "scan", "ytr", "asm"), scan_chunks=NCH,
          scan_steps=(1, 2, 3, 4, 5, 6, 7), moe_experts=NE, attn_heads=NH):
    nc = bass.Bass("TRN2", target_bir_lowering=False)
    st = ExitStack()
    ctx = Ctx(nc, st)

    x_tm = nc.dram_tensor("x_tm", [NTOK, D], F32, kind="ExternalInput").ap()
    wqkv = nc.dram_tensor("w_qkv_aug", [D, 5120], F32, kind="ExternalInput").ap()
    w_od = nc.dram_tensor("w_o", [D, D], F32, kind="ExternalInput").ap()
    w_preT = nc.dram_tensor("w_preT", [NE, D, 4096], F32, kind="ExternalInput").ap()
    w_gateT = nc.dram_tensor("w_gateT", [NE, D, 4096], F32, kind="ExternalInput").ap()
    w_postT = nc.dram_tensor("w_postT", [NE, 4096, D], F32, kind="ExternalInput").ap()
    cdram = {}
    for nm in CONST_NAMES:
        shp, dt = CONST_SHAPES[nm]
        cdram[nm] = nc.dram_tensor("c_" + nm, shp, dt, kind="ExternalInput").ap()
    out_t = nc.dram_tensor("out", [NTOK, D], F32, kind="ExternalOutput").ap()

    dbg = {}

    def idram(name, shape, dt):
        kind = "ExternalOutput" if name in debug_outputs else "Internal"
        h = nc.dram_tensor(name, shape, dt, kind=kind).ap()
        dbg[name] = h
        return h

    d_x1 = idram("d_x1", [NTOK, D], F32)
    d_xn1tm = idram("d_xn1tm", [NTOK, D], BF16)
    d_xn1fm = idram("d_xn1fm", [B_CORE * D, T], BF16)
    d_eo = idram("d_eo", [EOROWS, D], F32)
    d_idx = idram("d_idx", [NIDX, 1], U32)
    d_natt = idram("d_natt", [B_CORE * NH * DH, T], BF16)
    d_qk = idram("d_qk", [B_CORE * 2 * D, T], BF16)
    d_v = idram("d_v", [NTOK, D], BF16)
    d_rt = idram("d_rt", [128, 512], F32)
    d_y0 = idram("d_y0", [NTOK, 64], F32)

    for nm in CONST_NAMES:
        shp, dt = CONST_SHAPES[nm]
        ctx.alloc(nm, shp, dt)
    TRFSB = ctx.alloc("TRFSB", [64, 768], BF16)
    BOUTBC = ctx.alloc("BOUTBC", [128, 1024], F32)
    ARM = ctx.alloc("ARM", [128, 16384], BF16)
    ARK = ctx.alloc("ARK", [128, 8192], BF16)
    ARL = ctx.alloc("ARL", [128, 4096], BF16)
    ARS = ctx.alloc("ARS", [128, 2048], F32)
    ARW = ctx.alloc("ARW", [128, 6144], BF16)
    ARW2 = ctx.alloc("ARW2", [128, 6144], BF16)
    ARW3 = ctx.alloc("ARW3", [128, 6144], BF16)
    ARW4 = ctx.alloc("ARW4", [128, 6144], BF16)
    MOEO = ctx.alloc("MOEO", [128, 1024], F32)
    MOEO2 = ctx.alloc("MOEO2", [128, 1024], F32)
    RWIDE = ctx.alloc("RWIDE", [128, 256], F32)
    RTOP = ctx.alloc("RTOP", [128, 64], F32)
    RANKW = ctx.alloc("RANKW", [128, 256], F32)
    OH1W = ctx.alloc("OH1W", [128, 256], F32)
    OH2W = ctx.alloc("OH2W", [128, 256], F32)
    OHB = ctx.alloc("OHB", [128, 512], BF16)
    SLW = ctx.alloc("SLW", [128, 512], F32)
    EW = ctx.alloc("EW", [128, 64], F32)
    SELW = ctx.alloc("SELW", [128, 64], F32)
    KEEPW = ctx.alloc("KEEPW", [128, 64], F32)
    W12 = ctx.alloc("W12", [128, 64], F32)
    RROWS = ctx.alloc("RROWS", [128, 64], U32)
    RFW = ctx.alloc("RFW", [128, 64], F32)
    BASEST = ctx.alloc("BASEST", [8, 64], BF16)
    BASESTF = ctx.alloc("BASESTF", [8, 64], F32)
    SST = ctx.alloc("SST", [64, 66], F32)
    SSB = ctx.alloc("SSB", [64, 66], BF16)
    MISC = ctx.alloc("MISC", [128, 1024], F32)
    MISC2 = ctx.alloc("MISC2", [128, 1024], F32)
    MISCB = ctx.alloc("MISCB", [128, 512], BF16)
    IDXS = ctx.alloc("IDXS", [128, 16], U32)
    SCL = ctx.alloc("SCL", [128, 32], F32)
    XNT = ctx.alloc("XNT", [128, 1024], BF16)

    P0 = st.enter_context(nc.psum_tensor("P0", [128, 1024], F32))
    P1 = st.enter_context(nc.psum_tensor("P1", [128, 1024], F32))
    P2 = st.enter_context(nc.psum_tensor("P2", [128, 1024], F32))
    P3 = st.enter_context(nc.psum_tensor("P3", [128, 512], F32))
    PT = st.enter_context(nc.psum_tensor("PT", [128, 1024], BF16))

    blk = _BlkWrap(st.enter_context(nc.Block()))
    for s in ("s_ld", "s_g", "s_t", "s_v", "s_a", "s_o", "s_so", "s_bar"):
        ctx.sem(s)
    s_ld, s_g, s_t, s_v, s_a, s_o, s_so = ("s_ld", "s_g", "s_t", "s_v",
                                           "s_a", "s_o", "s_so")

    ID = ctx.sb["ID128"]
    ONESB = ctx.sb["ONESB"]
    ONESRF = ctx.sb["ONESRF"]
    ARM8 = ARM.rearrange("p (c u) -> p c u", c=8)
    PT8 = PT.rearrange("p (c u) -> p c u", c=8)

    def trf(name):
        i = TRF_IDX[name]
        return TRFSB[0:64, i * 64:(i + 1) * 64]

    def VC(eng, inst):
        val = ctx.inc(inst, "s_vc", 1)
        eng.wait_ge(ctx.sem("s_vc"), val)
        return inst

    def barrier():
        dma_sems = [(s, ctx.val(s)) for s in (s_ld, s_o, s_so, s_g)]
        for name in ("tensor", "vector", "scalar", "gpsimd", "sync"):
            def mark(eng):
                for s, v in dma_sems:
                    eng.wait_ge(ctx.sem(s), v)
                ctx.inc(eng.nop(), "s_bar", 1)
            getattr(blk, name)(mark)
        tgt = ctx.val("s_bar")
        for name in ("tensor", "vector", "scalar", "gpsimd", "sync"):
            def wt(eng):
                eng.wait_ge(ctx.sem("s_bar"), tgt)
            getattr(blk, name)(wt)

    @blk.sync
    def _(s):
        for nm in CONST_NAMES:
            ctx.inc(s.dma_start(out=ctx.sb[nm][:], in_=cdram[nm][:]), s_ld, 16)

    @blk.vector
    def _(v):
        v.memset(MISC[:], 0.0)
        ctx.inc(v.memset(IDXS[:], 0), s_v, 1)

    @blk.scalar
    def _(s):
        ctx.wait(s, s_v)
        for i0 in range(0, NIDX, 128):
            ctx.inc(s.dma_start(out=d_idx[i0:i0 + 128, :], in_=IDXS[:, 0:1]),
                    s_so, 16)
        ctx.inc(s.dma_start(out=d_eo[EOROWS - 1:EOROWS, :],
                            in_=MISC[0:1, 0:1024]), s_so, 16)

    @blk.tensor
    def _(t):
        ctx.wait(t, s_ld)
        bd = ctx.sb["BD"]

        def bdap(lvl, which):
            off = (lvl * 4 + which) * 64
            kr = NUM_NODES[lvl] * RANKS[lvl]
            return bd[0:kr, off:off + 64]
        specs = [(bdap(2, 2), bdap(2, 2)), (bdap(3, 2), bdap(3, 2)),
                 (bdap(2, 0), bdap(2, 0)), (bdap(3, 0), bdap(3, 0))]
        for lvl in range(4):
            specs.append((bdap(lvl, 0), bdap(lvl, 1)))
        for lvl in range(4):
            specs.append((bdap(lvl, 2), bdap(lvl, 3)))
        for i, (lhsT, rhs) in enumerate(specs):
            t.matmul(P0[0:64, i * 64:(i + 1) * 64], lhsT, rhs,
                     start=(i % 8 == 0), stop=True, skip_group_check=True)
        t.matmul(P1[:, 0:512], ONESRF[0:1, :], ctx.sb["BROW"][0:1, 0:512],
                 start=True, stop=True, skip_group_check=True)
        t.matmul(P1[:, 512:1024], ONESRF[0:1, :], ctx.sb["BROW"][0:1, 512:1024],
                 start=True, stop=True, skip_group_check=True)
        ctx.inc(t.last, s_t, 1)

    @blk.vector
    def _(v):
        ctx.wait(v, s_t)
        v.tensor_copy(TRFSB[0:64, :], P0[0:64, 0:768])
        ctx.inc(v.tensor_copy(BOUTBC[:], P1[:]), s_v, 1)

    barrier()

    def emit_ln_tile(dram_src, row0, fm_sink, after_tm=None):
        @blk.sync
        def _(s):
            ctx.wait(s, s_v)
            ctx.wait(s, s_a)
            ctx.inc(s.dma_start(out=MISC[:], in_=dram_src[row0:row0 + 128, :]),
                    s_ld, 16)
        ldv = ctx.val(s_ld)

        @blk.scalar
        def _(sc):
            sc.wait_ge(ctx.sem(s_ld), ldv)
            ctx.wait(sc, s_v)
            ctx.inc(sc.activation(MISC2[:], MISC[:], AF.Square,
                                  accum_out=SCL[:, 1:2]), s_a, 1)
        av = ctx.val(s_a)

        @blk.vector
        def _(v):
            v.wait_ge(ctx.sem(s_ld), ldv)
            VC(v, v.reduce_sum(SCL[:, 0:1], MISC[:], axis=AX.X))
            v.wait_ge(ctx.sem(s_a), av)
            VC(v, v.tensor_scalar_mul(SCL[:, 2:3], SCL[:, 0:1], 1.0 / D))
            VC(v, v.tensor_scalar_mul(SCL[:, 3:4], SCL[:, 1:2], 1.0 / D))
            VC(v, v.tensor_tensor(SCL[:, 4:5], SCL[:, 2:3], SCL[:, 2:3], ALU.mult))
            VC(v, v.tensor_sub(SCL[:, 5:6], SCL[:, 3:4], SCL[:, 4:5]))
            ctx.inc(v.tensor_scalar_add(SCL[:, 6:7], SCL[:, 5:6], 1e-5), s_v, 1)
        vv = ctx.val(s_v)

        @blk.scalar
        def _(sc):
            sc.wait_ge(ctx.sem(s_v), vv)
            ctx.inc(sc.activation(SCL[:, 7:8], SCL[:, 6:7], AF.Sqrt), s_a, 1)
        av2 = ctx.val(s_a)

        @blk.vector
        def _(v):
            v.wait_ge(ctx.sem(s_a), av2)
            VC(v, v.reciprocal(SCL[:, 8:9], SCL[:, 7:8]))
            ctx.wait(v, s_so)
            ctx.inc(v.tensor_scalar(XNT[:], MISC[:], SCL[:, 2:3], SCL[:, 8:9],
                                    ALU.subtract, ALU.mult), s_v, 1)
        vv2 = ctx.val(s_v)
        if after_tm is not None:
            after_tm(vv2)

        @blk.tensor
        def _(t):
            t.wait_ge(ctx.sem(s_v), vv2)
            ctx.wait(t, s_v)
            for j in range(8):
                t.matmul(PT[:, j * 128:(j + 1) * 128],
                         XNT[:, j * 128:(j + 1) * 128], ID[:],
                         start=True, stop=True, is_transpose=True,
                         skip_group_check=True)
            ctx.inc(t.last, s_t, 1)
        tv = ctx.val(s_t)

        @blk.vector
        def _(v):
            v.wait_ge(ctx.sem(s_t), tv)
            fm_sink(v)
            ctx.inc(v.last, s_v, 1)

    def emit_scan_chunk(b, ci):
        SS = scan_steps
        PWsb = ctx.sb["PW"]
        TQsb = ctx.sb["TQ"]
        CAUS = ctx.sb["CAUSREP"]
        M7 = ctx.sb["MASK7"]
        qf = ARK[0:64, ci * 128:ci * 128 + 128]
        kf = ARK[64:128, ci * 128:ci * 128 + 128]
        vf = ARK[0:64, 2048 + ci * 128:2048 + ci * 128 + 128]
        kf0 = ARL[0:64, 2048 + ci * 128:2048 + ci * 128 + 128]
        ktc = ARK[:, 4096 + ci * 64:4096 + ci * 64 + 64]
        vtc = ARK[:, 5120 + ci * 64:5120 + ci * 64 + 64]
        ytm = ARK[:, 6144 + ci * 64:6144 + ci * 64 + 64]
        SIG = ARM[:, 0:6016]
        WP = ARM[:, 6144:8192]
        QP = ARM[:, 8192:10240]
        US = ARM[:, 10240:10624]
        t1sb = MISCB[0:64, 0:64]
        t2sb = MISCB[0:64, 64:128]
        SAND = MISCB[0:64, 128:512]

        def ustream_ap(name, c0, c1):
            if name == 'ktm':
                return ktc[:, c0:c1]
            i = dict(G2k=0, G3k=1, R0v=2, R1v=3, R2v=4, R3v=5)[name]
            return US[:, i * 64 + c0:i * 64 + c1]

        if 1 not in SS:
            return

        @blk.tensor
        def _(t):
            ctx.wait(t, s_v)
            for j in range(8):
                t.matmul(P0[:, (j % 8) * 128:(j % 8) * 128 + 128],
                         PWsb[0:64, j * 128:j * 128 + 128], vf,
                         start=(j % 4 == 0), stop=True, skip_group_check=True)
            for j in range(8, 11):
                t.matmul(P1[:, (j - 8) * 128:(j - 8) * 128 + 128],
                         PWsb[0:64, j * 128:j * 128 + 128], vf,
                         start=(j == 8), stop=True, skip_group_check=True)
            for j in range(11, 15):
                t.matmul(P1[:, 512 + (j - 11) * 128:512 + (j - 11) * 128 + 128],
                         PWsb[64:128, j * 128:j * 128 + 128], kf,
                         start=(j == 11), stop=True, skip_group_check=True)
            t.matmul(P2[:, 0:128], PWsb[64:128, 15 * 128:15 * 128 + 128], kf,
                     start=True, stop=True, skip_group_check=True)
            for j in range(4):
                t.matmul(P2[:, 512 + j * 128:512 + j * 128 + 128],
                         TQsb[0:64, j * 128:j * 128 + 128], qf,
                         start=(j == 0), stop=True, skip_group_check=True)
            ctx.inc(t.last, s_t, 1)
        tv1 = ctx.val(s_t)

        @blk.vector
        def _(v):
            v.wait_ge(ctx.sem(s_t), tv1)
            v.tensor_copy(WP[:, 0:1024], P0[:])
            v.tensor_copy(WP[:, 1024:1408], P1[:, 0:384])
            v.tensor_copy(WP[:, 1408:1920], P1[:, 512:1024])
            v.tensor_copy(WP[:, 1920:2048], P2[:, 0:128])
            v.tensor_copy(QP[:, 0:512], P2[:, 512:1024])
            ctx.inc(v.last, s_v, 1)
        vv1 = ctx.val(s_v)

        if 2 not in SS:
            return

        @blk.tensor
        def _(t):
            t.wait_ge(ctx.sem(s_v), vv1)
            for j in range(4, 12):
                t.matmul(P0[:, (j - 4) * 128:(j - 4) * 128 + 128],
                         TQsb[0:64, j * 128:j * 128 + 128], qf,
                         start=((j - 4) % 4 == 0), stop=True,
                         skip_group_check=True)
            for j in range(12, 16):
                t.matmul(P1[:, (j - 12) * 128:(j - 12) * 128 + 128],
                         TQsb[0:64, j * 128:j * 128 + 128], qf,
                         start=(j == 12), stop=True, skip_group_check=True)
            us_specs = [("G2", kf0), ("G3", kf0), ("RT0", vf), ("RT1", vf),
                        ("RT2", vf), ("RT3", vf)]
            for i, (nm, src2) in enumerate(us_specs):
                t.matmul(P3[:, 64 + i * 64:64 + i * 64 + 64], src2, trf(nm),
                         start=(i == 0), stop=True, skip_group_check=True)
            t.matmul(P3[0:64, 0:64], ktc, vtc, start=False, stop=True,
                     skip_group_check=True)
            t.matmul(P3[0:64, 448:512], vtc, ktc, start=False, stop=True,
                     skip_group_check=True)
            ctx.inc(t.last, s_t, 1)
        tv2 = ctx.val(s_t)

        @blk.vector
        def _(v):
            v.wait_ge(ctx.sem(s_t), tv2)
            v.tensor_copy(QP[:, 512:1536], P0[:])
            v.tensor_copy(QP[:, 1536:2048], P1[:, 0:512])
            v.tensor_copy(US[:], P3[:, 64:448])
            v.tensor_copy(t1sb, P3[0:64, 0:64])
            v.tensor_copy(t2sb, P3[0:64, 448:512])
            ctx.inc(v.last, s_v, 1)
        vv2 = ctx.val(s_v)

        def score_mm(t, i, ps, col, first):
            wt, ws = comp_slot(i)
            qt_, qs = q_slot(i)
            assert ws == qs
            if wt < 8:
                wcol = wt * 128
            elif wt < 11:
                wcol = 1024 + (wt - 8) * 128
            elif wt < 15:
                wcol = 1408 + (wt - 11) * 128
            else:
                wcol = 1920
            qcol = qt_ * 128
            t.matmul(ps[:, col:col + 128],
                     WP[32 * ws:32 * ws + 32, wcol:wcol + 128],
                     QP[32 * qs:32 * qs + 32, qcol:qcol + 128],
                     start=first, stop=True, skip_group_check=True)

        if 3 not in SS:
            return
        @blk.tensor
        def _(t):
            t.wait_ge(ctx.sem(s_v), vv2)
            for n, i in enumerate(_CLS[0]):
                ps = (P0, P1)[n // 8]
                col = (n % 8) * 128
                score_mm(t, i, ps, col, (n % 4 == 0))
            t.matmul(P2[:, 0:128], kf0, qf, start=True, stop=True,
                     skip_group_check=True)
            ctx.inc(t.last, s_t, 1)
        tv3 = ctx.val(s_t)

        @blk.vector
        def _(v):
            v.wait_ge(ctx.sem(s_t), tv3)
            v.tensor_tensor(SIG[:, 0:1024], P0[:], CAUS[:], ALU.mult)
            v.tensor_tensor(SIG[:, 1024:2048], P1[:], CAUS[:], ALU.mult)
            v.tensor_tensor(SIG[:, 2048:2176], P2[:, 0:128], CAUS[:, 0:128],
                            ALU.mult)
            ctx.inc(v.last, s_v, 1)
        vv3 = ctx.val(s_v)

        if 4 not in SS:
            return
        @blk.tensor
        def _(t):
            t.wait_ge(ctx.sem(s_v), vv3)
            for n, i in enumerate(_CLS[1]):
                ps = (P0, P1)[n // 8]
                col = (n % 8) * 128
                score_mm(t, i, ps, col, (n % 4 == 0))
            ctx.inc(t.last, s_t, 1)
        tv4a = ctx.val(s_t)

        @blk.vector
        def _(v):
            v.wait_ge(ctx.sem(s_t), tv4a)
            n1 = len(_CLS[1])
            v.tensor_tensor(SIG[:, 2176:3200], P0[:], CAUS[:], ALU.mult)
            w = (n1 - 8) * 128
            v.tensor_tensor(SIG[:, 3200:3200 + w], P1[:, 0:w], CAUS[:, 0:w],
                            ALU.mult)
            ctx.inc(v.last, s_v, 1)
        vv4a = ctx.val(s_v)

        @blk.tensor
        def _(t):
            t.wait_ge(ctx.sem(s_v), vv4a)
            for n, i in enumerate(_CLS[2]):
                ps = (P0, P1)[n // 8]
                col = (n % 8) * 128
                score_mm(t, i, ps, col, (n % 4 == 0))
            ctx.inc(t.last, s_t, 1)
        tv4 = ctx.val(s_t)

        @blk.vector
        def _(v):
            v.wait_ge(ctx.sem(s_t), tv4)
            base2 = (17 + len(_CLS[1])) * 128
            n2 = len(_CLS[2])
            v.tensor_tensor(SIG[:, base2:base2 + 1024], P0[:], CAUS[:], ALU.mult)
            w = (n2 - 8) * 128
            v.tensor_tensor(SIG[:, base2 + 1024:base2 + 1024 + w], P1[:, 0:w],
                            CAUS[:, 0:w], ALU.mult)
            ctx.inc(v.last, s_v, 1)
        vv4 = ctx.val(s_v)

        if 5 not in SS:
            return
        @blk.tensor
        def _(t):
            t.wait_ge(ctx.sem(s_v), vv4)
            t.matmul(P3[:, 0:64], qf, SSB[0:64, 0:64], start=True, stop=False,
                     skip_group_check=True)
            for i in range(46):
                c0, c1 = COMPS[i]['ucols']
                sc0 = SIGCOL[i]
                t.matmul(P3[:, c0:c1], SIG[:, sc0:sc0 + 128],
                         ustream_ap(COMPS[i]['ustream'], c0, c1),
                         start=False, stop=True, skip_group_check=True)
            gc = SIGCOL["G"]
            t.matmul(P3[:, 64:65], SIG[:, gc:gc + 128], ONESB[:],
                     start=False, stop=False, skip_group_check=True)
            t.matmul(P3[:, 64:65], qf, SSB[0:64, 64:65], start=False, stop=True,
                     skip_group_check=True)
            t.matmul(P3[0:64, 65:66], ktc, ONESB[:], start=False, stop=True,
                     skip_group_check=True)
            t.matmul(P3[0:64, 128:192], t1sb, trf("G2"), start=False, stop=True,
                     skip_group_check=True)
            t.matmul(P3[0:64, 192:256], t1sb, trf("G3"), start=False, stop=True,
                     skip_group_check=True)
            for lvl in range(4):
                t.matmul(P3[0:64, 256 + lvl * 64:256 + lvl * 64 + 64], t2sb,
                         trf(f"RT{lvl}"), start=False, stop=True,
                         skip_group_check=True)
            ctx.inc(t.last, s_t, 1)
        tv5 = ctx.val(s_t)

        @blk.vector
        def _(v):
            v.wait_ge(ctx.sem(s_t), tv5)
            VC(v, v.tensor_scalar_max(SCL[:, 10:11], P3[:, 64:65], 1e-6))
            VC(v, v.reciprocal(SCL[:, 11:12], SCL[:, 10:11]))
            v.tensor_scalar(ytm[:], P3[:, 0:64], SCL[:, 11:12], None, ALU.mult)
            v.tensor_copy(SAND, P3[0:64, 128:512])
            VC(v, v.tensor_add(SST[0:64, 64:65], SST[0:64, 64:65],
                               P3[0:64, 65:66]))
            ctx.inc(v.last, s_v, 1)
        vv5 = ctx.val(s_v)

        if 6 not in SS:
            return
        @blk.tensor
        def _(t):
            t.wait_ge(ctx.sem(s_v), vv5)
            t.matmul(P0[0:64, 0:64], trf("H2"), SAND[0:64, 0:64], start=True,
                     stop=True, skip_group_check=True)
            t.matmul(P0[0:64, 64:128], trf("H3"), SAND[0:64, 64:128],
                     start=False, stop=True, skip_group_check=True)
            for lvl in range(4):
                t.matmul(P0[0:64, 128 + lvl * 64:128 + lvl * 64 + 64],
                         trf(f"WT{lvl}"), SAND[0:64, 128 + lvl * 64:128 + lvl * 64 + 64],
                         start=False, stop=True, skip_group_check=True)
            ctx.inc(t.last, s_t, 1)
        tv6 = ctx.val(s_t)

        @blk.vector
        def _(v):
            v.wait_ge(ctx.sem(s_t), tv6)
            v.tensor_tensor(ARS[0:64, 0:64], t2sb, M7[:, 0:64], ALU.mult)
            VC(v, v.tensor_tensor(ARS[0:64, 64:448], P0[0:64, 0:384],
                                  M7[:, 64:448], ALU.mult))
            for p in range(7):
                VC(v, v.tensor_add(SST[0:64, 0:64], SST[0:64, 0:64],
                                   ARS[0:64, p * 64:p * 64 + 64]))
            v.tensor_copy(SSB[0:64, 0:64], SST[0:64, 0:64])
            v.tensor_copy(SSB[0:64, 64:65], SST[0:64, 64:65])
            ctx.inc(v.last, s_v, 1)

    def emit_l0(b):
      if "ln" in l0_parts:
        for tt in range(16):
            def fm_sink(v, tt=tt):
                v.tensor_copy(ARM8[:, :, tt * 128:tt * 128 + 128], PT8[:, :, :])
            emit_ln_tile(x_tm, b * T + tt * 128, fm_sink)

      if "proj" in l0_parts:
        for tt in range(4):
            @blk.tensor
            def _(t, tt=tt):
                ctx.wait(t, s_v)
                for cc in range(8):
                    t.matmul(P0[:, 0:512],
                             ctx.sb["L0W"][:, cc * 192:cc * 192 + 128],
                             ARM8[:, cc, tt * 512:tt * 512 + 512],
                             start=(cc == 0), stop=(cc == 7),
                             skip_group_check=True)
                for cc in range(8):
                    t.matmul(P3[0:64, 0:512],
                             ctx.sb["L0W"][:, cc * 192 + 128:cc * 192 + 192],
                             ARM8[:, cc, tt * 512:tt * 512 + 512],
                             start=(cc == 0), stop=(cc == 7),
                             skip_group_check=True)
                ctx.inc(t.last, s_t, 1)
            tv0 = ctx.val(s_t)

            @blk.vector
            def _(v, tt=tt):
                v.wait_ge(ctx.sem(s_t), tv0)
                ctx.wait(v, s_a)
                ctx.inc(v.tensor_scalar_min(MISC[:, 0:512], P0[:, 0:512], 0.0),
                        s_v, 1)
            vv0 = ctx.val(s_v)

            @blk.scalar
            def _(sc, tt=tt):
                sc.wait_ge(ctx.sem(s_v), vv0)
                ctx.inc(sc.activation(MISC2[:, 0:512], MISC[:, 0:512], AF.Exp),
                        s_a, 1)
            av0 = ctx.val(s_a)

            @blk.vector
            def _(v, tt=tt):
                v.wait_ge(ctx.sem(s_a), av0)
                VC(v, v.tensor_scalar_add(MISC[:, 0:512], P0[:, 0:512], 1.0))
                VC(v, v.tensor_tensor(ARK[:, tt * 512:tt * 512 + 512],
                                      MISC[:, 0:512], MISC2[:, 0:512], ALU.max))
                v.tensor_copy(ARL[0:64, 2048 + tt * 512:2048 + tt * 512 + 512],
                              ARK[64:128, tt * 512:tt * 512 + 512])
                ctx.inc(v.tensor_copy(
                    ARK[0:64, 2048 + tt * 512:2048 + tt * 512 + 512],
                    P3[0:64, 0:512]), s_v, 1)

      if "kt" in l0_parts:
        for half in range(2):
            @blk.tensor
            def _(t, half=half):
                ctx.wait(t, s_v)
                for j in range(8):
                    ci = half * 8 + j
                    t.matmul(PT[:, j * 128:j * 128 + 64],
                             ARL[0:64, 2048 + ci * 128:2048 + ci * 128 + 128],
                             ID[0:64, 0:64],
                             start=True, stop=True, is_transpose=True,
                             skip_group_check=True)
                    t.matmul(PT[:, j * 128 + 64:j * 128 + 128],
                             ARK[0:64, 2048 + ci * 128:2048 + ci * 128 + 128],
                             ID[0:64, 0:64], start=True, stop=True,
                             is_transpose=True, skip_group_check=True)
                ctx.inc(t.last, s_t, 1)
            tvh = ctx.val(s_t)

            @blk.vector
            def _(v, half=half):
                v.wait_ge(ctx.sem(s_t), tvh)
                v.tensor_copy(
                    ARK[:, 4096 + half * 512:4096 + half * 512 + 512]
                    .rearrange("p (ci u) -> p ci u", ci=8),
                    PT8[:, :, 0:64])
                ctx.inc(v.tensor_copy(
                    ARK[:, 5120 + half * 512:5120 + half * 512 + 512]
                    .rearrange("p (ci u) -> p ci u", ci=8),
                    PT8[:, :, 64:128]), s_v, 1)

      if "ffn" in l0_parts:
        FFNP = ctx.sb["FFNP"]
        for hb in range(2):
            for t2 in range(2):
                @blk.tensor
                def _(t, hb=hb, t2=t2):
                    ctx.wait(t, s_v)
                    for cc in range(8):
                        t.matmul(P1[hb * 64:hb * 64 + 64,
                                    t2 * 512:t2 * 512 + 512],
                                 ctx.sb["WIN"][:, cc * 64:cc * 64 + 64],
                                 ARM8[:, cc, hb * 1024 + t2 * 512:
                                      hb * 1024 + t2 * 512 + 512],
                                 start=(cc == 0), stop=(cc == 7),
                                 skip_group_check=True)
                    ctx.inc(t.last, s_t, 1)
        tv2 = ctx.val(s_t)

        @blk.vector
        def _(v):
            v.wait_ge(ctx.sem(s_t), tv2)
            ctx.inc(v.tensor_scalar_add(ARS[:, 0:1024], P1[:], FFNP[:, 48:49]),
                    s_v, 1)
        vv2 = ctx.val(s_v)

        for i in range(16):
            @blk.vector
            def _(v, i=i):
                if i == 0:
                    v.wait_ge(ctx.sem(s_v), vv2)
                ctx.wait(v, s_a)
                ctx.inc(v.tensor_scalar(MISC[:], ARS[:, 0:1024],
                                        FFNP[:, i:i + 1], FFNP[:, 16 + i:17 + i],
                                        ALU.mult, ALU.add), s_v, 1)
            vh = ctx.val(s_v)

            @blk.scalar
            def _(sc, i=i):
                sc.wait_ge(ctx.sem(s_v), vh)
                ctx.inc(sc.activation(MISC2[:], MISC[:], AF.Relu), s_a, 1)
            ah = ctx.val(s_a)

            @blk.vector
            def _(v, i=i):
                v.wait_ge(ctx.sem(s_a), ah)
                if i == 0:
                    gi = v.tensor_scalar(ARS[:, 1024:2048], MISC2[:],
                                         FFNP[:, 32 + i:33 + i],
                                         FFNP[:, 49:50], ALU.mult, ALU.add)
                else:
                    gi = v.scalar_tensor_tensor(
                        ARS[:, 1024:2048], MISC2[:], FFNP[:, 32 + i:33 + i],
                        ARS[:, 1024:2048], ALU.mult, ALU.add)
                gv = ctx.inc(gi, s_v, 1)
                v.wait_ge(ctx.sem(s_v), gv)

      if "ffn" in l0_parts:
        @blk.vector
        def _(v):
            ctx.wait(v, s_v)
            ctx.inc(v.tensor_copy(XNT[:], ARS[:, 1024:2048]), s_v, 1)

      if "scan" in l0_parts:
        @blk.vector
        def _(v):
            v.memset(SST[:], 0.0)
            ctx.inc(v.memset(SSB[:], 0.0), s_v, 1)
        for ci in range(scan_chunks):
            emit_scan_chunk(b, ci)

      if "scan" in l0_parts and "d_sig" in debug_outputs and b == 0:
            @blk.vector
            def _(v):
                ctx.wait(v, s_v)
                ctx.inc(v.last, s_v, 1) if False else None
                ctx.inc(v.tensor_copy(MISCB[:, 0:512], ARM[:, 0:512]), s_v, 1)
            sigv = ctx.val(s_v)

            @blk.sync
            def _(s):
                s.wait_ge(ctx.sem(s_v), sigv)
                for sl3 in range(3):
                    ctx.inc(s.dma_start(
                        out=d_natt[sl3 * 128:sl3 * 128 + 128, 0:2048],
                        in_=ARM[:, sl3 * 2048:sl3 * 2048 + 2048]), s_o, 16)
                ctx.inc(s.dma_start(out=d_natt[384:512, 0:2048],
                                    in_=ARM[:, 6144:8192]), s_o, 16)
                ctx.inc(s.dma_start(out=d_natt[512:640, 0:2048],
                                    in_=ARM[:, 8192:10240]), s_o, 16)
                ctx.inc(s.dma_start(out=d_natt[640:768, 0:2048],
                                    in_=ARK[:, 0:2048]), s_o, 16)
                ctx.inc(s.dma_start(out=d_natt[768:896, 0:2048],
                                    in_=ARK[:, 2048:4096]), s_o, 16)
                ctx.inc(s.dma_start(out=d_natt[896:1024, 0:2048],
                                    in_=ARK[:, 4096:6144]), s_o, 16)
                ctx.inc(s.dma_start(out=d_natt[1024:1152, 0:384],
                                    in_=ARM[:, 10240:10624]), s_o, 16)

      if "scan" in l0_parts and "d_y0" in debug_outputs:
            @blk.vector
            def _(v):
                ctx.wait(v, s_v)
                ctx.inc(v.tensor_copy(MISC[:, 0:scan_chunks * 64],
                                      ARK[:, 6144:6144 + scan_chunks * 64]),
                        s_v, 1)
            dvv = ctx.val(s_v)

            @blk.sync
            def _(s):
                s.wait_ge(ctx.sem(s_v), dvv)
                dy = d_y0.rearrange("(a p) u -> a p u", p=128)
                for tt in range(scan_chunks):
                    ctx.inc(s.dma_start(
                        out=dy[b * 16 + tt],
                        in_=MISC[:, tt * 64:tt * 64 + 64]), s_o, 16)

      if "ytr" in l0_parts:
        for half in range(2):
            @blk.tensor
            def _(t, half=half):
                ctx.wait(t, s_v)
                for j in range(8):
                    ci = half * 8 + j
                    t.matmul(PT[0:64, j * 128:j * 128 + 128],
                             ARK[:, 6144 + ci * 64:6144 + ci * 64 + 64], ID[:],
                             start=True, stop=True, is_transpose=True,
                             skip_group_check=True)
                ctx.inc(t.last, s_t, 1)
            tvy = ctx.val(s_t)

            @blk.vector
            def _(v, half=half):
                v.wait_ge(ctx.sem(s_t), tvy)
                ctx.inc(v.tensor_copy(
                    ARL[0:64, half * 1024:half * 1024 + 1024], PT[0:64, :]),
                    s_v, 1)

      if "asm" in l0_parts:
        for tt in range(16):
            @blk.sync
            def _(s, tt=tt):
                ctx.wait(s, s_v)
                ctx.inc(s.dma_start(
                    out=MISC2[:],
                    in_=x_tm[b * T + tt * 128:b * T + tt * 128 + 128, :]),
                    s_ld, 16)
            ldv = ctx.val(s_ld)

            @blk.tensor
            def _(t, tt=tt):
                ctx.wait(t, s_v)
                hb = tt // 8
                for half in range(2):
                    t.matmul(P1[:, half * 512:half * 512 + 512],
                             ARL[0:64, tt * 128:tt * 128 + 128],
                             ctx.sb["L0WO"][0:64, half * 512:half * 512 + 512],
                             start=True, stop=True, skip_group_check=True)
                gsl = XNT
                for half in range(2):
                    t.matmul(P2[:, half * 512:half * 512 + 512],
                             gsl[hb * 64:hb * 64 + 64,
                                 (tt % 8) * 128:(tt % 8) * 128 + 128],
                             ctx.sb["WOUT"][hb * 64:hb * 64 + 64,
                                            half * 512:half * 512 + 512],
                             start=True, stop=True, skip_group_check=True)
                ctx.inc(t.last, s_t, 1)
            tvA = ctx.val(s_t)

            @blk.vector
            def _(v, tt=tt):
                v.wait_ge(ctx.sem(s_t), tvA)
                v.wait_ge(ctx.sem(s_ld), ldv)
                ctx.wait(v, s_o)
                VC(v, v.tensor_add(MISC[:], MISC2[:], P1[:]))
                VC(v, v.tensor_add(MISC[:], MISC[:], P2[:]))
                ctx.inc(v.tensor_add(MISC[:], MISC[:], BOUTBC[:]), s_v, 1)
            vvA = ctx.val(s_v)

            @blk.sync
            def _(s, tt=tt):
                s.wait_ge(ctx.sem(s_v), vvA)
                ctx.inc(s.dma_start(
                    out=d_x1[b * T + tt * 128:b * T + tt * 128 + 128, :],
                    in_=MISC[:]), s_o, 16)

    def emit_ln1(b):
        for tt in range(16):
            gtile = b * 16 + tt

            def after_tm(vv2, tt=tt, gtile=gtile):
                @blk.scalar
                def _(sc):
                    sc.wait_ge(ctx.sem(s_v), vv2)
                    ctx.inc(sc.dma_start(
                        out=d_xn1tm[gtile * 128:gtile * 128 + 128, :],
                        in_=XNT[:]), s_so, 16)

            def fm_sink(v, tt=tt, gtile=gtile):
                v.tensor_copy(ARM8[:, :, tt * 128:tt * 128 + 128], PT8[:, :, :])
            emit_ln_tile(d_x1, b * T + tt * 128, fm_sink, after_tm=after_tm)
            vfm = ctx.val(s_v)

            @blk.tensor
            def _(t, tt=tt, gtile=gtile):
                t.wait_ge(ctx.sem(s_v), vfm)
                for cc in range(8):
                    t.matmul(P3[:, 0:8],
                             ARM8[:, cc, tt * 128:tt * 128 + 128],
                             ctx.sb["RWT"][:, cc * 8:cc * 8 + 8],
                             start=(cc == 0), stop=(cc == 7),
                             skip_group_check=True)
                ctx.inc(t.last, s_t, 1)
            tvr = ctx.val(s_t)

            @blk.vector
            def _(v, tt=tt, gtile=gtile):
                v.wait_ge(ctx.sem(s_t), tvr)
                ctx.inc(v.tensor_copy(RWIDE[:, gtile * 8:gtile * 8 + 8],
                                      P3[:, 0:8]), s_v, 1)

        @blk.sync
        def _(s):
            ctx.wait(s, s_v)
            fmv = d_xn1fm.rearrange("(bb c p) u -> bb c p u", bb=B_CORE, p=128)
            for cc in range(8):
                ctx.inc(s.dma_start(out=fmv[b, cc],
                                    in_=ARM8[:, cc, :]), s_o, 16)

    def emit_route():
        TRIU = ctx.sb["TRIU"]
        ONESM = ctx.sb["ONESM"]
        IOTAE = ctx.sb["IOTAE"]
        for g in range(32):
            @blk.vector
            def _(v, g=g):
                if g == 0:
                    ctx.wait(v, s_v)
                lg = RWIDE[:, g * 8:g * 8 + 8]
                VC(v, v.max(MISC[:, 0:8], lg))
                v.tensor_copy(RTOP[:, g:g + 1], MISC[:, 0:1])
                v.tensor_copy(RTOP[:, 32 + g:32 + g + 1], MISC[:, 1:2])
                VC(v, v.max_index(IDXS[:, 0:8], lg, MISC[:, 0:8]))
                ctx.inc(v.tensor_copy(RANKW[:, g * 8:g * 8 + 8],
                                      IDXS[:, 0:8]), s_v, 1)

        @blk.vector
        def _(v):
            ctx.wait(v, s_v)
            VC(v, v.tensor_single_scalar(OH1W[:], RANKW[:], 0.0, ALU.is_equal))
            VC(v, v.tensor_single_scalar(OH2W[:], RANKW[:], 1.0, ALU.is_equal))
            ctx.inc(v.tensor_sub(W12[:, 0:32], RTOP[:, 0:32], RTOP[:, 32:64]),
                    s_v, 1)
        vv = ctx.val(s_v)

        @blk.scalar
        def _(sc):
            sc.wait_ge(ctx.sem(s_v), vv)
            ctx.inc(sc.activation(W12[:, 0:32], W12[:, 0:32], AF.Sigmoid),
                    s_a, 1)
        av = ctx.val(s_a)

        @blk.vector
        def _(v):
            v.wait_ge(ctx.sem(s_a), av)
            VC(v, v.tensor_scalar(W12[:, 32:64], W12[:, 0:32], -1.0, 1.0,
                                  ALU.mult, ALU.add))
            for g in range(32):
                VC(v, v.tensor_tensor(MISC[:, 0:8], OH1W[:, g * 8:g * 8 + 8],
                                      IOTAE[:], ALU.mult))
                VC(v, v.reduce_sum(EW[:, 2 * g:2 * g + 1], MISC[:, 0:8],
                                   axis=AX.X))
                VC(v, v.tensor_tensor(MISC[:, 0:8], OH2W[:, g * 8:g * 8 + 8],
                                      IOTAE[:], ALU.mult))
                VC(v, v.reduce_sum(EW[:, 2 * g + 1:2 * g + 2], MISC[:, 0:8],
                                   axis=AX.X))
            v.tensor_copy(OHB[:, 0:256], OH1W[:])
            ctx.inc(v.tensor_copy(OHB[:, 256:512], OH2W[:]), s_v, 1)
        vv2 = ctx.val(s_v)

        @blk.tensor
        def _(t):
            t.wait_ge(ctx.sem(s_v), vv2)
            for g in range(32):
                for k in range(2):
                    t.matmul(P3[0:8, (2 * g + k):(2 * g + k) + 1],
                             OHB[:, k * 256 + g * 8:k * 256 + g * 8 + 8],
                             ONESB[:], start=(g == 0 and k == 0), stop=True,
                             skip_group_check=True)
            ctx.inc(t.last, s_t, 1)
        tv = ctx.val(s_t)

        @blk.vector
        def _(v):
            v.wait_ge(ctx.sem(s_t), tv)
            ctx.inc(v.tensor_copy(MISCB[0:8, 0:64], P3[0:8, 0:64]), s_v, 1)
        vv3 = ctx.val(s_v)

        @blk.tensor
        def _(t):
            t.wait_ge(ctx.sem(s_v), vv3)
            t.matmul(PT[0:64, 0:8], MISCB[0:8, 0:64], ID[0:8, 0:8],
                     start=True, stop=True, is_transpose=True,
                     skip_group_check=True)
            ctx.inc(t.last, s_t, 1)
        tv2 = ctx.val(s_t)

        @blk.vector
        def _(v):
            v.wait_ge(ctx.sem(s_t), tv2)
            ctx.inc(v.tensor_copy(MISCB[0:64, 64:72], PT[0:64, 0:8]), s_v, 1)
        vv4 = ctx.val(s_v)

        @blk.tensor
        def _(t):
            t.wait_ge(ctx.sem(s_v), vv4)
            t.matmul(P3[0:64, 64:72], ctx.sb["UTRI64"][:], MISCB[0:64, 64:72],
                     start=True, stop=True, skip_group_check=True)
            ctx.inc(t.last, s_t, 1)
        tv3 = ctx.val(s_t)

        @blk.vector
        def _(v):
            v.wait_ge(ctx.sem(s_t), tv3)
            ctx.inc(v.tensor_copy(MISC[0:64, 256:264], P3[0:64, 64:72]), s_v, 1)
        vv5 = ctx.val(s_v)

        @blk.tensor
        def _(t):
            t.wait_ge(ctx.sem(s_v), vv5)
            t.matmul(P0[0:8, 0:64], MISC[0:64, 256:264],
                     ctx.sb["IDF32"][0:64, 0:64],
                     start=True, stop=True, is_transpose=True,
                     skip_group_check=True)
            ctx.inc(t.last, s_t, 1)
        tv4 = ctx.val(s_t)

        @blk.vector
        def _(v):
            v.wait_ge(ctx.sem(s_t), tv4)
            ctx.inc(v.tensor_copy(BASESTF[0:8, 0:64], P0[0:8, 0:64]), s_v, 1)
        vv6 = ctx.val(s_v)

        for g in range(32):
            for k in range(2):
                col = 2 * g + k

                @blk.tensor
                def _(t, g=g, k=k, col=col):
                    if g == 0 and k == 0:
                        t.wait_ge(ctx.sem(s_v), vv6)
                    else:
                        ctx.wait(t, s_v)
                    ohk = OHB[:, k * 256 + g * 8:k * 256 + g * 8 + 8]
                    t.matmul(P3[:, 128:136], TRIU[:], ohk, start=True,
                             stop=True, skip_group_check=True)
                    bb = BASESTF[0:8, col:col + 1].broadcast_to([8, 128])
                    t.matmul(P2[:, 0:8], bb, ctx.sb["IDF32"][0:8, 0:8],
                             start=True, stop=True, skip_group_check=True)
                    ctx.inc(t.last, s_t, 1)
                tvk = ctx.val(s_t)

                @blk.vector
                def _(v, g=g, k=k, col=col):
                    v.wait_ge(ctx.sem(s_t), tvk)
                    VC(v, v.tensor_copy(SLW[:, col * 8:col * 8 + 8],
                                        P3[:, 128:136]))
                    ctx.inc(VC(v, v.tensor_add(SLW[:, col * 8:col * 8 + 8],
                                               SLW[:, col * 8:col * 8 + 8],
                                               P2[:, 0:8])) if False else
                            v.tensor_add(SLW[:, col * 8:col * 8 + 8],
                                         SLW[:, col * 8:col * 8 + 8],
                                         P2[:, 0:8]), s_v, 1)

        @blk.vector
        def _(v):
            ctx.wait(v, s_v)
            for g in range(32):
                for k in range(2):
                    col = 2 * g + k
                    ohk = OHB[:, k * 256 + g * 8:k * 256 + g * 8 + 8]
                    VC(v, v.tensor_tensor(MISC[:, 0:8],
                                          SLW[:, col * 8:col * 8 + 8],
                                          ohk, ALU.mult))
                    VC(v, v.reduce_sum(SELW[:, col:col + 1], MISC[:, 0:8],
                                       axis=AX.X))
            VC(v, v.tensor_single_scalar(KEEPW[:], SELW[:], float(LCAP),
                                         ALU.is_lt))
            VC(v, v.tensor_scalar(MISC[:, 0:64], EW[:], float(LCAP), None,
                                  ALU.mult))
            VC(v, v.tensor_add(MISC[:, 0:64], MISC[:, 0:64], SELW[:]))
            VC(v, v.tensor_tensor(MISC[:, 0:64], MISC[:, 0:64], KEEPW[:],
                                  ALU.mult))
            VC(v, v.tensor_scalar(MISC2[:, 0:64], KEEPW[:], -float(NE * LCAP),
                                  float(NE * LCAP), ALU.mult, ALU.add))
            VC(v, v.tensor_add(MISC[:, 0:64], MISC[:, 0:64], MISC2[:, 0:64]))
            VC(v, v.tensor_copy(RROWS[:], MISC[:, 0:64]))
            ctx.inc(v.last, s_v, 1)

        @blk.vector
        def _(v):
            KE = KEEPW.rearrange("p (g k) -> p g k", k=2)
            kw1 = MISC[:, 128:160]
            kw2 = MISC[:, 160:192]
            VC(v, v.tensor_tensor(kw1, W12[:, 0:32], KE[:, :, 0], ALU.mult))
            VC(v, v.tensor_tensor(kw2, W12[:, 32:64], KE[:, :, 1], ALU.mult))
            VC(v, v.tensor_add(MISC[:, 64:96], kw1, kw2))
            VC(v, v.tensor_scalar_max(MISC[:, 64:96], MISC[:, 64:96], EPSM))
            VC(v, v.reciprocal(MISC[:, 96:128], MISC[:, 64:96]))
            RF = RFW.rearrange("p (g k) -> p g k", k=2)
            VC(v, v.tensor_tensor(RF[:, :, 0], kw1, MISC[:, 96:128], ALU.mult))
            VC(v, v.tensor_tensor(RF[:, :, 1], kw2, MISC[:, 96:128], ALU.mult))
            ctx.inc(v.last, s_v, 1)
        vfin = ctx.val(s_v)

        if "d_rt" in debug_outputs:
            @blk.sync
            def _(s):
                s.wait_ge(ctx.sem(s_v), vfin)
                ctx.inc(s.dma_start(out=d_rt[:, 0:64], in_=SELW[:]), s_o, 16)
                ctx.inc(s.dma_start(out=d_rt[:, 64:128], in_=EW[:]), s_o, 16)
                ctx.inc(s.dma_start(out=d_rt[:, 128:192], in_=RFW[:]), s_o, 16)
                ctx.inc(s.dma_start(out=d_rt[:, 192:256],
                                    in_=MISC[:, 0:64]), s_o, 16)

        @blk.gpsimd
        def _(g_):
            g_.wait_ge(ctx.sem(s_v), vfin)
            ctx.wait(g_, s_so)
            g0 = ctx.val(s_g)
            for g in range(32):
                sl = 1 + (g % 4)
                if g >= 4:
                    g_.wait_ge(ctx.sem(s_g), g0 + (g - 3) * 32)
                g_.iota(IDXS[:, sl:sl + 1], pattern=[[0, 1]], base=g * 128,
                        channel_multiplier=1)
                for k in range(2):
                    col = 2 * g + k
                    ctx.inc(g_.indirect_dma_start(
                        d_idx[:, :], IndirectOffsetOnAxis(RROWS[:, col:col + 1], 0),
                        IDXS[:, sl:sl + 1], None), s_g, 16)

    def emit_moe():
        CT = 256
        ncts = LCAP // CT
        for e in range(moe_experts):
            @blk.sync
            def _(s, e=e):
                ctx.wait(s, s_g)
                ctx.wait(s, s_t)
                ctx.wait(s, s_v)
                for j in range(12):
                    ctx.inc(s.dma_start(
                        out=IDXS[:, 4 + j:5 + j] if j < 12 else None,
                        in_=d_idx[e * LCAP + j * 128:e * LCAP + j * 128 + 128, :]),
                        s_ld, 16)
            ldv = ctx.val(s_ld)
            tpos = {}
            for j in range(12):
                @blk.gpsimd
                def _(g_, e=e, j=j):
                    if j == 0:
                        g_.wait_ge(ctx.sem(s_ld), ldv)
                    if j >= 4:
                        g_.wait_ge(ctx.sem(s_t), tpos[j - 4])
                    ctx.inc(g_.indirect_dma_start(
                        ARK[:, (j % 4) * 1024:(j % 4) * 1024 + 1024], None,
                        d_xn1tm[:, :],
                        IndirectOffsetOnAxis(IDXS[:, 4 + j:5 + j], 0)),
                        s_g, 16)
                gvj = ctx.val(s_g)

                @blk.tensor
                def _(t, e=e, j=j):
                    t.wait_ge(ctx.sem(s_g), gvj)
                    ctx.wait(t, s_v)
                    for cc in range(8):
                        t.matmul(PT[:, cc * 128:cc * 128 + 128],
                                 ARK[:, (j % 4) * 1024 + cc * 128:(j % 4) * 1024 + cc * 128 + 128],
                                 ID[:], start=True, stop=True,
                                 is_transpose=True, skip_group_check=True)
                    ctx.inc(t.last, s_t, 1)
                tpos[j] = ctx.val(s_t)

                @blk.vector
                def _(v, e=e, j=j):
                    v.wait_ge(ctx.sem(s_t), tpos[j])
                    ctx.inc(v.tensor_copy(
                        ARM[:, 0:12288].rearrange("p (c u) -> p c u", c=8)
                        [:, :, j * 128:j * 128 + 128],
                        PT8[:, :, :]), s_v, 1)
            eiv = ctx.val(s_v)

            WBUFS = (ARW, ARW2, ctx.sb["ARW3"], ctx.sb["ARW4"])
            ACTB = (ARL, ctx.sb["ARK"])
            buf_free = {0: 0, 1: 0, 2: 0, 3: 0}
            for pj in range(8):
                bufs = (WBUFS[(pj % 2) * 2], WBUFS[(pj % 2) * 2 + 1])
                gstart = ctx.val(s_g)

                @blk.gpsimd
                def _(g_, e=e, pj=pj, bufs=bufs):
                    for half in range(2):
                        fs = 2 * pj + half
                        wbuf = bufs[half]
                        bi = (pj % 2) * 2 + half
                        if buf_free[bi] > 0:
                            g_.wait_ge(ctx.sem(s_t), buf_free[bi])
                        if fs == 0:
                            g_.wait_ge(ctx.sem(s_v), eiv)
                        wp = w_preT[e].rearrange("(c p) f -> p c f", p=128)
                        wg = w_gateT[e].rearrange("(c p) f -> p c f", p=128)
                        ctx.inc(g_.dma_start(
                            out=wbuf[:, 0:2048].rearrange("p (c f) -> p c f", c=8),
                            in_=wp[:, :, fs * 256:fs * 256 + 256]), s_g, 16)
                        ctx.inc(g_.dma_start(
                            out=wbuf[:, 2048:4096].rearrange("p (c f) -> p c f", c=8),
                            in_=wg[:, :, fs * 256:fs * 256 + 256]), s_g, 16)
                        wpo = w_postT[e].rearrange("(c p) u -> p c u", p=128)
                        ctx.inc(g_.dma_start(
                            out=wbuf[:, 4096:6144].rearrange("p (c u) -> p c u", c=2),
                            in_=wpo[:, fs * 2:fs * 2 + 2, :]), s_g, 16)
                gw = [gstart + 48, gstart + 96]

                a0 = ctx.val(s_a)
                v0 = ctx.val(s_v)
                t0 = ctx.val(s_t)

                @blk.tensor
                def _(t, e=e, pj=pj, bufs=bufs):
                    for ci, (half, fc) in enumerate(
                            [(0, 0), (0, 1), (1, 0), (1, 1)]):
                        wbuf = bufs[half]
                        t.wait_ge(ctx.sem(s_g), gw[half])
                        for cp in range(3):
                            ps, col = ((P0, 0), (P0, 512), (P3, 0))[cp]
                            if ci > 0:
                                t.wait_ge(ctx.sem(s_v), v0 + 3 * (ci - 1) + cp + 1)
                            elif pj > 0 or e > 0:
                                t.wait_ge(ctx.sem(s_v), v0 - 6 + cp + 1)
                            for cc in range(8):
                                t.matmul(ps[:, col:col + 512],
                                         wbuf[:, 2048 + cc * 256 + fc * 128:
                                              2048 + cc * 256 + fc * 128 + 128],
                                         ARM[:, cc * 1536 + cp * 512:cc * 1536 + cp * 512 + 512],
                                         start=(cc == 0), stop=(cc == 7),
                                         skip_group_check=True)
                            ctx.inc(t.last, s_t, 1)
                        for cp in range(3):
                            ps, col = ((P0, 0), (P0, 512), (P3, 0))[cp]
                            t.wait_ge(ctx.sem(s_a), a0 + 3 * ci + cp + 1)
                            for cc in range(8):
                                t.matmul(ps[:, col:col + 512],
                                         wbuf[:, cc * 256 + fc * 128:
                                              cc * 256 + fc * 128 + 128],
                                         ARM[:, cc * 1536 + cp * 512:cc * 1536 + cp * 512 + 512],
                                         start=(cc == 0), stop=(cc == 7),
                                         skip_group_check=True)
                            ctx.inc(t.last, s_t, 1)
                    for ct in range(ncts):
                        t.wait_ge(ctx.sem(s_v), v0 + 12 + ct)
                        for half in range(2):
                            wbuf = bufs[half]
                            for fc in range(2):
                                act = ACTB[half][:, fc * 1536 + ct * 256:
                                                 fc * 1536 + ct * 256 + 256]
                                for cs in range(2):
                                    ps = P1 if cs == 0 else P2
                                    for dh in range(2):
                                        t.matmul(
                                            ps[:, dh * 512:dh * 512 + 512],
                                            act[:, cs * 128:cs * 128 + 128],
                                            wbuf[:, 4096 + fc * 1024 + dh * 512:
                                                 4096 + fc * 1024 + dh * 512 + 512],
                                            start=(half == 0 and fc == 0),
                                            stop=(half == 1 and fc == 1),
                                            skip_group_check=True)
                        ctx.inc(t.last, s_t, 1)

                @blk.scalar
                def _(sc, pj=pj):
                    for ci in range(4):
                        for cp in range(3):
                            sc.wait_ge(ctx.sem(s_t), t0 + 6 * ci + cp + 1)
                            dst = (MISC[:, 0:512], MISC[:, 512:1024],
                                   MISC2[:, 0:512])[cp]
                            srcp = (P0[:, 0:512], P0[:, 512:1024], P3[:])[cp]
                            ctx.inc(sc.activation(dst, srcp, AF.Silu), s_a, 1)

                @blk.vector
                def _(v, pj=pj):
                    for ci, (half, fc) in enumerate(
                            [(0, 0), (0, 1), (1, 0), (1, 1)]):
                        dst = ACTB[half]
                        for cp in range(3):
                            v.wait_ge(ctx.sem(s_t), t0 + 6 * ci + 3 + cp + 1)
                            srcp = (P0[:, 0:512], P0[:, 512:1024], P3[:])[cp]
                            sl = (MISC[:, 0:512], MISC[:, 512:1024],
                                  MISC2[:, 0:512])[cp]
                            ctx.inc(v.tensor_tensor(
                                dst[:, fc * 1536 + cp * 512:fc * 1536 + cp * 512 + 512],
                                srcp, sl, ALU.mult), s_v, 1)
                    for ct in range(ncts):
                        v.wait_ge(ctx.sem(s_t), t0 + 24 + ct + 1)
                        v.wait_ge(ctx.sem(s_g), gw[1] + 32 * ct)
                        v.tensor_copy(ctx.sb["MOEO"][:, 0:1024], P1[:])
                        ctx.inc(v.tensor_copy(ctx.sb["MOEO2"][:, 0:1024], P2[:]),
                                s_v, 1)

                @blk.gpsimd
                def _(g_, pj=pj, e=e):
                    for ct in range(ncts):
                        r0 = e * LCAP + ct * CT
                        g_.wait_ge(ctx.sem(s_v), v0 + 12 + ct + 1)
                        op = ALU.bypass if pj == 0 else ALU.add
                        ctx.inc(g_.dma_start(out=d_eo[r0:r0 + 128, :],
                                             in_=ctx.sb["MOEO"][:, 0:1024],
                                             accum_op=op), s_g, 16)
                        ctx.inc(g_.dma_start(out=d_eo[r0 + 128:r0 + 256, :],
                                             in_=ctx.sb["MOEO2"][:, 0:1024],
                                             accum_op=op), s_g, 16)
                tend = ctx.val(s_t)
                buf_free[(pj % 2) * 2] = tend
                buf_free[(pj % 2) * 2 + 1] = tend

    def emit_attn(b):
        s_a_alias = s_v
        fmv = d_xn1fm.rearrange("(bb c p) u -> bb c p u", bb=B_CORE, p=128)

        @blk.sync
        def _(s):
            ctx.wait(s, s_v)
            ctx.wait(s, s_o)
            for cc in range(8):
                ctx.inc(s.dma_start(out=ARM8[:, cc, :], in_=fmv[b, cc]),
                        s_ld, 16)
        ldx = ctx.val(s_ld)

        qkd = d_qk.rearrange("(bb two c p) u -> bb two c p u", bb=B_CORE,
                             two=2, p=128)
        for oc in range(8):
            for part, wcol in ((0, oc * 128), (1, (24 + oc) * 128)):
                wbuf = ARW if part == 0 else ARW2

                @blk.gpsimd
                def _(g_, oc=oc, wcol=wcol, wbuf=wbuf):
                    ctx.wait(g_, s_t)
                    wv = wqkv.rearrange("(c p) u -> c p u", p=128)
                    for cc in range(8):
                        ctx.inc(g_.dma_start(
                            out=wbuf[:, cc * 128:cc * 128 + 128],
                            in_=wv[cc, :, wcol:wcol + 128]), s_g, 16)
            gv = ctx.val(s_g)

            for tt in range(4):
                @blk.tensor
                def _(t, oc=oc, tt=tt):
                    t.wait_ge(ctx.sem(s_g), gv)
                    t.wait_ge(ctx.sem(s_ld), ldx)
                    ctx.wait(t, s_v)
                    for cc in range(8):
                        t.matmul(P0[:, 0:512],
                                 ARW[:, cc * 128:cc * 128 + 128],
                                 ARM8[:, cc, tt * 512:tt * 512 + 512],
                                 start=(cc == 0), stop=(cc == 7),
                                 skip_group_check=True)
                    for cc in range(8):
                        t.matmul(P1[:, 0:512],
                                 ARW2[:, cc * 128:cc * 128 + 128],
                                 ARM8[:, cc, tt * 512:tt * 512 + 512],
                                 start=(cc == 0), stop=(cc == 7),
                                 skip_group_check=True)
                    ctx.inc(t.last, s_t, 1)
                tvq = ctx.val(s_t)

                @blk.vector
                def _(v, oc=oc, tt=tt):
                    v.wait_ge(ctx.sem(s_t), tvq)
                    cs = slice(tt * 512, tt * 512 + 512)
                    VC(v, v.tensor_tensor(MISC[:, 0:512], P0[:, 0:512],
                                          ctx.sb["COSQ"][:, cs], ALU.mult))
                    VC(v, v.tensor_tensor(MISC2[:, 0:512], P1[:, 0:512],
                                          ctx.sb["SINQ"][:, cs], ALU.mult))
                    ctx.inc(v.tensor_add(ARK[:, tt * 512:tt * 512 + 512],
                                         MISC[:, 0:512], MISC2[:, 0:512]),
                            s_v, 1)
            vvq = ctx.val(s_v)

            @blk.sync
            def _(s, oc=oc):
                s.wait_ge(ctx.sem(s_v), vvq)
                ctx.inc(s.dma_start(out=qkd[b, 0, oc], in_=ARK[:, 0:2048]),
                        s_o, 16)

        for oc in range(8):
            for part, wcol in ((0, (8 + oc) * 128), (1, (32 + oc) * 128)):
                wbuf = ARW if part == 0 else ARW2

                @blk.gpsimd
                def _(g_, oc=oc, wcol=wcol, wbuf=wbuf):
                    ctx.wait(g_, s_t)
                    wv = wqkv.rearrange("(c p) u -> c p u", p=128)
                    for cc in range(8):
                        ctx.inc(g_.dma_start(
                            out=wbuf[:, cc * 128:cc * 128 + 128],
                            in_=wv[cc, :, wcol:wcol + 128]), s_g, 16)
            gv = ctx.val(s_g)

            for tt in range(4):
                @blk.tensor
                def _(t, oc=oc, tt=tt):
                    t.wait_ge(ctx.sem(s_g), gv)
                    ctx.wait(t, s_v)
                    for cc in range(8):
                        t.matmul(P0[:, 0:512],
                                 ARW[:, cc * 128:cc * 128 + 128],
                                 ARM8[:, cc, tt * 512:tt * 512 + 512],
                                 start=(cc == 0), stop=(cc == 7),
                                 skip_group_check=True)
                    for cc in range(8):
                        t.matmul(P1[:, 0:512],
                                 ARW2[:, cc * 128:cc * 128 + 128],
                                 ARM8[:, cc, tt * 512:tt * 512 + 512],
                                 start=(cc == 0), stop=(cc == 7),
                                 skip_group_check=True)
                    ctx.inc(t.last, s_t, 1)
                tvq = ctx.val(s_t)

                @blk.vector
                def _(v, oc=oc, tt=tt):
                    v.wait_ge(ctx.sem(s_t), tvq)
                    cs = slice(tt * 512, tt * 512 + 512)
                    VC(v, v.tensor_tensor(MISC[:, 0:512], P0[:, 0:512],
                                          ctx.sb["COSK"][:, cs], ALU.mult))
                    VC(v, v.tensor_tensor(MISC2[:, 0:512], P1[:, 0:512],
                                          ctx.sb["SINK"][:, cs], ALU.mult))
                    ctx.inc(v.tensor_add(ARK[:, tt * 512:tt * 512 + 512],
                                         MISC[:, 0:512], MISC2[:, 0:512]),
                            s_v, 1)
            vvq = ctx.val(s_v)

            @blk.sync
            def _(s, oc=oc):
                s.wait_ge(ctx.sem(s_v), vvq)
                ctx.inc(s.dma_start(out=qkd[b, 1, oc], in_=ARK[:, 0:2048]),
                        s_o, 16)

        def wvap(cc):
            if cc < 6:
                return ARW[:, cc * 1024:cc * 1024 + 1024]
            return ARW2[:, (cc - 6) * 1024:(cc - 6) * 1024 + 1024]

        @blk.gpsimd
        def _(g_):
            ctx.wait(g_, s_t)
            wv = wqkv.rearrange("(c p) u -> c p u", p=128)
            for cc in range(8):
                ctx.inc(g_.dma_start(out=wvap(cc),
                                     in_=wv[cc, :, 2048:3072]), s_g, 16)
        gvv = ctx.val(s_g)

        for tt in range(16):
            @blk.tensor
            def _(t, tt=tt):
                t.wait_ge(ctx.sem(s_g), gvv)
                ctx.wait(t, s_v)
                for half in range(2):
                    for cc in range(8):
                        t.matmul(P2[:, half * 512:half * 512 + 512],
                                 ARM8[:, cc, tt * 128:tt * 128 + 128],
                                 wvap(cc)[:, half * 512:half * 512 + 512],
                                 start=(cc == 0), stop=(cc == 7),
                                 skip_group_check=True)
                ctx.inc(t.last, s_t, 1)
            tvv = ctx.val(s_t)

            @blk.vector
            def _(v, tt=tt):
                v.wait_ge(ctx.sem(s_t), tvv)
                ctx.inc(v.tensor_copy(XNT[:], P2[:]), s_v, 1)
            vvv = ctx.val(s_v)

            @blk.sync
            def _(s, tt=tt):
                s.wait_ge(ctx.sem(s_v), vvv)
                ctx.inc(s.dma_start(
                    out=d_v[b * T + tt * 128:b * T + tt * 128 + 128, :],
                    in_=XNT[:]), s_o, 16)

        for h in range(attn_heads):
            cc = h // 2
            ro = (h % 2) * 64
            @blk.sync
            def _(s, h=h, cc=cc, ro=ro):
                ctx.wait(s, s_a)
                ctx.wait(s, s_t)
                ctx.inc(s.dma_start(out=ARK[0:64, 0:2048],
                                    in_=qkd[b, 0, cc][ro:ro + 64, :]), s_ld, 16)
                ctx.inc(s.dma_start(out=ARK[0:64, 2048:4096],
                                    in_=qkd[b, 1, cc][ro:ro + 64, :]), s_ld, 16)
                vv_ = d_v.rearrange("(g p) u -> g p u", p=128)
                for g in range(16):
                    ctx.inc(s.dma_start(
                        out=ARK[:, 4096 + g * 64:4096 + g * 64 + 64],
                        in_=vv_[b * 16 + g][:, h * 64:h * 64 + 64]), s_ld, 16)
            ldh = ctx.val(s_ld)

            for which in range(2):
                @blk.vector
                def _(v, h=h, which=which):
                    if which == 0:
                        v.wait_ge(ctx.sem(s_ld), ldh)
                    else:
                        ctx.wait(v, s_t)
                    qs = ARK[0:64, which * 2048:which * 2048 + 2048]
                    ctx.inc(v.tensor_tensor(ARL[0:64, 0:2048], qs, qs,
                                            ALU.mult), s_a_alias, 1)
                avh = ctx.val(s_a_alias)

                @blk.tensor
                def _(t, h=h, which=which):
                    t.wait_ge(ctx.sem(s_a_alias), avh)
                    for i4 in range(4):
                        ps = (P0, P0, P1, P1)[i4]
                        col = (i4 % 2) * 512
                        t.matmul(ps[0:1, col:col + 512], ONESB[0:64, :],
                                 ARL[0:64, i4 * 512:i4 * 512 + 512],
                                 start=True, stop=True, skip_group_check=True)
                    ctx.inc(t.last, s_t, 1)
                tvn = ctx.val(s_t)

                @blk.vector
                def _(v, h=h, which=which):
                    v.wait_ge(ctx.sem(s_t), tvn)
                    VC(v, v.reduce_max(SCL[0:1, 20:21], P0[0:1, :], axis=AX.X))
                    VC(v, v.reduce_max(SCL[0:1, 21:22], P1[0:1, :], axis=AX.X))
                    v.tensor_max(SCL[0:1, 12 + which:13 + which],
                                 SCL[0:1, 20:21], SCL[0:1, 21:22])
                    ctx.inc(v.last, s_v, 1)

            @blk.vector
            def _(v, h=h):
                ctx.wait(v, s_v)
                ctx.inc(v.tensor_tensor(SCL[0:1, 14:15], SCL[0:1, 12:13],
                                        SCL[0:1, 13:14], ALU.mult), s_v, 1)
            vvc = ctx.val(s_v)

            @blk.scalar
            def _(sc, h=h):
                sc.wait_ge(ctx.sem(s_v), vvc)
                ctx.inc(sc.activation(SCL[0:1, 15:16], SCL[0:1, 14:15],
                                      AF.Sqrt), s_a, 1)
            avc = ctx.val(s_a)

            @blk.vector
            def _(v, h=h):
                v.wait_ge(ctx.sem(s_a), avc)
                ctx.inc(v.tensor_scalar_mul(SCL[0:1, 16:17], SCL[0:1, 15:16],
                                            -1.0), s_v, 1)
            vvc2 = ctx.val(s_v)

            @blk.tensor
            def _(t, h=h):
                t.wait_ge(ctx.sem(s_v), vvc2)
                t.matmul(P3[:, 508:509], ONESRF[0:1, :], SCL[0:1, 16:17],
                         start=True, stop=True, skip_group_check=True)
                ctx.inc(t.last, s_t, 1)
            tvc = ctx.val(s_t)

            @blk.vector
            def _(v, h=h):
                v.wait_ge(ctx.sem(s_t), tvc)
                ctx.inc(v.tensor_copy(SCL[:, 17:18], P3[:, 508:509]), s_v, 1)
            vnegc = ctx.val(s_v)

            T0 = ctx.val(s_t)
            A0 = ctx.val(s_a)
            V0 = ctx.val(s_v)

            @blk.tensor
            def _(t, h=h):
                for qt in range(4):
                    Tq = T0 + 10 * qt
                    Aq = A0 + 16 * qt
                    for rd in range(8):
                        ps = P0 if rd % 2 == 0 else P1
                        if qt == 0 and rd == 0:
                            t.wait_ge(ctx.sem(s_v), vnegc)
                        if rd >= 2:
                            t.wait_ge(ctx.sem(s_a), Aq + 2 * (rd - 1))
                        if qt > 0 and rd < 2:
                            t.wait_ge(ctx.sem(s_a), Aq)
                            t.wait_ge(ctx.sem(s_v), V0 + 2 * qt)
                        for j in range(2):
                            kc = rd * 2 + j
                            t.matmul(ps[:, j * 512:j * 512 + 512],
                                     ARK[0:64, 2048 + kc * 128:2048 + kc * 128 + 128],
                                     ARK[0:64, qt * 512:qt * 512 + 512],
                                     start=True, stop=True,
                                     skip_group_check=True)
                        ctx.inc(t.last, s_t, 1)
                    t.wait_ge(ctx.sem(s_a), Aq + 16)
                    if qt > 0:
                        t.wait_ge(ctx.sem(s_v), V0 + 2 * qt)
                    for kc in range(16):
                        t.matmul(P3[0:1, 0:512], ONESB[:],
                                 ARM[:, kc * 512:kc * 512 + 512],
                                 start=(kc == 0), stop=(kc == 15),
                                 skip_group_check=True)
                    for kc in range(16):
                        t.matmul(P2[0:64, 0:512],
                                 ARK[:, 4096 + kc * 64:4096 + kc * 64 + 64],
                                 ARM[:, kc * 512:kc * 512 + 512],
                                 start=(kc == 0), stop=(kc == 15),
                                 skip_group_check=True)
                    ctx.inc(t.last, s_t, 1)
                    t.wait_ge(ctx.sem(s_v), V0 + 2 * qt + 1)
                    t.matmul(P0[0:64, 0:512], ONESRF[0:1, 0:64],
                             ARS[0:1, 0:512], start=True, stop=True,
                             skip_group_check=True)
                    ctx.inc(t.last, s_t, 1)
            assert ctx.val(s_t) == T0 + 40

            @blk.scalar
            def _(sc, h=h):
                for qt in range(4):
                    Tq = T0 + 10 * qt
                    for rd in range(8):
                        sc.wait_ge(ctx.sem(s_t), Tq + rd + 1)
                        ps = P0 if rd % 2 == 0 else P1
                        for j in range(2):
                            kc = rd * 2 + j
                            ctx.inc(sc.activation(
                                ARM[:, kc * 512:kc * 512 + 512],
                                ps[:, j * 512:j * 512 + 512], AF.Exp,
                                bias=SCL[:, 17:18]), s_a, 1)
            assert ctx.val(s_a) == A0 + 64

            @blk.vector
            def _(v, h=h):
                for qt in range(4):
                    Tq = T0 + 10 * qt
                    v.wait_ge(ctx.sem(s_t), Tq + 9)
                    ctx.inc(v.reciprocal(ARS[0:1, 0:512], P3[0:1, 0:512]),
                            s_v, 1)
                    v.wait_ge(ctx.sem(s_t), Tq + 10)
                    VC(v, v.tensor_copy(MISC[0:64, 0:512], P0[0:64, 0:512]))
                    ctx.inc(v.tensor_tensor(
                        ARL[0:64, 2048 + qt * 512:2048 + qt * 512 + 512],
                        P2[0:64, 0:512], MISC[0:64, 0:512], ALU.mult),
                        s_v, 1)
            assert ctx.val(s_v) == V0 + 8
            vnat = ctx.val(s_v)

            @blk.sync
            def _(s, h=h):
                s.wait_ge(ctx.sem(s_v), vnat)
                ctx.inc(s.dma_start(
                    out=d_natt[(b * NH + h) * 64:(b * NH + h) * 64 + 64, :],
                    in_=ARL[0:64, 2048:4096]), s_o, 16)

    def emit_fin(b):
        natv = d_natt.rearrange("(bb c p) u -> bb c p u", bb=B_CORE, p=128)

        @blk.sync
        def _(s):
            ctx.wait(s, s_o)
            ctx.wait(s, s_v)
            for cc in range(8):
                ctx.inc(s.dma_start(out=ARM8[:, cc, :], in_=natv[b, cc]),
                        s_ld, 16)
        ldn = ctx.val(s_ld)

        @blk.gpsimd
        def _(g_):
            ctx.wait(g_, s_t)
            wv = w_od.rearrange("(c p) u -> c p u", p=128)
            for cc in range(8):
                dst = (ARW[:, cc * 1024:cc * 1024 + 1024] if cc < 6 else
                       ARW2[:, (cc - 6) * 1024:(cc - 6) * 1024 + 1024])
                ctx.inc(g_.dma_start(out=dst, in_=wv[cc]), s_g, 16)
        gvo = ctx.val(s_g)

        def woap(cc):
            if cc < 6:
                return ARW[:, cc * 1024:cc * 1024 + 1024]
            return ARW2[:, (cc - 6) * 1024:(cc - 6) * 1024 + 1024]

        for tt in range(16):
            gtile = b * 16 + tt

            @blk.sync
            def _(s, tt=tt):
                ctx.wait(s, s_v)
                ctx.inc(s.dma_start(
                    out=MISC[:],
                    in_=d_x1[b * T + tt * 128:b * T + tt * 128 + 128, :]),
                    s_ld, 16)
            ldt = ctx.val(s_ld)

            @blk.gpsimd
            def _(g_, tt=tt, gtile=gtile):
                ctx.wait(g_, s_v)
                for k in range(2):
                    col = 2 * gtile + k
                    ctx.inc(g_.indirect_dma_start(
                        (MISC2 if k == 0 else ARS)[:, 0:1024], None,
                        d_eo[:, :],
                        IndirectOffsetOnAxis(RROWS[:, col:col + 1], 0)),
                        s_g, 16)
            gvt = ctx.val(s_g)

            @blk.tensor
            def _(t, tt=tt):
                t.wait_ge(ctx.sem(s_ld), ldn)
                t.wait_ge(ctx.sem(s_g), gvo)
                ctx.wait(t, s_v)
                for half in range(2):
                    for cc in range(8):
                        t.matmul(P2[:, half * 512:half * 512 + 512],
                                 ARM8[:, cc, tt * 128:tt * 128 + 128],
                                 woap(cc)[:, half * 512:half * 512 + 512],
                                 start=(cc == 0), stop=(cc == 7),
                                 skip_group_check=True)
                ctx.inc(t.last, s_t, 1)
            tvo = ctx.val(s_t)

            @blk.vector
            def _(v, tt=tt, gtile=gtile):
                v.wait_ge(ctx.sem(s_t), tvo)
                v.wait_ge(ctx.sem(s_ld), ldt)
                v.wait_ge(ctx.sem(s_g), gvt)
                VC(v, v.tensor_add(MISC[:], MISC[:], P2[:]))
                VC(v, v.scalar_tensor_tensor(MISC[:], MISC2[:],
                                             RFW[:, 2 * gtile:2 * gtile + 1],
                                             MISC[:], ALU.mult, ALU.add))
                ctx.inc(v.scalar_tensor_tensor(
                    MISC[:], ARS[:, 0:1024],
                    RFW[:, 2 * gtile + 1:2 * gtile + 2], MISC[:],
                    ALU.mult, ALU.add), s_v, 1)
            vvo = ctx.val(s_v)

            @blk.sync
            def _(s, tt=tt):
                s.wait_ge(ctx.sem(s_v), vvo)
                ctx.inc(s.dma_start(
                    out=out_t[b * T + tt * 128:b * T + tt * 128 + 128, :],
                    in_=MISC[:]), s_o, 16)

    if "l0" in phases:
        for b in range(B_CORE):
            with nc.named_scope(f"l0_{b}"):
                emit_l0(b)
            barrier()
    if "ln1" in phases:
        for b in range(B_CORE):
            with nc.named_scope(f"ln1_{b}"):
                emit_ln1(b)
            barrier()
    if "route" in phases:
        with nc.named_scope("route"):
            emit_route()
        barrier()
    if "moe" in phases:
        with nc.named_scope("moe"):
            emit_moe()
        barrier()
    if "attn" in phases:
        for b in range(B_CORE):
            with nc.named_scope(f"attn_{b}"):
                emit_attn(b)
            barrier()
    if "fin" in phases:
        for b in range(B_CORE):
            with nc.named_scope(f"fin_{b}"):
                emit_fin(b)
            barrier()

    @blk.sync
    def _(s):
        ctx.wait(s, s_o)
        ctx.wait(s, s_so)
        ctx.wait(s, s_g)
        ctx.wait(s, s_ld)

    st.close()
    return nc



_CACHE = {}


def kernel(**inp):
    x = f32(inp["L0_attn_x"])
    cns = host_constants(inp)
    if "nc" not in _CACHE:
        _CACHE["nc"] = build()
    nc = _CACHE["nc"]
    in_maps = []
    base = {
        "w_qkv_aug": np.concatenate(
            [f32(inp['L1_attn_w_qkv']),
             rope_perm_cols(f32(inp['L1_attn_w_qkv'])[:, :1024]),
             rope_perm_cols(f32(inp['L1_attn_w_qkv'])[:, 1024:2048])], axis=1),
        "w_o": f32(inp['L1_attn_w_o']),
        "w_preT": np.ascontiguousarray(
            f32(inp['L1_ffn_ff_pre_act_weight']).transpose(0, 2, 1)),
        "w_gateT": np.ascontiguousarray(
            f32(inp['L1_ffn_gate_weight']).transpose(0, 2, 1)),
        "w_postT": np.ascontiguousarray(
            f32(inp['L1_ffn_ff_post_act_weight']).transpose(0, 2, 1)),
    }
    for nm in CONST_NAMES:
        shp, dt = CONST_SHAPES[nm]
        arr = cns[nm]
        if dt == BF16:
            arr = arr.astype(ml_dtypes.bfloat16)
        elif dt == U32:
            arr = arr.astype(np.uint32)
        else:
            arr = arr.astype(np.float32)
        assert list(arr.shape) == shp, (nm, arr.shape, shp)
        base["c_" + nm] = arr
    for core in range(8):
        m = dict(base)
        m["x_tm"] = np.ascontiguousarray(
            x[2 * core:2 * core + 2].reshape(NTOK, D))
        in_maps.append(m)
    res = run_bass_kernel_spmd(nc, in_maps, core_ids=list(range(8)))
    out = np.concatenate(
        [res.results[c]["out"].reshape(2, T, D) for c in range(8)], axis=0)
    return out
